# version 03
# baseline (speedup 1.0000x reference)
import contextlib
import numpy as np
import ml_dtypes
import concourse.bass as bass
import concourse.mybir as mybir
from concourse.bass_utils import run_bass_kernel_spmd

F32 = mybir.dt.float32
BF16 = mybir.dt.bfloat16
I32 = mybir.dt.int32
AF = mybir.ActivationFunctionType
ALU = mybir.AluOpType
AX = mybir.AxisListType
BF = ml_dtypes.bfloat16

D = 1024
T = 2048
H = 16
DK = 64
DFF = 4096
NT = 16
NF = 8
NP = 8
EPS_LN = 1e-5
EPS_RMS = 1e-5
EPS_Q = 1e-8
ROPE_THETA = 10000.0
BSHIFT = 40.0

N_SUBLAYERS = 4
STOP_STAGE = 99
ATT_LIMIT = (99, 99, 99, 99)
STOP_LI = None
S5_PARTS = 9
DEV_TAPS = []


def _zigzag_idx(h):
    r = np.arange(T)
    return (2 * (r // 128) + h) * 128 + (r % 128)


def _rope_tables(pos):
    k = np.arange(DK) // 2
    inv = ROPE_THETA ** (-2.0 * k.astype(np.float64) / DK)
    ang = pos[None, :].astype(np.float64) * inv[:, None]
    cos = np.cos(ang)
    sgn = np.where(np.arange(DK) % 2 == 0, -1.0, 1.0)
    sin = np.sin(ang) * sgn[:, None]
    return cos.astype(np.float32), sin.astype(np.float32)


def _host_aux(h):
    idx = _zigzag_idx(h)
    cos, sin = _rope_tables(idx)
    stair = (np.arange(128)[:, None] <= np.arange(128)[None, :]).astype(BF)
    ones = np.ones((128, 128), BF)
    zeros = np.zeros((128, 128), BF)
    slots = [stair, zeros] if h == 0 else [ones, stair]
    maskdup = np.concatenate([np.concatenate([s, s], axis=1) for s in slots], axis=1)
    ident = np.eye(128, dtype=BF)
    pswap = np.zeros((128, 128), BF)
    dd = np.arange(128)
    sw = np.where(dd % 2 == 0, dd + 1, dd - 1)
    pswap[sw, dd] = 1.0
    return idx, cos, sin, maskdup, ident, pswap


class Sem:
    def __init__(self, es, nc, name):
        self.h = es.enter_context(nc.semaphore(name))
        self.n = 0

    def bump(self, k=1):
        self.n += k
        return self.n


class Prog:
    ENGS = ("sync", "gpsimd", "tensor", "vector", "scalar")

    def __init__(self):
        self.rec = {e: [] for e in self.ENGS}

    def emit(self, eng, builder, waits=(), inc=None, inc_by=1, chain=None):
        if chain is not None:
            waits = list(waits) + [(chain, chain.n)]
            inc = chain
            inc_by = 1
        waits = tuple((s, v) for (s, v) in waits)

        def fn(engine):
            for s, v in waits:
                engine.wait_ge(s.h, v)
            inst = builder(engine)
            if inc is not None:
                if inst is None:
                    engine.sem_inc(inc.h, inc_by)
                else:
                    inst.then_inc(inc.h, inc_by)
        self.rec[eng].append(fn)
        if inc is not None:
            inc.bump(inc_by)

    def finalize(self, nc):
        with nc.Block() as block:
            for e in self.ENGS:
                recs = tuple(self.rec[e])

                def body(engine, _r=recs):
                    for f in _r:
                        f(engine)
                getattr(block, e)(body)


def build_program(n_sub=None):
    if n_sub is None:
        n_sub = N_SUBLAYERS
    stop = STOP_STAGE if STOP_LI is None else 99
    nc = bass.Bass("TRN2", target_bir_lowering=False)

    def inp(name, shape, dtype=F32):
        return nc.dram_tensor(name, shape, dtype, kind="ExternalInput").ap()

    xin = inp("xin", [T, D])
    wqT = inp("wqT", [D, D]); wkT = inp("wkT", [D, D]); wvT = inp("wvT", [D, D])
    woT = inp("woT", [D, D])
    w1T = inp("w1T", [D, DFF]); w3T = inp("w3T", [D, DFF]); w2T = inp("w2T", [DFF, D])
    fwT = inp("fwT", [D, D])
    bqT = inp("bqT", [D, D]); bkT = inp("bkT", [D, D]); bvT = inp("bvT", [D, D])
    boT = inp("boT", [D, D])
    rcos = inp("rcos", [DK, T]); rsin = inp("rsin", [DK, T])
    maskin = inp("maskin", [128, 512], BF16)
    identin = inp("identin", [128, 128], BF16)
    pswapin = inp("pswapin", [128, 128], BF16)
    gqc = inp("gqc", [128, 1]); bqc = inp("bqc", [128, 1])
    gkc = inp("gkc", [128, 1]); bkc = inp("bkc", [128, 1])
    growsq = inp("growsq", [2, 128], BF16)
    growsk = inp("growsk", [2, 128], BF16)
    nbrowq = inp("nbrowq", [2, 128], BF16)
    nbrowk = inp("nbrowk", [2, 128], BF16)
    nbdupq = inp("nbdupq", [1, 128], BF16)
    nbdupk = inp("nbdupk", [1, 128], BF16)
    boc = inp("boc", [128, NF])
    n1c = inp("n1c", [128, NF]); n2c = inp("n2c", [128, NF])

    xout = nc.dram_tensor("xout", [T, D], F32, kind="ExternalOutput").ap()

    KSZ = NP * 128 * T
    VOFF = KSZ
    MOFF = 2 * KSZ
    CCE = 2 * KSZ + 2048
    cc_in_h = [nc.dram_tensor(f"cc_in{i}", [CCE // 2048, 2048], BF16) for i in range(2)]
    cc_out_h = [nc.dram_tensor(f"cc_out{i}", [2 * CCE // 2048, 2048], BF16)
                for i in range(2)]
    wq_park = nc.dram_tensor("wq_park", [128, 8192], BF16)
    w1bf = nc.dram_tensor("w1bf", [D * DFF // 2048, 2048], BF16)
    w3bf = nc.dram_tensor("w3bf", [D * DFF // 2048, 2048], BF16)
    w2bf = nc.dram_tensor("w2bf", [D * DFF // 2048, 2048], BF16)

    es = contextlib.ExitStack()
    with es:
        def sbt(name, shape, dtype=BF16):
            return es.enter_context(nc.sbuf_tensor(name, shape, dtype))

        def psumt(name, shape, dtype=F32):
            return es.enter_context(nc.psum_tensor(name, shape, dtype))

        x_res = sbt("x_res", [128, NT * D], F32)
        R1 = sbt("R1", [128, 16384])
        R2 = sbt("R2", [128, 16384])
        wA = sbt("wA", [128, 8256])
        wB = sbt("wB", [128, 10304])
        ysc = sbt("ysc", [128, 4096])
        stg = sbt("stg", [128, 4096])
        probs = sbt("probs", [128, 3072])
        scrf = stg[:].bitcast(F32)
        iscr = stg[:].bitcast(I32)
        yscf = ysc[:].bitcast(F32)
        rrowf = wB[0:1, 8256:10304].bitcast(F32)
        mask = sbt("mask", [128, 512])
        ident = sbt("ident", [128, 128])
        pswap = sbt("pswap", [128, 128])
        e1 = sbt("e1", [1, 128])
        e2 = sbt("e2", [1, 128])
        bd2 = sbt("bd2", [128, 4])
        onesrow = sbt("onesrow", [33, 512])
        growq = sbt("growq", [33, 128])
        growk = sbt("growk", [33, 128])
        nbq = sbt("nbq", [33, 128])
        nbk = sbt("nbk", [33, 128])
        growq2 = sbt("growq2", [1, 128])
        growk2 = sbt("growk2", [1, 128])
        nbqf = sbt("nbqf", [1, 128])
        nbkf = sbt("nbkf", [1, 128])
        onesf32 = es.enter_context(nc.sbuf_tensor("onesf32", [1, 128], F32))
        vecs = es.enter_context(nc.sbuf_tensor("vecs", [128, 256], F32))
        colsf = es.enter_context(nc.sbuf_tensor("colsf", [128, 16], F32))
        Mrow = es.enter_context(nc.sbuf_tensor("Mrow", [2, 64], F32))
        biascol = es.enter_context(nc.sbuf_tensor("biascol", [128, 16], F32))
        ln_cols = es.enter_context(nc.sbuf_tensor("ln_cols", [128, 3 * NF], F32))

        def Mq(z):
            return Mrow[0:2, z * 8:(z + 1) * 8]

        def Mr1(a, b):
            return Mrow[0:1, a:b]

        ps2 = [psumt(f"ps2_{i}", [128, 1024]) for i in range(4)]

        def pbank(i):
            return ps2[i // 2][:, (i % 2) * 512:(i % 2 + 1) * 512]

        def psb(i):
            return ps2[i // 2][:].bitcast(BF16)[:, (i % 2) * 1024:(i % 2 + 1) * 1024]

        p = Prog()
        SS = {n: Sem(es, nc, n) for n in
              ("dx", "dw", "dcc", "pe", "ve", "se", "po", "od")}
        dx, dw, dcc = SS["dx"], SS["dw"], SS["dcc"]
        pe, ve, se, po, od = SS["pe"], SS["ve"], SS["se"], SS["po"], SS["od"]

        def flatap(h, off, dims):
            return bass.AP(h, off, [list(d) for d in dims])

        p.emit("sync", lambda e: e.dma_start(
            x_res[:].rearrange("p (j d) -> p j d", d=D),
            xin.rearrange("(j p) d -> p j d", p=128)), inc=dx, inc_by=16)
        for dst, srcap in ((mask[:], maskin), (ident[:], identin),
                           (pswap[:], pswapin),
                           (colsf[:, 0:1], gqc), (colsf[:, 1:2], bqc),
                           (colsf[:, 2:3], gkc), (colsf[:, 3:4], bkc),
                           (ln_cols[:, 0:NF], n1c), (ln_cols[:, NF:2 * NF], n2c),
                           (ln_cols[:, 2 * NF:3 * NF], boc),
                           (growq[0:1, :], growsq[0:1, :]),
                           (growq[32:33, :], growsq[1:2, :]),
                           (growk[0:1, :], growsk[0:1, :]),
                           (growk[32:33, :], growsk[1:2, :]),
                           (nbq[0:1, :], nbrowq[0:1, :]),
                           (nbq[32:33, :], nbrowq[1:2, :]),
                           (nbk[0:1, :], nbrowk[0:1, :]),
                           (nbk[32:33, :], nbrowk[1:2, :]),
                           (growq2[0:1, :], growsq[1:2, :]),
                           (growk2[0:1, :], growsk[1:2, :]),
                           (nbqf[0:1, :], nbdupq),
                           (nbkf[0:1, :], nbdupk)):
            p.emit("sync", lambda e, d=dst, s=srcap: e.dma_start(d, s),
                   inc=dx, inc_by=16)
        DX0 = dx.n

        for ap_, val in ((e1[0:1, 0:64], 1.0), (e1[0:1, 64:128], 0.0),
                         (e2[0:1, 0:64], 0.0), (e2[0:1, 64:128], 1.0),
                         (bd2[0:64, 0:1], 1.0 / 64), (bd2[64:128, 0:1], 0.0),
                         (bd2[0:64, 1:2], 0.0), (bd2[64:128, 1:2], 1.0 / 64),
                         (bd2[0:64, 2:3], 1.0), (bd2[64:128, 2:3], 0.0),
                         (bd2[0:64, 3:4], 0.0), (bd2[64:128, 3:4], 1.0),
                         (onesrow[0:1, :], 1.0), (onesrow[32:33, :], 1.0),
                         (onesf32[:], 1.0)):
            p.emit("vector", lambda e, a=ap_, v=val: e.memset(a, v), chain=ve)
        for wbuf in (wA, wB):
            p.emit("vector", lambda e, wbuf=wbuf: e.memset(
                flatap(wbuf, 4096 + 64, [(wbuf.shape[1], 128), (130, 32), (1, 1)]),
                1.0), chain=ve)
            p.emit("vector", lambda e, wbuf=wbuf: e.memset(
                flatap(wbuf, 4096 + 129, [(wbuf.shape[1], 128), (130, 32), (1, 1)]),
                1.0), chain=ve)
        VE_CONST = ve.n

        def vslot(s_):
            return vecs[:, s_ * 16:(s_ + 1) * 16]

        def ln_stats(slot_m, slot_r, extra_rms_slot=None):
            m = vslot(slot_m); r = vslot(slot_r); nb = vslot(14)
            p.emit("vector", lambda e: e.reduce_sum(
                m, x_res[:].rearrange("p (j d) -> p j d", d=D), axis=AX.X),
                waits=[(dx, DX0)], chain=ve)
            p.emit("vector", lambda e: e.tensor_scalar_mul(m, m, 1.0 / D), chain=ve)
            for j in range(NT):
                half = (j % 2) * 1024
                wse = [(ve, ve.n)] if j >= 2 else [(dx, DX0), (se, se.n), (ve, ve.n)]
                p.emit("scalar", lambda e, j=j, half=half: e.activation(
                    scrf[:, half:half + 1024], x_res[:, j * D:(j + 1) * D],
                    AF.Square), waits=wse, inc=se)
                p.emit("vector", lambda e, j=j, half=half: e.reduce_sum(
                    r[:, j:j + 1],
                    scrf[:, half:half + 1024].rearrange("p (a d) -> p a d", a=1),
                    axis=AX.X), waits=[(se, se.n)], chain=ve)
            p.emit("vector", lambda e: e.tensor_scalar_mul(r, r, 1.0 / D), chain=ve)
            p.emit("vector", lambda e: e.tensor_mul(nb, m, m), chain=ve)
            p.emit("vector", lambda e: e.tensor_sub(r, r, nb), chain=ve)
            p.emit("vector", lambda e: e.tensor_scalar_add(r, r, EPS_LN), chain=ve)
            p.emit("scalar", lambda e: e.activation(r, r, AF.Ln),
                   waits=[(ve, ve.n)], chain=se)
            p.emit("scalar", lambda e: e.activation(r, r, AF.Exp, scale=-0.5),
                   chain=se)
            if extra_rms_slot is not None:
                s1 = vslot(extra_rms_slot)
                p.emit("vector", lambda e: e.tensor_mul(s1, r, r),
                       waits=[(se, se.n)], chain=ve)
                p.emit("vector", lambda e: e.tensor_scalar(
                    s1, s1, -EPS_LN, 1.0 + EPS_RMS, ALU.mult, ALU.add), chain=ve)
                p.emit("scalar", lambda e: e.activation(s1, s1, AF.Ln),
                       waits=[(ve, ve.n)], chain=se)
                p.emit("scalar", lambda e: e.activation(s1, s1, AF.Exp, scale=-0.5),
                       chain=se)
                p.emit("vector", lambda e: e.tensor_mul(s1, s1, r),
                       waits=[(se, se.n)], chain=ve)
            return se.n, ve.n

        def apply_transpose(src_fn, slot_s, slot_nb, dst, fold_cols=None,
                            jgs=(0, 1, 2, 3), wait0=()):
            s_ = vslot(slot_s); nb = vslot(slot_nb)
            prev_evac = []
            pe_tp = pe.n
            jgs = list(jgs)
            for gi, jg in enumerate(jgs):
                for jj in range(4):
                    j = jg * 4 + jj
                    w = list(wait0) if (jj == 0 and gi == 0) else []
                    if jj == 0 and gi > 0:
                        w.append((pe, pe_tp))
                    p.emit("vector", lambda e, j=j, jj=jj: e.tensor_scalar(
                        stg[:, jj * 1024:(jj + 1) * 1024], src_fn(j),
                        s_[:, j:j + 1], nb[:, j:j + 1], ALU.mult, ALU.add),
                        waits=w, inc=ve)
                ve_app = ve.n
                for f in range(NF):
                    for jj in range(4):
                        w = []
                        if f == 0 and jj == 0:
                            w = [(ve, ve_app)] + prev_evac
                        p.emit("tensor", lambda e, f=f, jj=jj: e.transpose(
                            psb(f)[:, jj * 128:(jj + 1) * 128],
                            stg[:, jj * 1024 + f * 128:jj * 1024 + (f + 1) * 128],
                            ident[:]),
                            waits=w, inc=pe if (f == NF - 1 and jj == 3) else None)
                pe_tp = pe.n
                prev_evac = []
                for f in range(NF):
                    if fold_cols is not None:
                        p.emit("vector", lambda e, f=f, jg=jg: e.tensor_scalar_mul(
                            dst[:, f * T + jg * 512:f * T + (jg + 1) * 512],
                            psb(f)[:, 0:512], fold_cols[:, f:f + 1]),
                            waits=[(pe, pe_tp)], inc=ve)
                    else:
                        p.emit("vector", lambda e, f=f, jg=jg: e.tensor_copy(
                            dst[:, f * T + jg * 512:f * T + (jg + 1) * 512],
                            psb(f)[:, 0:512]),
                            waits=[(pe, pe_tp)], inc=ve)
                    prev_evac = [(ve, ve.n)]
            return se.n, ve.n, pe.n

        def load_slab(dst, srcT, waits=()):
            p.emit("gpsimd", lambda e, dst=dst, srcT=srcT: e.dma_start(
                dst[:, 0:8192].rearrange("p (a o) -> p a o", o=D),
                srcT.rearrange("(a p) o -> p a o", p=128)),
                waits=waits, inc=dw, inc_by=16)
            return dw.n

        def proj_fm(slab, w_tgt, rhs_tgt, evac_fn, w_sem=None):
            if w_sem is None:
                w_sem = dw
            guards = [None, None, None, None]
            for f in range(NF):
                for n in range(4):
                    b = (f * 4 + n) % 4
                    for a in range(NF):
                        w = []
                        if a == 0:
                            w = [(w_sem, w_tgt), (ve, rhs_tgt)]
                            if guards[b] is not None:
                                w.append(guards[b])
                        p.emit("tensor", lambda e, f=f, n=n, a=a, b=b, slab=slab:
                               e.matmul(
                            pbank(b),
                            slab[:, a * 1024 + f * 128:a * 1024 + (f + 1) * 128],
                            R1[:, a * T + n * 512:a * T + (n + 1) * 512],
                            start=(a == 0), stop=(a == NF - 1)),
                            waits=w, inc=pe if a == NF - 1 else None)
                    guards[b] = evac_fn(f, n, b, pe.n)
            return pe.n

        ln_stats(0, 1, extra_rms_slot=2)
        p.emit("vector", lambda e: e.scalar_tensor_tensor(
            vslot(15), vslot(0), -1.0, vslot(2), ALU.mult, ALU.mult), chain=ve)
        p.emit("vector", lambda e: e.scalar_tensor_tensor(
            vslot(3), vslot(0), 1.0, vslot(1), ALU.mult, ALU.mult), chain=ve)
        p.emit("vector", lambda e: e.tensor_scalar_mul(
            vslot(13), vslot(3), -1.0), chain=ve)
        apply_transpose(lambda j: x_res[:, j * D:(j + 1) * D], 2, 15, R1,
                        fold_cols=ln_cols[:, 0:NF],
                        wait0=[(ve, ve.n), (se, se.n)])

        def attn_sublayer(li, wq_d, wk_d, wv_d, att_scale):
            cc_i, cc_o = cc_in_h[li], cc_out_h[li]
            stop = STOP_STAGE if (STOP_LI is None or li == STOP_LI) else 99
            lim = ATT_LIMIT if (STOP_LI is None or li == STOP_LI) \
                else (99, 99, 99, 99)
            if stop < 2:
                return 0, 0, 0
            if li == 0:
                for dstv, tbl in ((scrf, rcos), (yscf, rsin)):
                    for pp in range(2):
                        p.emit("sync", lambda e, dstv=dstv, tbl=tbl, pp=pp:
                               e.dma_start(dstv[pp * 64:(pp + 1) * 64, 0:2048], tbl),
                               waits=[(ve, ve.n), (pe, pe.n)] if pp == 0 else [],
                               inc=dx, inc_by=16)
            RT_DX = dx.n
            HFM_VE = ve.n

            wq_t = load_slab(wA, wq_d, waits=[(pe, pe.n)])
            wk_t = load_slab(wB, wk_d)

            def evac_q(f, n, b, pet):
                p.emit("vector", lambda e, f=f, n=n, b=b: e.tensor_copy(
                    R2[:, f * T + n * 512:f * T + (n + 1) * 512], pbank(b)),
                    waits=[(pe, pet)], inc=ve)
                return (ve, ve.n)
            proj_fm(wA, wq_t, HFM_VE, evac_q)
            wv_t = load_slab(wA, wv_d, waits=[(pe, pe.n)])

            first = {}

            def normfold(zi, f, wse):
                for n2 in range(2):
                    p.emit("tensor", lambda e, n2=n2: e.matmul(
                        ps2[3][0:2, 0:512], bd2[:, 2:4],
                        probs[:, 2048 + n2 * 512:2048 + (n2 + 1) * 512],
                        start=True, stop=True),
                        waits=wse + [(ve, ve.n)], inc=pe)
                    nm_pe = pe.n
                    p.emit("vector", lambda e: e.reduce_max(
                        Mq(6)[:, 0:1],
                        ps2[3][0:2, 0:512].rearrange("p (a n) -> p a n", a=1),
                        axis=AX.X), waits=[(pe, nm_pe)], chain=ve)
                    key = (zi, f)
                    if key not in first:
                        first[key] = True
                        p.emit("vector", lambda e, zi=zi, f=f: e.tensor_copy(
                            Mq(zi)[:, f:f + 1], Mq(6)[:, 0:1]), chain=ve)
                    else:
                        p.emit("vector", lambda e, zi=zi, f=f: e.tensor_tensor(
                            Mq(zi)[:, f:f + 1], Mq(zi)[:, f:f + 1], Mq(6)[:, 0:1],
                            ALU.max), chain=ve)
                    wse = []

            st = {"ktile_free": 0, "kship": 0, "sq_pe": 0}

            def qkln(buf_fn, grow, grow2, nbf):
                for ch2 in range(2):
                    z2 = ch2 * 1024
                    p.emit("scalar", lambda e, z2=z2: e.activation(
                        probs[:, 2048:3072], buf_fn(z2), AF.Square),
                        waits=[(ve, ve.n), (se, se.n), (pe, st["sq_pe"])],
                        inc=se)
                    sq_se = se.n
                    for ch in range(2):
                        for hd in range(2):
                            rp = hd * 32
                            p.emit("tensor", lambda e, z2=z2, ch=ch, hd=hd,
                                   rp=rp: e.matmul(
                                ps2[2][rp:rp + 1, ch * 512:(ch + 1) * 512],
                                bd2[:, hd:hd + 1],
                                buf_fn(z2)[:, ch * 512:(ch + 1) * 512],
                                start=True, stop=True),
                                waits=[(ve, ve.n), (se, sq_se)]
                                if (ch == 0 and hd == 0) else [], inc=None)
                            p.emit("tensor", lambda e, ch=ch, hd=hd, rp=rp:
                                   e.matmul(
                                ps2[3][rp:rp + 1, ch * 512:(ch + 1) * 512],
                                bd2[:, hd:hd + 1],
                                probs[:, 2048 + ch * 512:2048 + (ch + 1) * 512],
                                start=True, stop=True),
                                inc=pe if (ch == 1 and hd == 1) else None)
                    mm_pe = pe.n
                    p.emit("vector", lambda e: e.tensor_copy(
                        yscf[0:1, 0:1024], ps2[2][0:1, 0:1024]),
                        waits=[(pe, mm_pe)], chain=ve)
                    p.emit("vector", lambda e: e.tensor_copy(
                        scrf[0:1, 0:1024], ps2[2][32:33, 0:1024]), chain=ve)
                    p.emit("vector", lambda e: e.tensor_copy(
                        yscf[32:33, 0:1024], ps2[3][0:1, 0:1024]), chain=ve)
                    p.emit("vector", lambda e: e.tensor_copy(
                        scrf[32:33, 0:1024], ps2[3][32:33, 0:1024]), chain=ve)
                    for mrow, srow in ((yscf, yscf), (scrf, scrf)):
                        p.emit("vector", lambda e, mrow=mrow, srow=srow:
                               e.tensor_mul(
                            srow[32:33, 1024:2048], mrow[0:1, 0:1024],
                            mrow[0:1, 0:1024]), chain=ve)
                        p.emit("vector", lambda e, srow=srow: e.tensor_sub(
                            srow[32:33, 0:1024], srow[32:33, 0:1024],
                            srow[32:33, 1024:2048]), chain=ve)
                        p.emit("vector", lambda e, srow=srow:
                               e.tensor_scalar_add(
                            srow[32:33, 0:1024], srow[32:33, 0:1024],
                            EPS_LN), chain=ve)
                    for srow in (yscf, scrf):
                        p.emit("scalar", lambda e, srow=srow: e.activation(
                            srow[32:33, 0:1024], srow[32:33, 0:1024], AF.Ln),
                            waits=[(ve, ve.n)] if srow is yscf else [],
                            chain=se)
                        p.emit("scalar", lambda e, srow=srow: e.activation(
                            srow[0:1, 1024:2048], srow[32:33, 0:1024],
                            AF.Exp, scale=-0.5), chain=se)
                    for mrow in (yscf, scrf):
                        p.emit("vector", lambda e, mrow=mrow: e.tensor_mul(
                            mrow[32:33, 1024:2048], mrow[0:1, 0:1024],
                            mrow[0:1, 1024:2048]),
                            waits=[(se, se.n)] if mrow is yscf else [],
                            chain=ve)
                    p.emit("vector", lambda e: e.tensor_copy(
                        stg[0:1, 0:1024], yscf[0:1, 1024:2048]), chain=ve)
                    p.emit("vector", lambda e: e.tensor_copy(
                        stg[0:1, 1024:2048], scrf[0:1, 1024:2048]), chain=ve)
                    p.emit("vector", lambda e: e.tensor_copy(
                        stg[0:1, 2048:3072], yscf[32:33, 1024:2048]), chain=ve)
                    p.emit("vector", lambda e: e.tensor_copy(
                        stg[0:1, 3072:4096], scrf[32:33, 1024:2048]), chain=ve)
                    ROWS_VE = ve.n
                    for ch in range(2):
                        p.emit("tensor", lambda e, ch=ch, grow=grow: e.matmul(
                            ps2[2][:, ch * 512:(ch + 1) * 512],
                            grow[0:1, :],
                            stg[0:1, ch * 512:(ch + 1) * 512],
                            start=True, stop=False),
                            waits=[(ve, ROWS_VE)] if ch == 0 else [],
                            inc=None)
                        p.emit("tensor", lambda e, ch=ch, grow2=grow2: e.matmul(
                            ps2[2][:, ch * 512:(ch + 1) * 512],
                            grow2[0:1, :],
                            stg[0:1, 1024 + ch * 512:1024 + (ch + 1) * 512],
                            start=False, stop=True), inc=None)
                        p.emit("tensor", lambda e, ch=ch, grow=grow: e.matmul(
                            ps2[3][:, ch * 512:(ch + 1) * 512],
                            grow[0:1, :],
                            stg[0:1, 2048 + ch * 512:2048 + (ch + 1) * 512],
                            start=True, stop=False), inc=None)
                        p.emit("tensor", lambda e, ch=ch, grow2=grow2: e.matmul(
                            ps2[3][:, ch * 512:(ch + 1) * 512],
                            grow2[0:1, :],
                            stg[0:1, 3072 + ch * 512:3072 + (ch + 1) * 512],
                            start=False, stop=False), inc=None)
                        p.emit("tensor", lambda e, ch=ch, nbf=nbf: e.matmul(
                            ps2[3][:, ch * 512:(ch + 1) * 512],
                            nbf[0:1, :], onesrow[0:1, 0:512],
                            start=False, stop=True),
                            inc=pe if ch == 1 else None)
                    bc_pe = pe.n
                    p.emit("vector", lambda e, z2=z2: e.tensor_mul(
                        probs[:, 2048:3072], buf_fn(z2), ps2[2][:, 0:1024]),
                        waits=[(pe, bc_pe), (ve, ve.n), (se, se.n)], inc=ve)
                    p.emit("vector", lambda e, z2=z2: e.tensor_sub(
                        buf_fn(z2), probs[:, 2048:3072], ps2[3][:, 0:1024]),
                        inc=ve)
                    st["sq_pe"] = pe.n

            def ktransform(f):
                if li == 1:
                    qkln(lambda z: probs[:, z:z + 1024], growk, growk2, nbkf)
                for h2 in range(2):
                    z = h2 * 1024
                    if li == 0:
                        for n2 in range(2):
                            p.emit("tensor", lambda e, z=z, n2=n2: e.matmul(
                                ps2[2][:, n2 * 512:(n2 + 1) * 512], pswap[:],
                                probs[:, z + n2 * 512:z + (n2 + 1) * 512],
                                start=True, stop=True),
                                waits=[(ve, ve.n)], inc=pe if n2 == 1 else None)
                        swp = pe.n
                        p.emit("vector", lambda e, z=z: e.tensor_mul(
                            probs[:, 2048:3072], probs[:, z:z + 1024],
                            scrf[:, 0:1024] if z == 0 else scrf[:, 1024:2048]),
                            waits=[(dx, RT_DX), (pe, st["sq_pe"])], inc=ve)
                        p.emit("vector", lambda e, z=z: e.tensor_mul(
                            probs[:, z:z + 1024],
                            ps2[2][:].bitcast(F32)[:, 0:1024],
                            yscf[:, 0:1024] if z == 0 else yscf[:, 1024:2048]),
                            waits=[(pe, swp)], inc=ve)
                        p.emit("vector", lambda e, z=z: e.tensor_add(
                            probs[:, z:z + 1024], probs[:, 2048:3072],
                            probs[:, z:z + 1024]), inc=ve)
                    kr_ve = ve.n
                    p.emit("scalar", lambda e, z=z: e.activation(
                        probs[:, 2048:3072], probs[:, z:z + 1024], AF.Square),
                        waits=[(ve, kr_ve), (se, se.n), (pe, st["sq_pe"])], inc=se)
                    normfold(1, f, [(se, se.n)])
                    st["sq_pe"] = pe.n
                    p.emit("sync", lambda e, f=f, z=z, h2=h2: e.dma_start(
                        flatap(cc_i, f * 128 * T + h2 * 1024,
                               [(T, 128), (1, 1024)]),
                        probs[:, z:z + 1024]),
                        waits=[(ve, kr_ve)], inc=dcc, inc_by=16)
                st["kship"] = dcc.n
                st["ktile_free"] = ve.n

            def evac_k(f, n, b, pet):
                w = [(pe, pet)]
                if n == 0:
                    w += [(ve, st["ktile_free"]), (dcc, st["kship"])]
                p.emit("vector", lambda e, n=n, b=b: e.tensor_copy(
                    probs[:, n * 512:(n + 1) * 512], pbank(b)), waits=w, inc=ve)
                if n == 3:
                    ktransform(f)
                return (ve, ve.n)
            proj_fm(wB, wk_t, HFM_VE, evac_k)

            for f in range(NF):
                if li == 1:
                    qkln(lambda z, f=f: R2[:, f * T + z:f * T + z + 1024],
                         growq, growq2, nbqf)
                for h2 in range(2):
                    z = h2 * 1024
                    base = f * T + h2 * 1024
                    if li == 0:
                        for n2 in range(2):
                            p.emit("tensor", lambda e, base=base, n2=n2: e.matmul(
                                ps2[2][:, n2 * 512:(n2 + 1) * 512], pswap[:],
                                R2[:, base + n2 * 512:base + (n2 + 1) * 512],
                                start=True, stop=True),
                                waits=[(ve, ve.n)], inc=pe if n2 == 1 else None)
                        swp = pe.n
                        p.emit("vector", lambda e, base=base, z=z: e.tensor_mul(
                            probs[:, 0:1024], R2[:, base:base + 1024],
                            scrf[:, 0:1024] if z == 0 else scrf[:, 1024:2048]),
                            waits=[(dx, RT_DX), (ve, st["ktile_free"]),
                                   (dcc, st["kship"]), (pe, st["sq_pe"])], inc=ve)
                        p.emit("vector", lambda e, z=z: e.tensor_mul(
                            probs[:, 1024:2048],
                            ps2[2][:].bitcast(F32)[:, 0:1024],
                            yscf[:, 0:1024] if z == 0 else yscf[:, 1024:2048]),
                            waits=[(pe, swp)], inc=ve)
                        p.emit("vector", lambda e, base=base: e.tensor_add(
                            R2[:, base:base + 1024], probs[:, 0:1024],
                            probs[:, 1024:2048]), inc=ve)
                    p.emit("scalar", lambda e, base=base: e.activation(
                        probs[:, 2048:3072], R2[:, base:base + 1024], AF.Square),
                        waits=[(ve, ve.n), (se, se.n), (pe, st["sq_pe"])], inc=se)
                    normfold(0, f, [(se, se.n)])
                    st["sq_pe"] = pe.n
            Q_SE, Q_VE = se.n, ve.n

            st["vship"] = dcc.n

            def vtransform(f):
                vev = ve.n
                for g2 in range(2):
                    for jj in range(8):
                        w = [(ve, vev)] if jj == 0 else []
                        if jj == 0 and g2 == 0:
                            w.append((pe, st["sq_pe"]))
                        p.emit("tensor", lambda e, g2=g2, jj=jj: e.transpose(
                            ps2[2][:].bitcast(BF16)[:, jj * 128:(jj + 1) * 128],
                            probs[:, g2 * 1024 + jj * 128:g2 * 1024 + (jj + 1) * 128],
                            ident[:]), waits=w, inc=pe if jj == 7 else None)
                    tp_pe = pe.n
                    p.emit("vector", lambda e: e.tensor_copy(
                        probs[:, 2048:3072], ps2[2][:].bitcast(BF16)[:, 0:1024]),
                        waits=[(pe, tp_pe), (dcc, st["vship"])], inc=ve)
                    evv = ve.n
                    p.emit("sync", lambda e, f=f, g2=g2: e.dma_start(
                        flatap(cc_i, VOFF + f * T * 128 + g2 * 1024 * 128,
                               [(128, 128), (16384, 8), (1, 128)]),
                        probs[:, 2048:3072].rearrange("p (a d) -> p a d", d=128)),
                        waits=[(ve, evv)], inc=dcc, inc_by=16)
                    st["vship"] = dcc.n
                    vev = ve.n
                st["sq_pe"] = pe.n

            def evac_v(f, n, b, pet):
                w = [(pe, pet)]
                if n == 0:
                    w.append((dcc, st["vship"]))
                if f == 0 and n == 0:
                    w += [(ve, Q_VE), (se, Q_SE)]
                p.emit("vector", lambda e, n=n, b=b: e.tensor_copy(
                    probs[:, n * 512:(n + 1) * 512], pbank(b)), waits=w, inc=ve)
                if n == 3:
                    vtransform(f)
                return (ve, ve.n)
            proj_fm(wA, wv_t, HFM_VE, evac_v)

            p.emit("sync", lambda e: e.dma_start(
                flatap(cc_i, MOFF, [(16, 2), (1, 16)]), Mq(1).bitcast(BF16)),
                waits=[(ve, ve.n)], inc=dcc, inc_by=16)
            CC_W = dcc.n
            if stop < 3:
                return 0, 0, 0

            p.emit("gpsimd", lambda e: e.collective_compute(
                "AllGather", ALU.bypass,
                replica_groups=[[0, 1], [2, 3], [4, 5], [6, 7]],
                ins=[cc_i.ap().opt()], outs=[cc_o.ap().opt()]),
                waits=[(dcc, CC_W)], inc=dcc, inc_by=1)
            CC_DONE = dcc.n

            for r in range(2):
                p.emit("sync", lambda e, r=r: e.dma_start(
                    Mq(2 + r).bitcast(BF16),
                    flatap(cc_o, r * CCE + MOFF, [(16, 2), (1, 16)])),
                    waits=[(dcc, CC_DONE)] if r == 0 else [], inc=dx, inc_by=16)
            KM_DX = dx.n
            p.emit("vector", lambda e: e.tensor_tensor(Mq(4), Mq(2), Mq(3), ALU.max),
                   waits=[(dx, KM_DX), (ve, ve.n)], chain=ve)
            p.emit("vector", lambda e: e.tensor_mul(Mq(5), Mq(0), Mq(4)), chain=ve)
            p.emit("sync", lambda e: e.dma_start(Mr1(48, 64), Mq(5)),
                   waits=[(ve, ve.n)], inc=dx, inc_by=16)
            p.emit("vector", lambda e: e.tensor_tensor(
                Mr1(32, 40), Mr1(48, 56), Mr1(56, 64), ALU.max),
                waits=[(dx, dx.n)], chain=ve)
            p.emit("scalar", lambda e: e.activation(Mr1(32, 40), Mr1(32, 40), AF.Ln),
                   waits=[(ve, ve.n)], chain=se)
            p.emit("scalar", lambda e: e.activation(
                Mr1(32, 40), Mr1(32, 40), AF.Exp, scale=0.5), chain=se)
            p.emit("vector", lambda e: e.tensor_scalar(
                Mr1(32, 40), Mr1(32, 40), -att_scale, BSHIFT, ALU.mult, ALU.add),
                waits=[(se, se.n)], chain=ve)
            p.emit("vector", lambda e: e.tensor_copy(probs[0:1, 0:8], Mr1(32, 40)),
                   chain=ve)
            BR_VE = ve.n
            p.emit("tensor", lambda e: e.matmul(
                pbank(0)[:, 0:8], onesrow[0:1, 0:128], probs[0:1, 0:8],
                start=True, stop=True), waits=[(ve, BR_VE)], inc=pe)
            for wbuf in (wA, wB):
                p.emit("vector", lambda e, wbuf=wbuf: e.memset(
                    flatap(wbuf, 4096 + 64, [(wbuf.shape[1], 128), (130, 32), (1, 1)]),
                    1.0), waits=[(pe, pe.n), (dcc, CC_W)], chain=ve)
                p.emit("vector", lambda e, wbuf=wbuf: e.memset(
                    flatap(wbuf, 4096 + 129, [(wbuf.shape[1], 128), (130, 32), (1, 1)]),
                    1.0), chain=ve)
            p.emit("vector", lambda e: e.tensor_copy(
                biascol[:, 0:8], pbank(0)[:, 0:8]), waits=[(pe, pe.n)], inc=ve)
            BIAS_VE = ve.n
            if stop < 4:
                return 0, 0, 0

            exp_t = {}
            pv_t = {}
            pair_done = {}
            prev_chunk_gate = [(ve, BIAS_VE)]
            LP, LC, LT, LE = lim
            for pr in range(min(NP, LP)):
                wbuf = wA if pr % 2 == 0 else wB
                W0 = wbuf.shape[1]
                w0 = [(dcc, CC_DONE)]
                if pr >= 2:
                    w0.append((pe, pair_done[pr - 2]))
                for r in range(2):
                    p.emit("sync", lambda e, r=r, pr=pr, wbuf=wbuf: e.dma_start(
                        wbuf[:, r * 2048:(r + 1) * 2048],
                        flatap(cc_o, r * CCE + pr * 128 * T,
                               [(T, 128), (1, 2048)])),
                        waits=w0 if r == 0 else [], inc=dcc, inc_by=16)
                    for s_ in range(2):
                        p.emit("sync", lambda e, r=r, s_=s_, pr=pr, wbuf=wbuf,
                               W0=W0: e.dma_start(
                            flatap(wbuf, 4096 + r * 130 + s_ * 65,
                                   [(W0, 128), (260, 16), (1, 64)]),
                            flatap(cc_o, r * CCE + VOFF + pr * T * 128 + s_ * 64,
                                   [(128, 128), (16384, 16), (1, 64)])),
                            inc=dcc, inc_by=16)
                load_t = dcc.n
                for c in range(min(4, LC)):
                    nkt = 8 * c + 8
                    for t in range(min(nkt, LT)):
                        sl = t % 3
                        jmin = max(4 * c, -(-(t - 1) // 2))
                        qoff = (jmin - 4 * c) * 128
                        qbase = pr * T + c * 512
                        tcol = (t % 2) * 2048 + (t // 2) * 128
                        w = [(dcc, load_t)]
                        if (pr, c, t - 3) in exp_t:
                            w.append((se, exp_t[(pr, c, t - 3)]))
                        if t == 0:
                            w += prev_chunk_gate
                        p.emit("tensor", lambda e, sl=sl, qoff=qoff, qbase=qbase,
                               tcol=tcol, wbuf=wbuf: e.matmul(
                            ps2[sl][:, qoff:512],
                            wbuf[0:64, tcol:tcol + 128],
                            R2[0:64, qbase + qoff:qbase + 512],
                            start=True, stop=True, tile_position=(0, 0)),
                            waits=w)
                        p.emit("tensor", lambda e, sl=sl, qoff=qoff, qbase=qbase,
                               tcol=tcol, wbuf=wbuf: e.matmul(
                            ps2[sl][:, 512 + qoff:1024],
                            wbuf[64:128, tcol:tcol + 128],
                            R2[64:128, qbase + qoff:qbase + 512],
                            start=True, stop=True, tile_position=(64, 0)),
                            inc=pe)
                        qk_pe = pe.n
                        wexp = [(pe, qk_pe)]
                        if (pr, c, t - 3) in pv_t:
                            wexp.append((pe, pv_t[(pr, c, t - 3)]))
                        N_t = 512 - qoff
                        p.emit("scalar", lambda e, sl=sl, qoff=qoff, pr=pr,
                               N_t=N_t: e.activation(
                            flatap(probs, sl * 1024 + qoff,
                                   [(3072, 128), (512, 2), (1, N_t)]),
                            flatap(ps2[sl], qoff, [(1024, 128), (512, 2), (1, N_t)]),
                            AF.Exp, bias=biascol[:, pr:pr + 1], scale=att_scale),
                            waits=wexp, inc=se)
                        exp_t[(pr, c, t)] = se.n
                        gate = (se, se.n)
                        jj = t // 2
                        if 4 * c <= jj < 4 * c + 4:
                            sk = t % 2
                            p.emit("vector", lambda e, sl=sl, qoff=qoff, sk=sk:
                                   e.tensor_mul(
                                flatap(probs, sl * 1024 + qoff,
                                       [(3072, 128), (512, 2), (1, 128)]),
                                flatap(probs, sl * 1024 + qoff,
                                       [(3072, 128), (512, 2), (1, 128)]),
                                flatap(mask, sk * 256,
                                       [(512, 128), (128, 2), (1, 128)])),
                                waits=[(se, se.n)], inc=ve)
                            gate = (ve, ve.n)
                        for hd in range(2):
                            p.emit("tensor", lambda e, qoff=qoff, sl=sl, hd=hd,
                                   t=t, wbuf=wbuf, nkt=nkt: e.matmul(
                                ps2[3][0:65, hd * 512 + qoff:(hd + 1) * 512],
                                wbuf[:, 4096 + t * 130 + hd * 65:
                                     4096 + t * 130 + hd * 65 + 65],
                                probs[:, sl * 1024 + hd * 512 + qoff:
                                      sl * 1024 + (hd + 1) * 512],
                                start=(t == 0), stop=(t == nkt - 1)),
                                waits=[gate] if hd == 0 else [],
                                inc=pe if hd == 1 else None)
                        pv_t[(pr, c, t)] = pe.n
                    if LE == 1:
                        p.emit("vector", lambda e: e.tensor_copy(
                            R2[:].bitcast(F32)[:, 0:512], ps2[3][:, 0:512]),
                            waits=[(pe, pv_t[(pr, c, nkt - 1)]),
                                   (se, exp_t[(pr, c, nkt - 1)]), (ve, ve.n)],
                            chain=ve)
                        p.emit("vector", lambda e: e.tensor_copy(
                            R2[:].bitcast(F32)[:, 512:1024], ps2[3][:, 512:1024]),
                            chain=ve)
                        prev_chunk_gate = [(ve, ve.n)]
                        continue
                    if LE == 0 or LT < nkt:
                        prev_chunk_gate = [(ve, ve.n)]
                        continue
                    PV_GATE = exp_t[(pr, c, nkt - 1)]
                    PV_END = pv_t[(pr, c, nkt - 1)]
                    p.emit("vector", lambda e: e.reciprocal(
                        rrowf[0:1, 0:512], ps2[3][64:65, 0:512]),
                        waits=[(pe, PV_END), (ve, ve.n)], chain=ve)
                    p.emit("vector", lambda e: e.reciprocal(
                        rrowf[0:1, 512:1024], ps2[3][64:65, 512:1024]), chain=ve)
                    p.emit("vector", lambda e: e.tensor_copy(
                        probs[0:1, 0:1024], rrowf[0:1, 0:1024]), chain=ve)
                    RB_VE = ve.n
                    if LE == 2:
                        prev_chunk_gate = [(ve, ve.n)]
                        continue
                    p.emit("tensor", lambda e: e.matmul(
                        ps2[0][:, 0:512], e1[0:1, :], probs[0:1, 0:512],
                        start=True, stop=False),
                        waits=[(ve, RB_VE), (se, PV_GATE)])
                    p.emit("tensor", lambda e: e.matmul(
                        ps2[0][:, 0:512], e2[0:1, :], probs[0:1, 512:1024],
                        start=False, stop=True), inc=pe)
                    RB_PE = pe.n
                    if LE == 3:
                        prev_chunk_gate = [(ve, ve.n)]
                        continue
                    p.emit("vector", lambda e: e.tensor_copy(
                        probs[:, 1024:1536], ps2[0][:, 0:512]),
                        waits=[(pe, RB_PE)], inc=ve)
                    RB2_VE = ve.n
                    if LE == 4:
                        prev_chunk_gate = [(ve, ve.n)]
                        continue
                    for hd in range(2):
                        p.emit("vector", lambda e, hd=hd, pr=pr, c=c: e.tensor_mul(
                            R2[hd * 64:(hd + 1) * 64,
                               pr * T + c * 512:pr * T + (c + 1) * 512],
                            ps2[3][0:64, hd * 512:(hd + 1) * 512],
                            probs[hd * 64:(hd + 1) * 64, 1024:1536]),
                            waits=[(ve, RB2_VE)] if hd == 0 else [], inc=ve)
                    prev_chunk_gate = [(ve, ve.n)]
                pair_done[pr] = pe.n
            return ve.n, pe.n, se.n

        attn_sublayer(0, wqT, wkT, wvT, 0.125)

        wo_t = load_slab(wB, woT, waits=[(pe, pe.n), (dcc, dcc.n)])
        s5p = S5_PARTS
        for c in (range(4) if stop >= 5 else []):
            tp_prev = []
            for f in range(NF):
                b = f % 4
                for a in range(NF):
                    w = []
                    if a == 0:
                        w = [(dw, wo_t), (ve, ve.n)]
                    p.emit("tensor", lambda e, f=f, c=c, a=a, b=b: e.matmul(
                        pbank(b),
                        wB[:, a * 1024 + f * 128:a * 1024 + (f + 1) * 128],
                        R2[:, a * T + c * 512:a * T + (c + 1) * 512],
                        start=(a == 0), stop=(a == NF - 1)),
                        waits=w, inc=pe if a == NF - 1 else None)
                wo_pe = pe.n
                p.emit("vector", lambda e, b=b: e.tensor_copy(
                    stg[:, 0:512], pbank(b)), waits=[(pe, wo_pe)] + tp_prev, inc=ve)
                ev_ve = ve.n
                for jj in range(4):
                    p.emit("tensor", lambda e, jj=jj: e.transpose(
                        psb(4)[:, jj * 128:(jj + 1) * 128],
                        stg[:, jj * 128:(jj + 1) * 128], ident[:]),
                        waits=[(ve, ev_ve)] if jj == 0 else [],
                        inc=pe if jj == 3 else None)
                tp_pe = pe.n
                tp_prev = [(pe, tp_pe)]
                p.emit("vector", lambda e, f=f: e.tensor_copy(
                    flatap(ysc, f * 128, [(4096, 128), (1024, 4), (1, 128)]),
                    psb(4)[:, 0:512].rearrange("p (a d) -> p a d", d=128)),
                    waits=[(pe, tp_pe)], inc=ve)
            YT_VE = ve.n
            if s5p < 2:
                continue
            for jj in range(4):
                j = c * 4 + jj
                p.emit("vector", lambda e, j=j, jj=jj: e.scalar_tensor_tensor(
                    ysc[:, jj * 1024:(jj + 1) * 1024],
                    x_res[:, j * D:(j + 1) * D], vslot(1)[:, j:j + 1],
                    ysc[:, jj * 1024:(jj + 1) * 1024], ALU.mult, ALU.add),
                    waits=[(ve, YT_VE)] if jj == 0 else [], inc=ve)
            T_VE = ve.n
            if s5p < 3:
                continue
            for jj in range(4):
                j = c * 4 + jj
                wsq = [(ve, T_VE), (se, se.n)] if jj < 2 else [(ve, ve.n)]
                p.emit("scalar", lambda e, j=j, jj=jj: e.activation(
                    scrf[:, (jj % 2) * 1024:(jj % 2) * 1024 + 1024],
                    ysc[:, jj * 1024:(jj + 1) * 1024], AF.Square,
                    bias=vslot(13)[:, j:j + 1]), waits=wsq, inc=se)
                p.emit("vector", lambda e, j=j, jj=jj: e.reduce_sum(
                    vslot(4)[:, j:j + 1],
                    scrf[:, (jj % 2) * 1024:(jj % 2) * 1024 + 1024].rearrange(
                        "p (a d) -> p a d", a=1), axis=AX.X),
                    waits=[(se, se.n)], chain=ve)
            c4 = vslot(5)[:, c * 4:c * 4 + 4]
            s4 = vslot(4)[:, c * 4:c * 4 + 4]
            p.emit("vector", lambda e, c4=c4, s4=s4: e.tensor_scalar(
                c4, s4, 1.0 / D, EPS_RMS, ALU.mult, ALU.add), chain=ve)
            p.emit("scalar", lambda e, c4=c4: e.activation(c4, c4, AF.Ln),
                   waits=[(ve, ve.n)], chain=se)
            p.emit("scalar", lambda e, c4=c4: e.activation(
                c4, c4, AF.Exp, scale=-0.5), chain=se)
            n4 = vslot(6)[:, c * 4:c * 4 + 4]
            m4 = vslot(3)[:, c * 4:c * 4 + 4]
            p.emit("vector", lambda e, n4=n4, m4=m4, c4=c4:
                   e.scalar_tensor_tensor(n4, m4, -1.0, c4, ALU.mult, ALU.mult),
                   waits=[(se, se.n)], chain=ve)
            apply_transpose(lambda j, c=c: ysc[:, (j - c * 4) * 1024:
                                               (j - c * 4 + 1) * 1024],
                            5, 6, R1, fold_cols=ln_cols[:, NF:2 * NF], jgs=(c,),
                            wait0=[(ve, ve.n), (se, se.n)])
            if s5p < 4:
                continue
            for jj in range(4):
                j = c * 4 + jj
                p.emit("vector", lambda e, j=j, jj=jj: e.tensor_add(
                    x_res[:, j * D:(j + 1) * D], x_res[:, j * D:(j + 1) * D],
                    ysc[:, jj * 1024:(jj + 1) * 1024]),
                    waits=[(ve, ve.n)] if jj == 0 else [], inc=ve)
                p.emit("vector", lambda e, j=j: e.tensor_scalar(
                    x_res[:, j * D:(j + 1) * D], x_res[:, j * D:(j + 1) * D],
                    vslot(3)[:, j:j + 1], None, ALU.subtract), inc=ve)
        H2_VE, H2_SE, H2_PE = ve.n, se.n, pe.n

        for srcT, dstbf, OW in ((w1T, w1bf, DFF), (w3T, w3bf, DFF), (w2T, w2bf, D)):
            ng = (D * DFF) // (128 * OW)
            p.emit("gpsimd", lambda e, s=srcT, d=dstbf, OW=OW, ng=ng: e.dma_start(
                flatap(d, 0, [(OW, 128), (OW * 128, ng), (1, OW)]),
                s.rearrange("(g p) o -> p g o", p=128)),
                inc=dw, inc_by=16)
        PRECAST = dw.n

        for c in (range(4) if stop >= 6 else []):
            wslab_t = {}

            def emit_ffn_load(gh, c, wv):
                zA = (gh % 2) * 4096
                p.emit("gpsimd", lambda e, gh=gh, zA=zA: e.dma_start(
                    wA[:, zA:zA + 4096].rearrange("p (a o) -> p a o", o=512),
                    flatap(w1bf, gh * 512, [(DFF, 128), (DFF * 128, 8), (1, 512)])),
                    waits=wv, inc=dw, inc_by=16)
                p.emit("gpsimd", lambda e, gh=gh, zA=zA: e.dma_start(
                    wB[:, zA:zA + 4096].rearrange("p (a o) -> p a o", o=512),
                    flatap(w3bf, gh * 512, [(DFF, 128), (DFF * 128, 8), (1, 512)])),
                    inc=dw, inc_by=16)
                wslab_t[gh] = dw.n

            emit_ffn_load(0, c, [(dw, PRECAST), (pe, pe.n)])
            emit_ffn_load(1, c, [])
            for gh in range(8):
                zA = (gh % 2) * 4096
                for dt in range(4):
                    dtg = gh * 4 + dt
                    b1 = dtg % 2
                    b3 = 2 + dtg % 2
                    for a in range(NF):
                        w = [] if a else [(dw, wslab_t[gh]),
                                          (se, se.n), (ve, ve.n)]
                        p.emit("tensor", lambda e, zA=zA, dt=dt, a=a, b1=b1, c=c:
                               e.matmul(
                            pbank(b1),
                            wA[:, zA + a * 512 + dt * 128:zA + a * 512 + (dt + 1) * 128],
                            R1[:, a * T + c * 512:a * T + (c + 1) * 512],
                            start=(a == 0), stop=(a == NF - 1)),
                            waits=w, inc=pe if a == NF - 1 else None)
                    p1_pe = pe.n
                    for a in range(NF):
                        p.emit("tensor", lambda e, zA=zA, dt=dt, a=a, b3=b3, c=c:
                               e.matmul(
                            pbank(b3),
                            wB[:, zA + a * 512 + dt * 128:zA + a * 512 + (dt + 1) * 128],
                            R1[:, a * T + c * 512:a * T + (c + 1) * 512],
                            start=(a == 0), stop=(a == NF - 1)),
                            waits=[(ve, ve.n)] if a == 0 else [],
                            inc=pe if a == NF - 1 else None)
                    p3_pe = pe.n
                    p.emit("scalar", lambda e, b1=b1, dtg=dtg: e.activation(
                        probs[:, 0:512] if dtg % 2 == 0 else probs[:, 512:1024],
                        pbank(b1), AF.Silu),
                        waits=[(pe, p1_pe), (se, se.n), (ve, ve.n)], inc=se)
                    p.emit("vector", lambda e, b3=b3, dtg=dtg: e.tensor_mul(
                        R2[:, dtg * 512:(dtg + 1) * 512],
                        probs[:, 0:512] if dtg % 2 == 0 else probs[:, 512:1024],
                        pbank(b3)),
                        waits=[(pe, p3_pe), (se, se.n)], inc=ve)
                if gh + 2 < 8:
                    emit_ffn_load(gh + 2, c, [(pe, pe.n)])
            P_VE = ve.n
            w2_t = {}

            def emit_w2_load(f, wv):
                zB = (f % 2) * 4096
                p.emit("gpsimd", lambda e, f=f, zB=zB: e.dma_start(
                    wA[:, zB:zB + 4096].rearrange("p (a o) -> p a o", o=128),
                    flatap(w2bf, f * 128, [(D, 128), (D * 128, 32), (1, 128)])),
                    waits=wv, inc=dw, inc_by=16)
                w2_t[f] = dw.n

            emit_w2_load(0, [(dw, PRECAST), (ve, P_VE), (pe, pe.n)])
            emit_w2_load(1, [])
            tp_prev = []
            bank_guard = {0: None, 1: None}
            for f in range(NF):
                zB = (f % 2) * 4096
                b = f % 2
                for dt in range(32):
                    w = [] if dt else ([(dw, w2_t[f]), (ve, P_VE)] +
                                       ([bank_guard[b]] if bank_guard[b] else []))
                    p.emit("tensor", lambda e, zB=zB, dt=dt, f=f, b=b: e.matmul(
                        pbank(b), wA[:, zB + dt * 128:zB + (dt + 1) * 128],
                        R2[:, dt * 512:(dt + 1) * 512],
                        start=(dt == 0), stop=(dt == 31)),
                        waits=w, inc=pe if dt == 31 else None)
                ff_pe = pe.n
                p.emit("vector", lambda e, b=b: e.tensor_copy(
                    stg[:, 2048:2560], pbank(b)),
                    waits=[(pe, ff_pe)] + tp_prev, inc=ve)
                bank_guard[b] = (ve, ve.n)
                ev_ve = ve.n
                for jj in range(4):
                    p.emit("tensor", lambda e, jj=jj: e.transpose(
                        psb(4)[:, jj * 128:(jj + 1) * 128],
                        stg[:, 2048 + jj * 128:2048 + (jj + 1) * 128], ident[:]),
                        waits=[(ve, ev_ve)] if jj == 0 else [],
                        inc=pe if jj == 3 else None)
                tp_pe = pe.n
                tp_prev = [(pe, tp_pe)]
                p.emit("vector", lambda e, f=f, c=c: e.tensor_add(
                    flatap(x_res, c * 4 * D + f * 128, [(NT * D, 128), (D, 4), (1, 128)]),
                    flatap(x_res, c * 4 * D + f * 128, [(NT * D, 128), (D, 4), (1, 128)]),
                    psb(4)[:, 0:512].rearrange("p (a d) -> p a d", d=128)),
                    waits=[(pe, tp_pe), (ve, ve.n)], inc=ve)
                if f + 2 < NF:
                    emit_w2_load(f + 2, [(pe, pe.n)])
        S1_VE, S1_SE, S1_PE = ve.n, se.n, pe.n

        def ffn_q(first):
            ln_stats(7, 8)
            mx = vslot(9); mn = vslot(10); av = vslot(11); cz = vslot(12)
            nbz = vslot(14)
            p.emit("vector", lambda e: e.reduce_max(
                mx, x_res[:].rearrange("p (j d) -> p j d", d=D), axis=AX.X),
                chain=ve)
            p.emit("vector", lambda e: e.tensor_reduce(
                mn, x_res[:].rearrange("p (j d) -> p j d", d=D), axis=AX.X,
                op=ALU.min), chain=ve)
            p.emit("vector", lambda e: e.tensor_sub(mx, mx, vslot(7)), chain=ve)
            p.emit("vector", lambda e: e.tensor_sub(mn, vslot(7), mn), chain=ve)
            p.emit("vector", lambda e: e.tensor_tensor(av, mx, mn, ALU.max), chain=ve)
            p.emit("vector", lambda e: e.tensor_mul(av, av, vslot(8)),
                   waits=[(se, se.n)], chain=ve)
            p.emit("vector", lambda e: e.tensor_scalar_add(av, av, EPS_Q), chain=ve)
            p.emit("vector", lambda e: e.reciprocal(cz, av), chain=ve)
            p.emit("vector", lambda e: e.tensor_scalar_mul(cz, cz, 127.0), chain=ve)
            p.emit("vector", lambda e: e.tensor_mul(cz, cz, vslot(8)), chain=ve)
            p.emit("vector", lambda e: e.scalar_tensor_tensor(
                nbz, vslot(7), -1.0, cz, ALU.mult, ALU.mult), chain=ve)
            LNQ_VE, LNQ_SE = ve.n, se.n

            if first:
                R2f = R2[:].bitcast(F32)
                p.emit("sync", lambda e: e.dma_start(
                    R2f[:, 0:8192].rearrange("p (a o) -> p a o", o=D),
                    fwT.rearrange("(a p) o -> p a o", p=128)),
                    waits=[(pe, pe.n), (ve, ve.n)], inc=dx, inc_by=16)
                FW_DX = dx.n
                p.emit("vector", lambda e: e.reduce_sum(
                    colsf[:, 4:5], R2f[:, 0:8192].rearrange(
                        "p (a o) -> p a o", a=1), axis=AX.X,
                    apply_absolute_value=True),
                    waits=[(dx, FW_DX)], chain=ve)
                p.emit("sync", lambda e: e.dma_start(
                    scrf[0:1, 0:128], colsf[:, 4:5]),
                    waits=[(ve, ve.n)], inc=dx, inc_by=16)
                p.emit("vector", lambda e: e.reduce_sum(
                    Mr1(32, 33), scrf[0:1, 0:128].rearrange("p (a o) -> p a o", a=1),
                    axis=AX.X), waits=[(dx, dx.n)], chain=ve)
                p.emit("vector", lambda e: e.tensor_scalar(
                    Mr1(32, 33), Mr1(32, 33), 1.0 / (D * D), EPS_Q,
                    ALU.mult, ALU.add), chain=ve)
                p.emit("vector", lambda e: e.reciprocal(
                    Mr1(33, 34), Mr1(32, 33)), chain=ve)
                p.emit("vector", lambda e: e.tensor_scalar_mul(
                    Mr1(34, 35), Mr1(32, 33), 1.0 / 127.0), chain=ve)
                WS_VE = ve.n
                p.emit("tensor", lambda e: e.matmul(
                    pbank(0)[:, 0:2], onesf32[0:1, :], Mr1(33, 35),
                    start=True, stop=True), waits=[(ve, WS_VE), (pe, pe.n)], inc=pe)
                p.emit("vector", lambda e: e.tensor_copy(
                    colsf[:, 5:7], pbank(0)[:, 0:2]), waits=[(pe, pe.n)], chain=ve)
                for ch in range(4):
                    z = ch * 2048
                    p.emit("vector", lambda e, z=z: e.tensor_scalar_mul(
                        R2f[:, z:z + 2048], R2f[:, z:z + 2048], colsf[:, 5:6]),
                        chain=ve)
                    p.emit("vector", lambda e, z=z: e.tensor_copy(
                        iscr[:, 0:2048], R2f[:, z:z + 2048]), chain=ve)
                    p.emit("vector", lambda e, z=z: e.tensor_copy(
                        R2f[:, z:z + 2048], iscr[:, 0:2048]), chain=ve)
                    p.emit("vector", lambda e, z=z: e.tensor_scalar(
                        R2f[:, z:z + 2048], R2f[:, z:z + 2048], 1.0, -1.0,
                        ALU.min, ALU.max), chain=ve)
                    p.emit("vector", lambda e, z=z: e.tensor_copy(
                        wA[:, z:z + 2048], R2f[:, z:z + 2048]), chain=ve)
                WQ_VE = ve.n
                p.emit("sync", lambda e: e.dma_start(wq_park.ap(), wA[:, 0:8192]),
                       waits=[(ve, WQ_VE)], inc=dcc, inc_by=16)
                wtgt, wsem = WQ_VE, ve
            else:
                p.emit("gpsimd", lambda e: e.dma_start(
                    wA[:, 0:8192], wq_park.ap()),
                    waits=[(pe, pe.n)], inc=dw, inc_by=16)
                wtgt, wsem = dw.n, dw

            prev_evac = []
            pe_tp = pe.n
            for jg in range(4):
                for jj in range(4):
                    j = jg * 4 + jj
                    w = [(ve, ve.n), (se, se.n)] if (jg == 0 and jj == 0) else []
                    if jj == 0 and jg > 0:
                        w.append((pe, pe_tp))
                    p.emit("vector", lambda e, j=j, jj=jj: e.tensor_scalar(
                        scrf[:, 0:1024], x_res[:, j * D:(j + 1) * D],
                        cz[:, j:j + 1], nbz[:, j:j + 1], ALU.mult, ALU.add),
                        waits=w, chain=ve)
                    p.emit("vector", lambda e, jj=jj: e.tensor_copy(
                        iscr[:, 1024:2048], scrf[:, 0:1024]), chain=ve)
                    p.emit("vector", lambda e, jj=jj: e.tensor_copy(
                        ysc[:, jj * 1024:(jj + 1) * 1024], iscr[:, 1024:2048]),
                        chain=ve)
                ve_app = ve.n
                for f in range(NF):
                    for jj in range(4):
                        w = []
                        if f == 0 and jj == 0:
                            w = [(ve, ve_app)] + prev_evac
                        p.emit("tensor", lambda e, f=f, jj=jj: e.transpose(
                            psb(f)[:, jj * 128:(jj + 1) * 128],
                            ysc[:, jj * 1024 + f * 128:jj * 1024 + (f + 1) * 128],
                            ident[:]),
                            waits=w, inc=pe if (f == NF - 1 and jj == 3) else None)
                pe_tp = pe.n
                prev_evac = []
                for f in range(NF):
                    p.emit("vector", lambda e, f=f, jg=jg: e.tensor_copy(
                        R1[:, f * T + jg * 512:f * T + (jg + 1) * 512],
                        psb(f)[:, 0:512]), waits=[(pe, pe_tp)], inc=ve)
                    prev_evac = [(ve, ve.n)]
            Z_VE = ve.n

            guards = [None] * 4
            tp_prev = []
            for f in range(NF):
                for n in range(4):
                    b = (f * 4 + n) % 4
                    for a in range(NF):
                        w = []
                        if a == 0:
                            w = [(wsem, wtgt), (ve, Z_VE)]
                            if guards[b] is not None:
                                w.append(guards[b])
                        p.emit("tensor", lambda e, f=f, n=n, a=a, b=b: e.matmul(
                            pbank(b),
                            wA[:, a * 1024 + f * 128:a * 1024 + (f + 1) * 128],
                            R1[:, a * T + n * 512:a * T + (n + 1) * 512],
                            start=(a == 0), stop=(a == NF - 1)),
                            waits=w, inc=pe if a == NF - 1 else None)
                    q_pe = pe.n
                    p.emit("vector", lambda e, b=b: e.tensor_scalar_mul(
                        probs[:, 0:512], pbank(b), colsf[:, 6:7]),
                        waits=[(pe, q_pe)] + tp_prev, inc=ve)
                    guards[b] = (ve, ve.n)
                    ev_ve = ve.n
                    for jj in range(4):
                        p.emit("tensor", lambda e, jj=jj: e.transpose(
                            psb(4)[:, jj * 128:(jj + 1) * 128],
                            probs[:, jj * 128:(jj + 1) * 128], ident[:]),
                            waits=[(ve, ev_ve)] if jj == 0 else [],
                            inc=pe if jj == 3 else None)
                    tp_pe = pe.n
                    tp_prev = [(pe, tp_pe)]
                    p.emit("vector", lambda e: e.tensor_copy(
                        probs[:, 512:1024], psb(4)[:, 0:512]),
                        waits=[(pe, tp_pe)], inc=ve)
                    pv_ve = ve.n
                    for jj in range(4):
                        j = n * 4 + jj
                        p.emit("vector", lambda e, j=j, jj=jj, f=f:
                               e.scalar_tensor_tensor(
                            x_res[:, j * D + f * 128:j * D + (f + 1) * 128],
                            probs[:, 512 + jj * 128:512 + (jj + 1) * 128],
                            av[:, j:j + 1],
                            x_res[:, j * D + f * 128:j * D + (f + 1) * 128],
                            ALU.mult, ALU.add),
                            waits=[(ve, pv_ve)] if jj == 0 else [], inc=ve)

        if n_sub >= 2:
            ffn_q(True)

        if n_sub >= 3:
            ln_stats(0, 1)
            p.emit("vector", lambda e: e.scalar_tensor_tensor(
                vslot(15), vslot(0), -1.0, vslot(1), ALU.mult, ALU.mult),
                waits=[(se, se.n)], chain=ve)
            apply_transpose(lambda j: x_res[:, j * D:(j + 1) * D], 1, 15, R1,
                            fold_cols=None, wait0=[(ve, ve.n), (se, se.n),
                                                   (pe, pe.n)])
            attn_sublayer(1, bqT, bkT, bvT, 1.0)
            wo2_t = load_slab(wB, boT, waits=[(pe, pe.n), (dcc, dcc.n)])
            tp_prev = []
            bank_guard2 = {}
            for c in range(4):
                for f in range(NF):
                    b = f % 4
                    for a in range(NF):
                        w = []
                        if a == 0:
                            w = [(dw, wo2_t), (ve, ve.n)]
                            if bank_guard2.get(b):
                                w.append(bank_guard2[b])
                        p.emit("tensor", lambda e, f=f, c=c, a=a, b=b: e.matmul(
                            pbank(b),
                            wB[:, a * 1024 + f * 128:a * 1024 + (f + 1) * 128],
                            R2[:, a * T + c * 512:a * T + (c + 1) * 512],
                            start=(a == 0), stop=(a == NF - 1)),
                            waits=w, inc=pe if a == NF - 1 else None)
                    wo_pe = pe.n
                    p.emit("vector", lambda e, b=b, f=f: e.tensor_scalar(
                        stg[:, 0:512], pbank(b), 1.0,
                        ln_cols[:, 2 * NF + f:2 * NF + f + 1], ALU.mult, ALU.add),
                        waits=[(pe, wo_pe)] + tp_prev, inc=ve)
                    bank_guard2[b] = (ve, ve.n)
                    ev_ve = ve.n
                    for jj in range(4):
                        p.emit("tensor", lambda e, jj=jj: e.transpose(
                            psb(4)[:, jj * 128:(jj + 1) * 128],
                            stg[:, jj * 128:(jj + 1) * 128], ident[:]),
                            waits=[(ve, ev_ve)] if jj == 0 else [],
                            inc=pe if jj == 3 else None)
                    tp_pe = pe.n
                    tp_prev = [(pe, tp_pe)]
                    p.emit("vector", lambda e, f=f, c=c: e.tensor_add(
                        flatap(x_res, c * 4 * D + f * 128,
                               [(NT * D, 128), (D, 4), (1, 128)]),
                        flatap(x_res, c * 4 * D + f * 128,
                               [(NT * D, 128), (D, 4), (1, 128)]),
                        psb(4)[:, 0:512].rearrange("p (a d) -> p a d", d=128)),
                        waits=[(pe, tp_pe), (ve, ve.n)], inc=ve)

        if n_sub >= 4:
            ffn_q(False)

        TAPS = []
        loc = locals()
        for name, fn in DEV_TAPS:
            ap_sb, shape, dtype = fn(loc)
            o = nc.dram_tensor(f"tap_{name}", shape, dtype, kind="ExternalOutput").ap()
            TAPS.append((o, ap_sb, dtype))

        p.emit("sync", lambda e: e.dma_start(
            xout.rearrange("(j p) d -> p j d", p=128),
            x_res[:].rearrange("p (j d) -> p j d", d=D)),
            waits=[(se, se.n), (ve, ve.n), (pe, pe.n), (po, po.n), (dw, dw.n),
                   (dcc, dcc.n), (dx, dx.n)],
            inc=od, inc_by=16)
        for o, ap_sb, dtype in TAPS:
            if dtype == F32:
                p.emit("gpsimd", lambda e, o=o, a=ap_sb: e.dma_start(o, a),
                       waits=[(od, 16)], inc=od, inc_by=16)
            else:
                p.emit("sync", lambda e, o=o, a=ap_sb: e.dma_start(o, a),
                       inc=od, inc_by=16)
        p.emit("sync", lambda e: None, waits=[(od, od.n)])

        p.finalize(nc)
    return nc


_CACHE = {}


def _grow(v):
    v = np.asarray(v, np.float32)
    r = np.zeros((2, 128), BF)
    r[0, 0:64] = v.astype(BF)
    r[1, 64:128] = v.astype(BF)
    return r


def _prep_inputs(inputs, h):
    idx, cos, sin, maskdup, ident, pswap = _host_aux(h)
    tp = lambda a: np.ascontiguousarray(np.asarray(a, np.float32).T)
    g2 = lambda v: np.tile(np.asarray(v, np.float32), 2).reshape(128, 1)
    d = {
        "wqT": tp(inputs["L0_attn_a_wq"]), "wkT": tp(inputs["L0_attn_a_wk"]),
        "wvT": tp(inputs["L0_attn_a_wv"]), "woT": tp(inputs["L0_attn_a_wo"]),
        "w1T": tp(inputs["L0_attn_a_w1"]), "w3T": tp(inputs["L0_attn_a_w3"]),
        "w2T": tp(inputs["L0_attn_a_w2"]),
        "fwT": tp(inputs["L0_ffn_weight"]),
        "bqT": tp(inputs["L0_attn_b_Wq"]), "bkT": tp(inputs["L0_attn_b_Wk"]),
        "bvT": tp(inputs["L0_attn_b_Wv"]), "boT": tp(inputs["L0_attn_b_Wo"]),
        "rcos": cos, "rsin": sin,
        "maskin": maskdup, "identin": ident, "pswapin": pswap,
        "gqc": g2(inputs["L0_attn_b_gq"]), "bqc": g2(inputs["L0_attn_b_bq"]),
        "gkc": g2(inputs["L0_attn_b_gk"]), "bkc": g2(inputs["L0_attn_b_bk"]),
        "boc": np.ascontiguousarray(
            np.asarray(inputs["L0_attn_b_bo"], np.float32).reshape(NF, 128).T),
        "growsq": _grow(inputs["L0_attn_b_gq"]),
        "growsk": _grow(inputs["L0_attn_b_gk"]),
        "nbrowq": _grow(-np.asarray(inputs["L0_attn_b_bq"], np.float64)),
        "nbrowk": _grow(-np.asarray(inputs["L0_attn_b_bk"], np.float64)),
        "nbdupq": np.tile(-np.asarray(inputs["L0_attn_b_bq"], np.float32),
                          2).reshape(1, 128).astype(BF),
        "nbdupk": np.tile(-np.asarray(inputs["L0_attn_b_bk"], np.float32),
                          2).reshape(1, 128).astype(BF),
        "n1c": np.ascontiguousarray(
            np.asarray(inputs["L0_attn_a_norm1_w"], np.float32).reshape(NF, 128).T),
        "n2c": np.ascontiguousarray(
            np.asarray(inputs["L0_attn_a_norm2_w"], np.float32).reshape(NF, 128).T),
    }
    return idx, d


def kernel(**inputs):
    x = np.asarray(inputs["L0_attn_a_x"], np.float32)
    if "prog" not in _CACHE:
        _CACHE["prog"] = build_program()
    nc = _CACHE["prog"]

    in_maps = []
    idxs = []
    for c in range(8):
        b, h = c // 2, c % 2
        idx, d = _prep_inputs(inputs, h)
        d["xin"] = np.ascontiguousarray(x[b, idx, :])
        in_maps.append(d)
        idxs.append(idx)

    res = run_bass_kernel_spmd(nc, in_maps, core_ids=list(range(8)))

    out = np.empty_like(x)
    for c in range(8):
        b = c // 2
        out[b, idxs[c], :] = np.asarray(res.results[c]["xout"], np.float32)
    extras = {}
    for c in range(8):
        for k, v in res.results[c].items():
            if k.startswith("tap_"):
                extras.setdefault(k, []).append(np.asarray(v))
    kernel.last_taps = extras
    return out


# version 04
# speedup vs baseline: 1.1791x; 1.1791x over previous
import contextlib
import numpy as np
import ml_dtypes
import concourse.bass as bass
import concourse.mybir as mybir
from concourse.bass_utils import run_bass_kernel_spmd

F32 = mybir.dt.float32
BF16 = mybir.dt.bfloat16
I32 = mybir.dt.int32
AF = mybir.ActivationFunctionType
ALU = mybir.AluOpType
AX = mybir.AxisListType
BF = ml_dtypes.bfloat16

D = 1024
T = 2048
H = 16
DK = 64
DFF = 4096
NT = 16
NF = 8
NP = 8
EPS_LN = 1e-5
EPS_RMS = 1e-5
EPS_Q = 1e-8
ROPE_THETA = 10000.0
BSHIFT = 40.0

N_SUBLAYERS = 4
STOP_STAGE = 99
ATT_LIMIT = (99, 99, 99, 99)
STOP_LI = None
S5_PARTS = 9
DEV_TAPS = []


def _zigzag_idx(h):
    r = np.arange(T)
    return (2 * (r // 128) + h) * 128 + (r % 128)


def _rope_tables(pos):
    k = np.arange(DK) // 2
    inv = ROPE_THETA ** (-2.0 * k.astype(np.float64) / DK)
    ang = pos[None, :].astype(np.float64) * inv[:, None]
    cos = np.cos(ang)
    sgn = np.where(np.arange(DK) % 2 == 0, -1.0, 1.0)
    sin = np.sin(ang) * sgn[:, None]
    return cos.astype(np.float32), sin.astype(np.float32)


def _host_aux(h):
    idx = _zigzag_idx(h)
    cos, sin = _rope_tables(idx)
    stair = (np.arange(128)[:, None] <= np.arange(128)[None, :]).astype(BF)
    ones = np.ones((128, 128), BF)
    zeros = np.zeros((128, 128), BF)
    slots = [stair, zeros] if h == 0 else [ones, stair]
    maskdup = np.concatenate([np.concatenate([s, s], axis=1) for s in slots], axis=1)
    ident = np.eye(128, dtype=BF)
    pswap = np.zeros((128, 128), BF)
    dd = np.arange(128)
    sw = np.where(dd % 2 == 0, dd + 1, dd - 1)
    pswap[sw, dd] = 1.0
    return idx, cos, sin, maskdup, ident, pswap


class Sem:
    def __init__(self, es, nc, name):
        self.h = es.enter_context(nc.semaphore(name))
        self.n = 0

    def bump(self, k=1):
        self.n += k
        return self.n


class Prog:
    ENGS = ("sync", "gpsimd", "tensor", "vector", "scalar")

    def __init__(self):
        self.rec = {e: [] for e in self.ENGS}

    def emit(self, eng, builder, waits=(), inc=None, inc_by=1, chain=None):
        if chain is not None:
            waits = list(waits) + [(chain, chain.n)]
            inc = chain
            inc_by = 1
        waits = tuple((s, v) for (s, v) in waits)

        def fn(engine):
            for s, v in waits:
                engine.wait_ge(s.h, v)
            inst = builder(engine)
            if inc is not None:
                if inst is None:
                    engine.sem_inc(inc.h, inc_by)
                else:
                    inst.then_inc(inc.h, inc_by)
        self.rec[eng].append(fn)
        if inc is not None:
            inc.bump(inc_by)

    def finalize(self, nc):
        with nc.Block() as block:
            for e in self.ENGS:
                recs = tuple(self.rec[e])

                def body(engine, _r=recs):
                    for f in _r:
                        f(engine)
                getattr(block, e)(body)


def build_program(n_sub=None):
    if n_sub is None:
        n_sub = N_SUBLAYERS
    stop = STOP_STAGE if STOP_LI is None else 99
    nc = bass.Bass("TRN2", target_bir_lowering=False)

    def inp(name, shape, dtype=F32):
        return nc.dram_tensor(name, shape, dtype, kind="ExternalInput").ap()

    xin = inp("xin", [T, D])
    wqT = inp("wqT", [D, D]); wkT = inp("wkT", [D, D]); wvT = inp("wvT", [D, D])
    woT = inp("woT", [D, D])
    w1T = inp("w1T", [D, DFF]); w3T = inp("w3T", [D, DFF]); w2T = inp("w2T", [DFF, D])
    fwT = inp("fwT", [D, D])
    bqT = inp("bqT", [D, D]); bkT = inp("bkT", [D, D]); bvT = inp("bvT", [D, D])
    boT = inp("boT", [D, D])
    rcos = inp("rcos", [DK, T]); rsin = inp("rsin", [DK, T])
    maskin = inp("maskin", [128, 512], BF16)
    identin = inp("identin", [128, 128], BF16)
    pswapin = inp("pswapin", [128, 128], BF16)
    gqc = inp("gqc", [128, 1]); bqc = inp("bqc", [128, 1])
    gkc = inp("gkc", [128, 1]); bkc = inp("bkc", [128, 1])
    growsq = inp("growsq", [2, 128], BF16)
    growsk = inp("growsk", [2, 128], BF16)
    nbrowq = inp("nbrowq", [2, 128], BF16)
    nbrowk = inp("nbrowk", [2, 128], BF16)
    nbdupq = inp("nbdupq", [1, 128], BF16)
    nbdupk = inp("nbdupk", [1, 128], BF16)
    boc = inp("boc", [128, NF])
    n1c = inp("n1c", [128, NF]); n2c = inp("n2c", [128, NF])

    xout = nc.dram_tensor("xout", [T, D], F32, kind="ExternalOutput").ap()

    KSZ = NP * 128 * T
    VOFF = KSZ
    MOFF = 2 * KSZ
    CCE = 2 * KSZ + 2048
    cc_in_h = [nc.dram_tensor(f"cc_in{i}", [CCE // 2048, 2048], BF16) for i in range(2)]
    cc_out_h = [nc.dram_tensor(f"cc_out{i}", [2 * CCE // 2048, 2048], BF16)
                for i in range(2)]
    wq_park = nc.dram_tensor("wq_park", [128, 8192], BF16)
    w1bf = nc.dram_tensor("w1bf", [D * DFF // 2048, 2048], BF16)
    w3bf = nc.dram_tensor("w3bf", [D * DFF // 2048, 2048], BF16)
    w2bf = nc.dram_tensor("w2bf", [D * DFF // 2048, 2048], BF16)

    es = contextlib.ExitStack()
    with es:
        def sbt(name, shape, dtype=BF16):
            return es.enter_context(nc.sbuf_tensor(name, shape, dtype))

        def psumt(name, shape, dtype=F32):
            return es.enter_context(nc.psum_tensor(name, shape, dtype))

        x_res = sbt("x_res", [128, NT * D], F32)
        R1 = sbt("R1", [128, 16384])
        R2 = sbt("R2", [128, 16384])
        wA = sbt("wA", [128, 8256])
        wB = sbt("wB", [128, 10304])
        ysc = sbt("ysc", [128, 4096])
        stg = sbt("stg", [128, 4096])
        probs = sbt("probs", [128, 3072])
        scrf = stg[:].bitcast(F32)
        iscr = stg[:].bitcast(I32)
        yscf = ysc[:].bitcast(F32)
        rrowf = wB[0:1, 8256:10304].bitcast(F32)
        mask = sbt("mask", [128, 512])
        ident = sbt("ident", [128, 128])
        pswap = sbt("pswap", [128, 128])
        e1 = sbt("e1", [1, 128])
        e2 = sbt("e2", [1, 128])
        bd2 = sbt("bd2", [128, 4])
        onesrow = sbt("onesrow", [33, 512])
        growq = sbt("growq", [33, 128])
        growk = sbt("growk", [33, 128])
        nbq = sbt("nbq", [33, 128])
        nbk = sbt("nbk", [33, 128])
        growq2 = sbt("growq2", [1, 128])
        growk2 = sbt("growk2", [1, 128])
        nbqf = sbt("nbqf", [1, 128])
        nbkf = sbt("nbkf", [1, 128])
        onesf32 = es.enter_context(nc.sbuf_tensor("onesf32", [1, 128], F32))
        vecs = es.enter_context(nc.sbuf_tensor("vecs", [128, 256], F32))
        colsf = es.enter_context(nc.sbuf_tensor("colsf", [128, 16], F32))
        Mrow = es.enter_context(nc.sbuf_tensor("Mrow", [2, 64], F32))
        biascol = es.enter_context(nc.sbuf_tensor("biascol", [128, 16], F32))
        ln_cols = es.enter_context(nc.sbuf_tensor("ln_cols", [128, 3 * NF], F32))

        def Mq(z):
            return Mrow[0:2, z * 8:(z + 1) * 8]

        def Mr1(a, b):
            return Mrow[0:1, a:b]

        ps2 = [psumt(f"ps2_{i}", [128, 1024]) for i in range(4)]

        def pbank(i):
            return ps2[i // 2][:, (i % 2) * 512:(i % 2 + 1) * 512]

        def psb(i):
            return ps2[i // 2][:].bitcast(BF16)[:, (i % 2) * 1024:(i % 2 + 1) * 1024]

        p = Prog()
        SS = {n: Sem(es, nc, n) for n in
              ("dx", "dw", "dcc", "pe", "ve", "se", "po", "od")}
        dx, dw, dcc = SS["dx"], SS["dw"], SS["dcc"]
        pe, ve, se, po, od = SS["pe"], SS["ve"], SS["se"], SS["po"], SS["od"]

        def flatap(h, off, dims):
            return bass.AP(h, off, [list(d) for d in dims])

        p.emit("sync", lambda e: e.dma_start(
            x_res[:].rearrange("p (j d) -> p j d", d=D),
            xin.rearrange("(j p) d -> p j d", p=128)), inc=dx, inc_by=16)
        for dst, srcap in ((mask[:], maskin), (ident[:], identin),
                           (pswap[:], pswapin),
                           (colsf[:, 0:1], gqc), (colsf[:, 1:2], bqc),
                           (colsf[:, 2:3], gkc), (colsf[:, 3:4], bkc),
                           (ln_cols[:, 0:NF], n1c), (ln_cols[:, NF:2 * NF], n2c),
                           (ln_cols[:, 2 * NF:3 * NF], boc),
                           (growq[0:1, :], growsq[0:1, :]),
                           (growq[32:33, :], growsq[1:2, :]),
                           (growk[0:1, :], growsk[0:1, :]),
                           (growk[32:33, :], growsk[1:2, :]),
                           (nbq[0:1, :], nbrowq[0:1, :]),
                           (nbq[32:33, :], nbrowq[1:2, :]),
                           (nbk[0:1, :], nbrowk[0:1, :]),
                           (nbk[32:33, :], nbrowk[1:2, :]),
                           (growq2[0:1, :], growsq[1:2, :]),
                           (growk2[0:1, :], growsk[1:2, :]),
                           (nbqf[0:1, :], nbdupq),
                           (nbkf[0:1, :], nbdupk)):
            p.emit("sync", lambda e, d=dst, s=srcap: e.dma_start(d, s),
                   inc=dx, inc_by=16)
        DX0 = dx.n

        for ap_, val in ((e1[0:1, 0:64], 1.0), (e1[0:1, 64:128], 0.0),
                         (e2[0:1, 0:64], 0.0), (e2[0:1, 64:128], 1.0),
                         (bd2[0:64, 0:1], 1.0 / 64), (bd2[64:128, 0:1], 0.0),
                         (bd2[0:64, 1:2], 0.0), (bd2[64:128, 1:2], 1.0 / 64),
                         (bd2[0:64, 2:3], 1.0), (bd2[64:128, 2:3], 0.0),
                         (bd2[0:64, 3:4], 0.0), (bd2[64:128, 3:4], 1.0),
                         (onesrow[0:1, :], 1.0), (onesrow[32:33, :], 1.0),
                         (onesf32[:], 1.0)):
            p.emit("vector", lambda e, a=ap_, v=val: e.memset(a, v), chain=ve)
        for wbuf in (wA, wB):
            p.emit("vector", lambda e, wbuf=wbuf: e.memset(
                flatap(wbuf, 4096 + 64, [(wbuf.shape[1], 128), (130, 32), (1, 1)]),
                1.0), chain=ve)
            p.emit("vector", lambda e, wbuf=wbuf: e.memset(
                flatap(wbuf, 4096 + 129, [(wbuf.shape[1], 128), (130, 32), (1, 1)]),
                1.0), chain=ve)
        VE_CONST = ve.n

        def vslot(s_):
            return vecs[:, s_ * 16:(s_ + 1) * 16]

        def ln_stats(slot_m, slot_r, extra_rms_slot=None):
            m = vslot(slot_m); r = vslot(slot_r); nb = vslot(14)
            p.emit("vector", lambda e: e.reduce_sum(
                m, x_res[:].rearrange("p (j d) -> p j d", d=D), axis=AX.X),
                waits=[(dx, DX0)], chain=ve)
            p.emit("vector", lambda e: e.tensor_scalar_mul(m, m, 1.0 / D), chain=ve)
            for j in range(NT):
                half = (j % 2) * 1024
                wse = [(ve, ve.n)] if j >= 2 else [(dx, DX0), (se, se.n), (ve, ve.n)]
                p.emit("scalar", lambda e, j=j, half=half: e.activation(
                    scrf[:, half:half + 1024], x_res[:, j * D:(j + 1) * D],
                    AF.Square), waits=wse, inc=se)
                p.emit("vector", lambda e, j=j, half=half: e.reduce_sum(
                    r[:, j:j + 1],
                    scrf[:, half:half + 1024].rearrange("p (a d) -> p a d", a=1),
                    axis=AX.X), waits=[(se, se.n)], chain=ve)
            p.emit("vector", lambda e: e.tensor_scalar_mul(r, r, 1.0 / D), chain=ve)
            p.emit("vector", lambda e: e.tensor_mul(nb, m, m), chain=ve)
            p.emit("vector", lambda e: e.tensor_sub(r, r, nb), chain=ve)
            p.emit("vector", lambda e: e.tensor_scalar_add(r, r, EPS_LN), chain=ve)
            p.emit("scalar", lambda e: e.activation(r, r, AF.Ln),
                   waits=[(ve, ve.n)], chain=se)
            p.emit("scalar", lambda e: e.activation(r, r, AF.Exp, scale=-0.5),
                   chain=se)
            if extra_rms_slot is not None:
                s1 = vslot(extra_rms_slot)
                p.emit("vector", lambda e: e.tensor_mul(s1, r, r),
                       waits=[(se, se.n)], chain=ve)
                p.emit("vector", lambda e: e.tensor_scalar(
                    s1, s1, -EPS_LN, 1.0 + EPS_RMS, ALU.mult, ALU.add), chain=ve)
                p.emit("scalar", lambda e: e.activation(s1, s1, AF.Ln),
                       waits=[(ve, ve.n)], chain=se)
                p.emit("scalar", lambda e: e.activation(s1, s1, AF.Exp, scale=-0.5),
                       chain=se)
                p.emit("vector", lambda e: e.tensor_mul(s1, s1, r),
                       waits=[(se, se.n)], chain=ve)
            return se.n, ve.n

        def apply_transpose(src_fn, slot_s, slot_nb, dst, fold_cols=None,
                            jgs=(0, 1, 2, 3), wait0=()):
            s_ = vslot(slot_s); nb = vslot(slot_nb)
            prev_evac = []
            pe_tp = pe.n
            jgs = list(jgs)
            for gi, jg in enumerate(jgs):
                for jj in range(4):
                    j = jg * 4 + jj
                    w = list(wait0) if (jj == 0 and gi == 0) else []
                    if jj == 0 and gi > 0:
                        w.append((pe, pe_tp))
                    p.emit("vector", lambda e, j=j, jj=jj: e.tensor_scalar(
                        stg[:, jj * 1024:(jj + 1) * 1024], src_fn(j),
                        s_[:, j:j + 1], nb[:, j:j + 1], ALU.mult, ALU.add),
                        waits=w, inc=ve)
                ve_app = ve.n
                for f in range(NF):
                    for jj in range(4):
                        w = []
                        if f == 0 and jj == 0:
                            w = [(ve, ve_app)] + prev_evac
                        p.emit("tensor", lambda e, f=f, jj=jj: e.transpose(
                            psb(f)[:, jj * 128:(jj + 1) * 128],
                            stg[:, jj * 1024 + f * 128:jj * 1024 + (f + 1) * 128],
                            ident[:]),
                            waits=w, inc=pe if (f == NF - 1 and jj == 3) else None)
                pe_tp = pe.n
                prev_evac = []
                for f in range(NF):
                    if fold_cols is not None:
                        p.emit("vector", lambda e, f=f, jg=jg: e.tensor_scalar_mul(
                            dst[:, f * T + jg * 512:f * T + (jg + 1) * 512],
                            psb(f)[:, 0:512], fold_cols[:, f:f + 1]),
                            waits=[(pe, pe_tp)], inc=ve)
                    else:
                        p.emit("vector", lambda e, f=f, jg=jg: e.tensor_copy(
                            dst[:, f * T + jg * 512:f * T + (jg + 1) * 512],
                            psb(f)[:, 0:512]),
                            waits=[(pe, pe_tp)], inc=ve)
                    prev_evac = [(ve, ve.n)]
            return se.n, ve.n, pe.n

        def load_slab(dst, srcT, waits=()):
            p.emit("gpsimd", lambda e, dst=dst, srcT=srcT: e.dma_start(
                dst[:, 0:8192].rearrange("p (a o) -> p a o", o=D),
                srcT.rearrange("(a p) o -> p a o", p=128)),
                waits=waits, inc=dw, inc_by=16)
            return dw.n

        def proj_fm(slab, w_tgt, rhs_tgt, evac_fn, w_sem=None):
            if w_sem is None:
                w_sem = dw
            guards = [None, None, None, None]
            for f in range(NF):
                for n in range(4):
                    b = (f * 4 + n) % 4
                    for a in range(NF):
                        w = []
                        if a == 0:
                            w = [(w_sem, w_tgt), (ve, rhs_tgt)]
                            if guards[b] is not None:
                                w.append(guards[b])
                        p.emit("tensor", lambda e, f=f, n=n, a=a, b=b, slab=slab:
                               e.matmul(
                            pbank(b),
                            slab[:, a * 1024 + f * 128:a * 1024 + (f + 1) * 128],
                            R1[:, a * T + n * 512:a * T + (n + 1) * 512],
                            start=(a == 0), stop=(a == NF - 1)),
                            waits=w, inc=pe if a == NF - 1 else None)
                    guards[b] = evac_fn(f, n, b, pe.n)
            return pe.n

        ln_stats(0, 1, extra_rms_slot=2)
        p.emit("vector", lambda e: e.scalar_tensor_tensor(
            vslot(15), vslot(0), -1.0, vslot(2), ALU.mult, ALU.mult), chain=ve)
        p.emit("vector", lambda e: e.scalar_tensor_tensor(
            vslot(3), vslot(0), 1.0, vslot(1), ALU.mult, ALU.mult), chain=ve)
        p.emit("vector", lambda e: e.tensor_scalar_mul(
            vslot(13), vslot(3), -1.0), chain=ve)
        apply_transpose(lambda j: x_res[:, j * D:(j + 1) * D], 2, 15, R1,
                        fold_cols=ln_cols[:, 0:NF],
                        wait0=[(ve, ve.n), (se, se.n)])

        def attn_sublayer(li, wq_d, wk_d, wv_d, att_scale):
            cc_i, cc_o = cc_in_h[li], cc_out_h[li]
            stop = STOP_STAGE if (STOP_LI is None or li == STOP_LI) else 99
            lim = ATT_LIMIT if (STOP_LI is None or li == STOP_LI) \
                else (99, 99, 99, 99)
            if stop < 2:
                return 0, 0, 0
            if li == 0:
                for dstv, tbl in ((scrf, rcos), (yscf, rsin)):
                    for pp in range(2):
                        p.emit("sync", lambda e, dstv=dstv, tbl=tbl, pp=pp:
                               e.dma_start(dstv[pp * 64:(pp + 1) * 64, 0:2048], tbl),
                               waits=[(ve, ve.n), (pe, pe.n)] if pp == 0 else [],
                               inc=dx, inc_by=16)
            RT_DX = dx.n
            HFM_VE = ve.n

            wq_t = load_slab(wA, wq_d, waits=[(pe, pe.n)])
            wk_t = load_slab(wB, wk_d)

            def evac_q(f, n, b, pet):
                p.emit("vector", lambda e, f=f, n=n, b=b: e.tensor_copy(
                    R2[:, f * T + n * 512:f * T + (n + 1) * 512], pbank(b)),
                    waits=[(pe, pet)], inc=ve)
                return (ve, ve.n)
            proj_fm(wA, wq_t, HFM_VE, evac_q)
            wv_t = load_slab(wA, wv_d, waits=[(pe, pe.n)])

            first = {}

            def normfold(zi, f, wse):
                for n2 in range(2):
                    p.emit("tensor", lambda e, n2=n2: e.matmul(
                        ps2[3][0:2, 0:512], bd2[:, 2:4],
                        probs[:, 2048 + n2 * 512:2048 + (n2 + 1) * 512],
                        start=True, stop=True),
                        waits=wse + [(ve, ve.n)], inc=pe)
                    nm_pe = pe.n
                    p.emit("vector", lambda e: e.reduce_max(
                        Mq(6)[:, 0:1],
                        ps2[3][0:2, 0:512].rearrange("p (a n) -> p a n", a=1),
                        axis=AX.X), waits=[(pe, nm_pe)], chain=ve)
                    key = (zi, f)
                    if key not in first:
                        first[key] = True
                        p.emit("vector", lambda e, zi=zi, f=f: e.tensor_copy(
                            Mq(zi)[:, f:f + 1], Mq(6)[:, 0:1]), chain=ve)
                    else:
                        p.emit("vector", lambda e, zi=zi, f=f: e.tensor_tensor(
                            Mq(zi)[:, f:f + 1], Mq(zi)[:, f:f + 1], Mq(6)[:, 0:1],
                            ALU.max), chain=ve)
                    wse = []

            st = {"ktile_free": 0, "kship": 0, "sq_pe": 0}

            def qkln(buf_fn, grow, grow2, nbf):
                for ch2 in range(2):
                    z2 = ch2 * 1024
                    p.emit("scalar", lambda e, z2=z2: e.activation(
                        probs[:, 2048:3072], buf_fn(z2), AF.Square),
                        waits=[(ve, ve.n), (se, se.n), (pe, st["sq_pe"])],
                        inc=se)
                    sq_se = se.n
                    for ch in range(2):
                        for hd in range(2):
                            rp = hd * 32
                            p.emit("tensor", lambda e, z2=z2, ch=ch, hd=hd,
                                   rp=rp: e.matmul(
                                ps2[2][rp:rp + 1, ch * 512:(ch + 1) * 512],
                                bd2[:, hd:hd + 1],
                                buf_fn(z2)[:, ch * 512:(ch + 1) * 512],
                                start=True, stop=True),
                                waits=[(ve, ve.n), (se, sq_se)]
                                if (ch == 0 and hd == 0) else [], inc=None)
                            p.emit("tensor", lambda e, ch=ch, hd=hd, rp=rp:
                                   e.matmul(
                                ps2[3][rp:rp + 1, ch * 512:(ch + 1) * 512],
                                bd2[:, hd:hd + 1],
                                probs[:, 2048 + ch * 512:2048 + (ch + 1) * 512],
                                start=True, stop=True),
                                inc=pe if (ch == 1 and hd == 1) else None)
                    mm_pe = pe.n
                    p.emit("vector", lambda e: e.tensor_copy(
                        yscf[0:1, 0:1024], ps2[2][0:1, 0:1024]),
                        waits=[(pe, mm_pe)], chain=ve)
                    p.emit("vector", lambda e: e.tensor_copy(
                        scrf[0:1, 0:1024], ps2[2][32:33, 0:1024]), chain=ve)
                    p.emit("vector", lambda e: e.tensor_copy(
                        yscf[32:33, 0:1024], ps2[3][0:1, 0:1024]), chain=ve)
                    p.emit("vector", lambda e: e.tensor_copy(
                        scrf[32:33, 0:1024], ps2[3][32:33, 0:1024]), chain=ve)
                    for mrow, srow in ((yscf, yscf), (scrf, scrf)):
                        p.emit("vector", lambda e, mrow=mrow, srow=srow:
                               e.tensor_mul(
                            srow[32:33, 1024:2048], mrow[0:1, 0:1024],
                            mrow[0:1, 0:1024]), chain=ve)
                        p.emit("vector", lambda e, srow=srow: e.tensor_sub(
                            srow[32:33, 0:1024], srow[32:33, 0:1024],
                            srow[32:33, 1024:2048]), chain=ve)
                        p.emit("vector", lambda e, srow=srow:
                               e.tensor_scalar_add(
                            srow[32:33, 0:1024], srow[32:33, 0:1024],
                            EPS_LN), chain=ve)
                    for srow in (yscf, scrf):
                        p.emit("scalar", lambda e, srow=srow: e.activation(
                            srow[32:33, 0:1024], srow[32:33, 0:1024], AF.Ln),
                            waits=[(ve, ve.n)] if srow is yscf else [],
                            chain=se)
                        p.emit("scalar", lambda e, srow=srow: e.activation(
                            srow[0:1, 1024:2048], srow[32:33, 0:1024],
                            AF.Exp, scale=-0.5), chain=se)
                    for mrow in (yscf, scrf):
                        p.emit("vector", lambda e, mrow=mrow: e.tensor_mul(
                            mrow[32:33, 1024:2048], mrow[0:1, 0:1024],
                            mrow[0:1, 1024:2048]),
                            waits=[(se, se.n)] if mrow is yscf else [],
                            chain=ve)
                    p.emit("vector", lambda e: e.tensor_copy(
                        stg[0:1, 0:1024], yscf[0:1, 1024:2048]), chain=ve)
                    p.emit("vector", lambda e: e.tensor_copy(
                        stg[0:1, 1024:2048], scrf[0:1, 1024:2048]), chain=ve)
                    p.emit("vector", lambda e: e.tensor_copy(
                        stg[0:1, 2048:3072], yscf[32:33, 1024:2048]), chain=ve)
                    p.emit("vector", lambda e: e.tensor_copy(
                        stg[0:1, 3072:4096], scrf[32:33, 1024:2048]), chain=ve)
                    ROWS_VE = ve.n
                    for ch in range(2):
                        p.emit("tensor", lambda e, ch=ch, grow=grow: e.matmul(
                            ps2[2][:, ch * 512:(ch + 1) * 512],
                            grow[0:1, :],
                            stg[0:1, ch * 512:(ch + 1) * 512],
                            start=True, stop=False),
                            waits=[(ve, ROWS_VE)] if ch == 0 else [],
                            inc=None)
                        p.emit("tensor", lambda e, ch=ch, grow2=grow2: e.matmul(
                            ps2[2][:, ch * 512:(ch + 1) * 512],
                            grow2[0:1, :],
                            stg[0:1, 1024 + ch * 512:1024 + (ch + 1) * 512],
                            start=False, stop=True), inc=None)
                        p.emit("tensor", lambda e, ch=ch, grow=grow: e.matmul(
                            ps2[3][:, ch * 512:(ch + 1) * 512],
                            grow[0:1, :],
                            stg[0:1, 2048 + ch * 512:2048 + (ch + 1) * 512],
                            start=True, stop=False), inc=None)
                        p.emit("tensor", lambda e, ch=ch, grow2=grow2: e.matmul(
                            ps2[3][:, ch * 512:(ch + 1) * 512],
                            grow2[0:1, :],
                            stg[0:1, 3072 + ch * 512:3072 + (ch + 1) * 512],
                            start=False, stop=False), inc=None)
                        p.emit("tensor", lambda e, ch=ch, nbf=nbf: e.matmul(
                            ps2[3][:, ch * 512:(ch + 1) * 512],
                            nbf[0:1, :], onesrow[0:1, 0:512],
                            start=False, stop=True),
                            inc=pe if ch == 1 else None)
                    bc_pe = pe.n
                    p.emit("vector", lambda e, z2=z2: e.tensor_mul(
                        probs[:, 2048:3072], buf_fn(z2), ps2[2][:, 0:1024]),
                        waits=[(pe, bc_pe), (ve, ve.n), (se, se.n)], inc=ve)
                    p.emit("vector", lambda e, z2=z2: e.tensor_sub(
                        buf_fn(z2), probs[:, 2048:3072], ps2[3][:, 0:1024]),
                        inc=ve)
                    st["sq_pe"] = pe.n

            def ktransform(f):
                if li == 1:
                    qkln(lambda z: probs[:, z:z + 1024], growk, growk2, nbkf)
                for h2 in range(2):
                    z = h2 * 1024
                    if li == 0:
                        for n2 in range(2):
                            p.emit("tensor", lambda e, z=z, n2=n2: e.matmul(
                                ps2[2][:, n2 * 512:(n2 + 1) * 512], pswap[:],
                                probs[:, z + n2 * 512:z + (n2 + 1) * 512],
                                start=True, stop=True),
                                waits=[(ve, ve.n)], inc=pe if n2 == 1 else None)
                        swp = pe.n
                        p.emit("vector", lambda e, z=z: e.tensor_mul(
                            probs[:, 2048:3072], probs[:, z:z + 1024],
                            scrf[:, 0:1024] if z == 0 else scrf[:, 1024:2048]),
                            waits=[(dx, RT_DX), (pe, st["sq_pe"])], inc=ve)
                        p.emit("vector", lambda e, z=z: e.tensor_mul(
                            probs[:, z:z + 1024],
                            ps2[2][:].bitcast(F32)[:, 0:1024],
                            yscf[:, 0:1024] if z == 0 else yscf[:, 1024:2048]),
                            waits=[(pe, swp)], inc=ve)
                        p.emit("vector", lambda e, z=z: e.tensor_add(
                            probs[:, z:z + 1024], probs[:, 2048:3072],
                            probs[:, z:z + 1024]), inc=ve)
                    kr_ve = ve.n
                    p.emit("scalar", lambda e, z=z: e.activation(
                        probs[:, 2048:3072], probs[:, z:z + 1024], AF.Square),
                        waits=[(ve, kr_ve), (se, se.n), (pe, st["sq_pe"])], inc=se)
                    normfold(1, f, [(se, se.n)])
                    st["sq_pe"] = pe.n
                    p.emit("sync", lambda e, f=f, z=z, h2=h2: e.dma_start(
                        flatap(cc_i, f * 128 * T + h2 * 1024,
                               [(T, 128), (1, 1024)]),
                        probs[:, z:z + 1024]),
                        waits=[(ve, kr_ve)], inc=dcc, inc_by=16)
                st["kship"] = dcc.n
                st["ktile_free"] = ve.n

            def evac_k(f, n, b, pet):
                w = [(pe, pet)]
                if n == 0:
                    w += [(ve, st["ktile_free"]), (dcc, st["kship"])]
                p.emit("vector", lambda e, n=n, b=b: e.tensor_copy(
                    probs[:, n * 512:(n + 1) * 512], pbank(b)), waits=w, inc=ve)
                if n == 3:
                    ktransform(f)
                return (ve, ve.n)
            proj_fm(wB, wk_t, HFM_VE, evac_k)

            for f in range(NF):
                if li == 1:
                    qkln(lambda z, f=f: R2[:, f * T + z:f * T + z + 1024],
                         growq, growq2, nbqf)
                for h2 in range(2):
                    z = h2 * 1024
                    base = f * T + h2 * 1024
                    if li == 0:
                        for n2 in range(2):
                            p.emit("tensor", lambda e, base=base, n2=n2: e.matmul(
                                ps2[2][:, n2 * 512:(n2 + 1) * 512], pswap[:],
                                R2[:, base + n2 * 512:base + (n2 + 1) * 512],
                                start=True, stop=True),
                                waits=[(ve, ve.n)], inc=pe if n2 == 1 else None)
                        swp = pe.n
                        p.emit("vector", lambda e, base=base, z=z: e.tensor_mul(
                            probs[:, 0:1024], R2[:, base:base + 1024],
                            scrf[:, 0:1024] if z == 0 else scrf[:, 1024:2048]),
                            waits=[(dx, RT_DX), (ve, st["ktile_free"]),
                                   (dcc, st["kship"]), (pe, st["sq_pe"])], inc=ve)
                        p.emit("vector", lambda e, z=z: e.tensor_mul(
                            probs[:, 1024:2048],
                            ps2[2][:].bitcast(F32)[:, 0:1024],
                            yscf[:, 0:1024] if z == 0 else yscf[:, 1024:2048]),
                            waits=[(pe, swp)], inc=ve)
                        p.emit("vector", lambda e, base=base: e.tensor_add(
                            R2[:, base:base + 1024], probs[:, 0:1024],
                            probs[:, 1024:2048]), inc=ve)
                    p.emit("scalar", lambda e, base=base: e.activation(
                        probs[:, 2048:3072], R2[:, base:base + 1024], AF.Square),
                        waits=[(ve, ve.n), (se, se.n), (pe, st["sq_pe"])], inc=se)
                    normfold(0, f, [(se, se.n)])
                    st["sq_pe"] = pe.n
            Q_SE, Q_VE = se.n, ve.n

            st["vship"] = dcc.n

            def vtransform(f):
                vev = ve.n
                for g2 in range(2):
                    for jj in range(8):
                        w = [(ve, vev)] if jj == 0 else []
                        if jj == 0 and g2 == 0:
                            w.append((pe, st["sq_pe"]))
                        p.emit("tensor", lambda e, g2=g2, jj=jj: e.transpose(
                            ps2[2][:].bitcast(BF16)[:, jj * 128:(jj + 1) * 128],
                            probs[:, g2 * 1024 + jj * 128:g2 * 1024 + (jj + 1) * 128],
                            ident[:]), waits=w, inc=pe if jj == 7 else None)
                    tp_pe = pe.n
                    p.emit("vector", lambda e: e.tensor_copy(
                        probs[:, 2048:3072], ps2[2][:].bitcast(BF16)[:, 0:1024]),
                        waits=[(pe, tp_pe), (dcc, st["vship"])], inc=ve)
                    evv = ve.n
                    p.emit("sync", lambda e, f=f, g2=g2: e.dma_start(
                        flatap(cc_i, VOFF + f * T * 128 + g2 * 1024 * 128,
                               [(128, 128), (16384, 8), (1, 128)]),
                        probs[:, 2048:3072].rearrange("p (a d) -> p a d", d=128)),
                        waits=[(ve, evv)], inc=dcc, inc_by=16)
                    st["vship"] = dcc.n
                    vev = ve.n
                st["sq_pe"] = pe.n

            def evac_v(f, n, b, pet):
                w = [(pe, pet)]
                if n == 0:
                    w.append((dcc, st["vship"]))
                if f == 0 and n == 0:
                    w += [(ve, Q_VE), (se, Q_SE)]
                p.emit("vector", lambda e, n=n, b=b: e.tensor_copy(
                    probs[:, n * 512:(n + 1) * 512], pbank(b)), waits=w, inc=ve)
                if n == 3:
                    vtransform(f)
                return (ve, ve.n)
            proj_fm(wA, wv_t, HFM_VE, evac_v)

            p.emit("sync", lambda e: e.dma_start(
                flatap(cc_i, MOFF, [(16, 2), (1, 16)]), Mq(1).bitcast(BF16)),
                waits=[(ve, ve.n)], inc=dcc, inc_by=16)
            CC_W = dcc.n
            if stop < 3:
                return 0, 0, 0

            p.emit("gpsimd", lambda e: e.collective_compute(
                "AllGather", ALU.bypass,
                replica_groups=[[0, 1], [2, 3], [4, 5], [6, 7]],
                ins=[cc_i.ap().opt()], outs=[cc_o.ap().opt()]),
                waits=[(dcc, CC_W)], inc=dcc, inc_by=1)
            CC_DONE = dcc.n

            for r in range(2):
                p.emit("sync", lambda e, r=r: e.dma_start(
                    Mq(2 + r).bitcast(BF16),
                    flatap(cc_o, r * CCE + MOFF, [(16, 2), (1, 16)])),
                    waits=[(dcc, CC_DONE)] if r == 0 else [], inc=dx, inc_by=16)
            KM_DX = dx.n
            p.emit("vector", lambda e: e.tensor_tensor(Mq(4), Mq(2), Mq(3), ALU.max),
                   waits=[(dx, KM_DX), (ve, ve.n)], chain=ve)
            p.emit("vector", lambda e: e.tensor_mul(Mq(5), Mq(0), Mq(4)), chain=ve)
            p.emit("sync", lambda e: e.dma_start(Mr1(48, 64), Mq(5)),
                   waits=[(ve, ve.n)], inc=dx, inc_by=16)
            p.emit("vector", lambda e: e.tensor_tensor(
                Mr1(32, 40), Mr1(48, 56), Mr1(56, 64), ALU.max),
                waits=[(dx, dx.n)], chain=ve)
            p.emit("scalar", lambda e: e.activation(Mr1(32, 40), Mr1(32, 40), AF.Ln),
                   waits=[(ve, ve.n)], chain=se)
            p.emit("scalar", lambda e: e.activation(
                Mr1(32, 40), Mr1(32, 40), AF.Exp, scale=0.5), chain=se)
            p.emit("vector", lambda e: e.tensor_scalar(
                Mr1(32, 40), Mr1(32, 40), -att_scale, BSHIFT, ALU.mult, ALU.add),
                waits=[(se, se.n)], chain=ve)
            p.emit("vector", lambda e: e.tensor_copy(probs[0:1, 0:8], Mr1(32, 40)),
                   chain=ve)
            BR_VE = ve.n
            p.emit("tensor", lambda e: e.matmul(
                pbank(0)[:, 0:8], onesrow[0:1, 0:128], probs[0:1, 0:8],
                start=True, stop=True), waits=[(ve, BR_VE)], inc=pe)
            for wbuf in (wA, wB):
                p.emit("vector", lambda e, wbuf=wbuf: e.memset(
                    flatap(wbuf, 4096 + 64, [(wbuf.shape[1], 128), (130, 32), (1, 1)]),
                    1.0), waits=[(pe, pe.n), (dcc, CC_W)], chain=ve)
                p.emit("vector", lambda e, wbuf=wbuf: e.memset(
                    flatap(wbuf, 4096 + 129, [(wbuf.shape[1], 128), (130, 32), (1, 1)]),
                    1.0), chain=ve)
            p.emit("vector", lambda e: e.tensor_copy(
                biascol[:, 0:8], pbank(0)[:, 0:8]), waits=[(pe, pe.n)], inc=ve)
            BIAS_VE = ve.n
            if stop < 4:
                return 0, 0, 0

            exp_t = {}
            pv_t = {}
            pair_done = {}
            prev_chunk_gate = [(ve, BIAS_VE)]
            LP, LC, LT, LE = lim
            for pr in range(min(NP, LP)):
                wbuf = wA if pr % 2 == 0 else wB
                W0 = wbuf.shape[1]
                w0 = [(dcc, CC_DONE)]
                if pr >= 2:
                    w0.append((pe, pair_done[pr - 2]))
                for r in range(2):
                    p.emit("sync", lambda e, r=r, pr=pr, wbuf=wbuf: e.dma_start(
                        wbuf[:, r * 2048:(r + 1) * 2048],
                        flatap(cc_o, r * CCE + pr * 128 * T,
                               [(T, 128), (1, 2048)])),
                        waits=w0 if r == 0 else [], inc=dcc, inc_by=16)
                    for s_ in range(2):
                        p.emit("sync", lambda e, r=r, s_=s_, pr=pr, wbuf=wbuf,
                               W0=W0: e.dma_start(
                            flatap(wbuf, 4096 + r * 130 + s_ * 65,
                                   [(W0, 128), (260, 16), (1, 64)]),
                            flatap(cc_o, r * CCE + VOFF + pr * T * 128 + s_ * 64,
                                   [(128, 128), (16384, 16), (1, 64)])),
                            inc=dcc, inc_by=16)
                load_t = dcc.n
                for c in range(min(4, LC)):
                    nkt = 8 * c + 8
                    for t in range(min(nkt, LT)):
                        sl = t % 3
                        jmin = max(4 * c, -(-(t - 1) // 2))
                        qoff = (jmin - 4 * c) * 128
                        qbase = pr * T + c * 512
                        tcol = (t % 2) * 2048 + (t // 2) * 128
                        w = [(dcc, load_t)]
                        if (pr, c, t - 3) in exp_t:
                            w.append((se, exp_t[(pr, c, t - 3)]))
                        if t == 0:
                            w += prev_chunk_gate
                        p.emit("tensor", lambda e, sl=sl, qoff=qoff, qbase=qbase,
                               tcol=tcol, wbuf=wbuf: e.matmul(
                            ps2[sl][:, qoff:512],
                            wbuf[0:64, tcol:tcol + 128],
                            R2[0:64, qbase + qoff:qbase + 512],
                            start=True, stop=True, tile_position=(0, 0)),
                            waits=w)
                        p.emit("tensor", lambda e, sl=sl, qoff=qoff, qbase=qbase,
                               tcol=tcol, wbuf=wbuf: e.matmul(
                            ps2[sl][:, 512 + qoff:1024],
                            wbuf[64:128, tcol:tcol + 128],
                            R2[64:128, qbase + qoff:qbase + 512],
                            start=True, stop=True, tile_position=(64, 0)),
                            inc=pe)
                        qk_pe = pe.n
                        wexp = [(pe, qk_pe)]
                        if (pr, c, t - 3) in pv_t:
                            wexp.append((pe, pv_t[(pr, c, t - 3)]))
                        N_t = 512 - qoff
                        p.emit("scalar", lambda e, sl=sl, qoff=qoff, pr=pr,
                               N_t=N_t: e.activation(
                            flatap(probs, sl * 1024 + qoff,
                                   [(3072, 128), (512, 2), (1, N_t)]),
                            flatap(ps2[sl], qoff, [(1024, 128), (512, 2), (1, N_t)]),
                            AF.Exp, bias=biascol[:, pr:pr + 1], scale=att_scale),
                            waits=wexp, inc=se)
                        exp_t[(pr, c, t)] = se.n
                        gate = (se, se.n)
                        jj = t // 2
                        if 4 * c <= jj < 4 * c + 4:
                            sk = t % 2
                            p.emit("vector", lambda e, sl=sl, qoff=qoff, sk=sk:
                                   e.tensor_mul(
                                flatap(probs, sl * 1024 + qoff,
                                       [(3072, 128), (512, 2), (1, 128)]),
                                flatap(probs, sl * 1024 + qoff,
                                       [(3072, 128), (512, 2), (1, 128)]),
                                flatap(mask, sk * 256,
                                       [(512, 128), (128, 2), (1, 128)])),
                                waits=[(se, se.n)], inc=ve)
                            gate = (ve, ve.n)
                        for hd in range(2):
                            p.emit("tensor", lambda e, qoff=qoff, sl=sl, hd=hd,
                                   t=t, wbuf=wbuf, nkt=nkt: e.matmul(
                                ps2[3][0:65, hd * 512 + qoff:(hd + 1) * 512],
                                wbuf[:, 4096 + t * 130 + hd * 65:
                                     4096 + t * 130 + hd * 65 + 65],
                                probs[:, sl * 1024 + hd * 512 + qoff:
                                      sl * 1024 + (hd + 1) * 512],
                                start=(t == 0), stop=(t == nkt - 1)),
                                waits=[gate] if hd == 0 else [],
                                inc=pe if hd == 1 else None)
                        pv_t[(pr, c, t)] = pe.n
                    if LE == 1:
                        p.emit("vector", lambda e: e.tensor_copy(
                            R2[:].bitcast(F32)[:, 0:512], ps2[3][:, 0:512]),
                            waits=[(pe, pv_t[(pr, c, nkt - 1)]),
                                   (se, exp_t[(pr, c, nkt - 1)]), (ve, ve.n)],
                            chain=ve)
                        p.emit("vector", lambda e: e.tensor_copy(
                            R2[:].bitcast(F32)[:, 512:1024], ps2[3][:, 512:1024]),
                            chain=ve)
                        prev_chunk_gate = [(ve, ve.n)]
                        continue
                    if LE == 0 or LT < nkt:
                        prev_chunk_gate = [(ve, ve.n)]
                        continue
                    PV_GATE = exp_t[(pr, c, nkt - 1)]
                    PV_END = pv_t[(pr, c, nkt - 1)]
                    p.emit("vector", lambda e: e.reciprocal(
                        rrowf[0:1, 0:512], ps2[3][64:65, 0:512]),
                        waits=[(pe, PV_END), (ve, ve.n)], chain=ve)
                    p.emit("vector", lambda e: e.reciprocal(
                        rrowf[0:1, 512:1024], ps2[3][64:65, 512:1024]), chain=ve)
                    p.emit("vector", lambda e: e.tensor_copy(
                        probs[0:1, 0:1024], rrowf[0:1, 0:1024]), chain=ve)
                    RB_VE = ve.n
                    if LE == 2:
                        prev_chunk_gate = [(ve, ve.n)]
                        continue
                    p.emit("tensor", lambda e: e.matmul(
                        ps2[0][:, 0:512], e1[0:1, :], probs[0:1, 0:512],
                        start=True, stop=False),
                        waits=[(ve, RB_VE), (se, PV_GATE)])
                    p.emit("tensor", lambda e: e.matmul(
                        ps2[0][:, 0:512], e2[0:1, :], probs[0:1, 512:1024],
                        start=False, stop=True), inc=pe)
                    RB_PE = pe.n
                    if LE == 3:
                        prev_chunk_gate = [(ve, ve.n)]
                        continue
                    p.emit("vector", lambda e: e.tensor_copy(
                        probs[:, 1024:1536], ps2[0][:, 0:512]),
                        waits=[(pe, RB_PE)], inc=ve)
                    RB2_VE = ve.n
                    if LE == 4:
                        prev_chunk_gate = [(ve, ve.n)]
                        continue
                    for hd in range(2):
                        p.emit("vector", lambda e, hd=hd, pr=pr, c=c: e.tensor_mul(
                            R2[hd * 64:(hd + 1) * 64,
                               pr * T + c * 512:pr * T + (c + 1) * 512],
                            ps2[3][0:64, hd * 512:(hd + 1) * 512],
                            probs[hd * 64:(hd + 1) * 64, 1024:1536]),
                            waits=[(ve, RB2_VE)] if hd == 0 else [], inc=ve)
                    prev_chunk_gate = [(ve, ve.n)]
                pair_done[pr] = pe.n
            return ve.n, pe.n, se.n

        attn_sublayer(0, wqT, wkT, wvT, 0.125)

        wo_t = load_slab(wB, woT, waits=[(pe, pe.n), (dcc, dcc.n)])
        s5p = S5_PARTS
        for c in (range(4) if stop >= 5 else []):
            tp_prev = []
            for f in range(NF):
                b = f % 4
                for a in range(NF):
                    w = []
                    if a == 0:
                        w = [(dw, wo_t), (ve, ve.n)]
                    p.emit("tensor", lambda e, f=f, c=c, a=a, b=b: e.matmul(
                        pbank(b),
                        wB[:, a * 1024 + f * 128:a * 1024 + (f + 1) * 128],
                        R2[:, a * T + c * 512:a * T + (c + 1) * 512],
                        start=(a == 0), stop=(a == NF - 1)),
                        waits=w, inc=pe if a == NF - 1 else None)
                wo_pe = pe.n
                p.emit("vector", lambda e, b=b: e.tensor_copy(
                    stg[:, 0:512], pbank(b)), waits=[(pe, wo_pe)] + tp_prev, inc=ve)
                ev_ve = ve.n
                for jj in range(4):
                    p.emit("tensor", lambda e, jj=jj: e.transpose(
                        psb(4)[:, jj * 128:(jj + 1) * 128],
                        stg[:, jj * 128:(jj + 1) * 128], ident[:]),
                        waits=[(ve, ev_ve)] if jj == 0 else [],
                        inc=pe if jj == 3 else None)
                tp_pe = pe.n
                tp_prev = [(pe, tp_pe)]
                p.emit("vector", lambda e, f=f: e.tensor_copy(
                    flatap(ysc, f * 128, [(4096, 128), (1024, 4), (1, 128)]),
                    psb(4)[:, 0:512].rearrange("p (a d) -> p a d", d=128)),
                    waits=[(pe, tp_pe)], inc=ve)
            YT_VE = ve.n
            if s5p < 2:
                continue
            for jj in range(4):
                j = c * 4 + jj
                p.emit("vector", lambda e, j=j, jj=jj: e.scalar_tensor_tensor(
                    ysc[:, jj * 1024:(jj + 1) * 1024],
                    x_res[:, j * D:(j + 1) * D], vslot(1)[:, j:j + 1],
                    ysc[:, jj * 1024:(jj + 1) * 1024], ALU.mult, ALU.add),
                    waits=[(ve, YT_VE)] if jj == 0 else [], inc=ve)
            T_VE = ve.n
            if s5p < 3:
                continue
            for jj in range(4):
                j = c * 4 + jj
                wsq = [(ve, T_VE), (se, se.n)] if jj < 2 else [(ve, ve.n)]
                p.emit("scalar", lambda e, j=j, jj=jj: e.activation(
                    scrf[:, (jj % 2) * 1024:(jj % 2) * 1024 + 1024],
                    ysc[:, jj * 1024:(jj + 1) * 1024], AF.Square,
                    bias=vslot(13)[:, j:j + 1]), waits=wsq, inc=se)
                p.emit("vector", lambda e, j=j, jj=jj: e.reduce_sum(
                    vslot(4)[:, j:j + 1],
                    scrf[:, (jj % 2) * 1024:(jj % 2) * 1024 + 1024].rearrange(
                        "p (a d) -> p a d", a=1), axis=AX.X),
                    waits=[(se, se.n)], chain=ve)
            c4 = vslot(5)[:, c * 4:c * 4 + 4]
            s4 = vslot(4)[:, c * 4:c * 4 + 4]
            p.emit("vector", lambda e, c4=c4, s4=s4: e.tensor_scalar(
                c4, s4, 1.0 / D, EPS_RMS, ALU.mult, ALU.add), chain=ve)
            p.emit("scalar", lambda e, c4=c4: e.activation(c4, c4, AF.Ln),
                   waits=[(ve, ve.n)], chain=se)
            p.emit("scalar", lambda e, c4=c4: e.activation(
                c4, c4, AF.Exp, scale=-0.5), chain=se)
            n4 = vslot(6)[:, c * 4:c * 4 + 4]
            m4 = vslot(3)[:, c * 4:c * 4 + 4]
            p.emit("vector", lambda e, n4=n4, m4=m4, c4=c4:
                   e.scalar_tensor_tensor(n4, m4, -1.0, c4, ALU.mult, ALU.mult),
                   waits=[(se, se.n)], chain=ve)
            apply_transpose(lambda j, c=c: ysc[:, (j - c * 4) * 1024:
                                               (j - c * 4 + 1) * 1024],
                            5, 6, R1, fold_cols=ln_cols[:, NF:2 * NF], jgs=(c,),
                            wait0=[(ve, ve.n), (se, se.n)])
            if s5p < 4:
                continue
            for jj in range(4):
                j = c * 4 + jj
                p.emit("vector", lambda e, j=j, jj=jj: e.tensor_add(
                    x_res[:, j * D:(j + 1) * D], x_res[:, j * D:(j + 1) * D],
                    ysc[:, jj * 1024:(jj + 1) * 1024]),
                    waits=[(ve, ve.n)] if jj == 0 else [], inc=ve)
                p.emit("vector", lambda e, j=j: e.tensor_scalar(
                    x_res[:, j * D:(j + 1) * D], x_res[:, j * D:(j + 1) * D],
                    vslot(3)[:, j:j + 1], None, ALU.subtract), inc=ve)
        H2_VE, H2_SE, H2_PE = ve.n, se.n, pe.n

        for srcT, dstbf, OW in ((w1T, w1bf, DFF), (w3T, w3bf, DFF), (w2T, w2bf, D)):
            ng = (D * DFF) // (128 * OW)
            p.emit("gpsimd", lambda e, s=srcT, d=dstbf, OW=OW, ng=ng: e.dma_start(
                flatap(d, 0, [(OW, 128), (OW * 128, ng), (1, OW)]),
                s.rearrange("(g p) o -> p g o", p=128)),
                inc=dw, inc_by=16)
        PRECAST = dw.n

        for c in (range(4) if stop >= 6 else []):
            wslab_t = {}

            def emit_ffn_load(gh, c, wv):
                zA = (gh % 2) * 4096
                p.emit("gpsimd", lambda e, gh=gh, zA=zA: e.dma_start(
                    wA[:, zA:zA + 4096].rearrange("p (a o) -> p a o", o=512),
                    flatap(w1bf, gh * 512, [(DFF, 128), (DFF * 128, 8), (1, 512)])),
                    waits=wv, inc=dw, inc_by=16)
                p.emit("gpsimd", lambda e, gh=gh, zA=zA: e.dma_start(
                    wB[:, zA:zA + 4096].rearrange("p (a o) -> p a o", o=512),
                    flatap(w3bf, gh * 512, [(DFF, 128), (DFF * 128, 8), (1, 512)])),
                    inc=dw, inc_by=16)
                wslab_t[gh] = dw.n

            emit_ffn_load(0, c, [(dw, PRECAST), (pe, pe.n)])
            emit_ffn_load(1, c, [])
            for gh in range(8):
                zA = (gh % 2) * 4096
                for dt in range(4):
                    dtg = gh * 4 + dt
                    b1 = dtg % 2
                    b3 = 2 + dtg % 2
                    for a in range(NF):
                        w = [] if a else [(dw, wslab_t[gh]),
                                          (se, se.n), (ve, ve.n)]
                        p.emit("tensor", lambda e, zA=zA, dt=dt, a=a, b1=b1, c=c:
                               e.matmul(
                            pbank(b1),
                            wA[:, zA + a * 512 + dt * 128:zA + a * 512 + (dt + 1) * 128],
                            R1[:, a * T + c * 512:a * T + (c + 1) * 512],
                            start=(a == 0), stop=(a == NF - 1)),
                            waits=w, inc=pe if a == NF - 1 else None)
                    p1_pe = pe.n
                    for a in range(NF):
                        p.emit("tensor", lambda e, zA=zA, dt=dt, a=a, b3=b3, c=c:
                               e.matmul(
                            pbank(b3),
                            wB[:, zA + a * 512 + dt * 128:zA + a * 512 + (dt + 1) * 128],
                            R1[:, a * T + c * 512:a * T + (c + 1) * 512],
                            start=(a == 0), stop=(a == NF - 1)),
                            waits=[(ve, ve.n)] if a == 0 else [],
                            inc=pe if a == NF - 1 else None)
                    p3_pe = pe.n
                    p.emit("scalar", lambda e, b1=b1, dtg=dtg: e.activation(
                        probs[:, 0:512] if dtg % 2 == 0 else probs[:, 512:1024],
                        pbank(b1), AF.Silu),
                        waits=[(pe, p1_pe), (se, se.n), (ve, ve.n)], inc=se)
                    p.emit("vector", lambda e, b3=b3, dtg=dtg: e.tensor_mul(
                        R2[:, dtg * 512:(dtg + 1) * 512],
                        probs[:, 0:512] if dtg % 2 == 0 else probs[:, 512:1024],
                        pbank(b3)),
                        waits=[(pe, p3_pe), (se, se.n)], inc=ve)
                if gh + 2 < 8:
                    emit_ffn_load(gh + 2, c, [(pe, pe.n)])
            P_VE = ve.n
            w2_t = {}

            def emit_w2_load(f, wv):
                zB = (f % 2) * 4096
                p.emit("gpsimd", lambda e, f=f, zB=zB: e.dma_start(
                    wA[:, zB:zB + 4096].rearrange("p (a o) -> p a o", o=128),
                    flatap(w2bf, f * 128, [(D, 128), (D * 128, 32), (1, 128)])),
                    waits=wv, inc=dw, inc_by=16)
                w2_t[f] = dw.n

            emit_w2_load(0, [(dw, PRECAST), (ve, P_VE), (pe, pe.n)])
            emit_w2_load(1, [])
            tp_prev = []
            bank_guard = {0: None, 1: None}
            for f in range(NF):
                zB = (f % 2) * 4096
                b = f % 2
                for dt in range(32):
                    w = [] if dt else ([(dw, w2_t[f]), (ve, P_VE)] +
                                       ([bank_guard[b]] if bank_guard[b] else []))
                    p.emit("tensor", lambda e, zB=zB, dt=dt, f=f, b=b: e.matmul(
                        pbank(b), wA[:, zB + dt * 128:zB + (dt + 1) * 128],
                        R2[:, dt * 512:(dt + 1) * 512],
                        start=(dt == 0), stop=(dt == 31)),
                        waits=w, inc=pe if dt == 31 else None)
                ff_pe = pe.n
                p.emit("vector", lambda e, b=b: e.tensor_copy(
                    stg[:, 2048:2560], pbank(b)),
                    waits=[(pe, ff_pe)] + tp_prev, inc=ve)
                bank_guard[b] = (ve, ve.n)
                ev_ve = ve.n
                for jj in range(4):
                    p.emit("tensor", lambda e, jj=jj: e.transpose(
                        psb(4)[:, jj * 128:(jj + 1) * 128],
                        stg[:, 2048 + jj * 128:2048 + (jj + 1) * 128], ident[:]),
                        waits=[(ve, ev_ve)] if jj == 0 else [],
                        inc=pe if jj == 3 else None)
                tp_pe = pe.n
                tp_prev = [(pe, tp_pe)]
                p.emit("vector", lambda e, f=f, c=c: e.tensor_add(
                    flatap(x_res, c * 4 * D + f * 128, [(NT * D, 128), (D, 4), (1, 128)]),
                    flatap(x_res, c * 4 * D + f * 128, [(NT * D, 128), (D, 4), (1, 128)]),
                    psb(4)[:, 0:512].rearrange("p (a d) -> p a d", d=128)),
                    waits=[(pe, tp_pe), (ve, ve.n)], inc=ve)
                if f + 2 < NF:
                    emit_w2_load(f + 2, [(pe, pe.n)])
        S1_VE, S1_SE, S1_PE = ve.n, se.n, pe.n

        def ffn_q(first):
            ln_stats(7, 8)
            mx = vslot(9); mn = vslot(10); av = vslot(11); cz = vslot(12)
            nbz = vslot(14)
            p.emit("vector", lambda e: e.reduce_max(
                mx, x_res[:].rearrange("p (j d) -> p j d", d=D), axis=AX.X),
                chain=ve)
            p.emit("vector", lambda e: e.tensor_reduce(
                mn, x_res[:].rearrange("p (j d) -> p j d", d=D), axis=AX.X,
                op=ALU.min), chain=ve)
            p.emit("vector", lambda e: e.tensor_sub(mx, mx, vslot(7)), chain=ve)
            p.emit("vector", lambda e: e.tensor_sub(mn, vslot(7), mn), chain=ve)
            p.emit("vector", lambda e: e.tensor_tensor(av, mx, mn, ALU.max), chain=ve)
            p.emit("vector", lambda e: e.tensor_mul(av, av, vslot(8)),
                   waits=[(se, se.n)], chain=ve)
            p.emit("vector", lambda e: e.tensor_scalar_add(av, av, EPS_Q), chain=ve)
            p.emit("vector", lambda e: e.reciprocal(cz, av), chain=ve)
            p.emit("vector", lambda e: e.tensor_scalar_mul(cz, cz, 127.0), chain=ve)
            p.emit("vector", lambda e: e.tensor_mul(cz, cz, vslot(8)), chain=ve)
            p.emit("vector", lambda e: e.scalar_tensor_tensor(
                nbz, vslot(7), -1.0, cz, ALU.mult, ALU.mult), chain=ve)
            LNQ_VE, LNQ_SE = ve.n, se.n

            if first:
                R2f = R2[:].bitcast(F32)
                p.emit("sync", lambda e: e.dma_start(
                    R2f[:, 0:8192].rearrange("p (a o) -> p a o", o=D),
                    fwT.rearrange("(a p) o -> p a o", p=128)),
                    waits=[(pe, pe.n), (ve, ve.n)], inc=dx, inc_by=16)
                FW_DX = dx.n
                p.emit("vector", lambda e: e.reduce_sum(
                    colsf[:, 4:5], R2f[:, 0:8192].rearrange(
                        "p (a o) -> p a o", a=1), axis=AX.X,
                    apply_absolute_value=True),
                    waits=[(dx, FW_DX)], chain=ve)
                p.emit("sync", lambda e: e.dma_start(
                    scrf[0:1, 0:128], colsf[:, 4:5]),
                    waits=[(ve, ve.n)], inc=dx, inc_by=16)
                p.emit("vector", lambda e: e.reduce_sum(
                    Mr1(32, 33), scrf[0:1, 0:128].rearrange("p (a o) -> p a o", a=1),
                    axis=AX.X), waits=[(dx, dx.n)], chain=ve)
                p.emit("vector", lambda e: e.tensor_scalar(
                    Mr1(32, 33), Mr1(32, 33), 1.0 / (D * D), EPS_Q,
                    ALU.mult, ALU.add), chain=ve)
                p.emit("vector", lambda e: e.reciprocal(
                    Mr1(33, 34), Mr1(32, 33)), chain=ve)
                p.emit("vector", lambda e: e.tensor_scalar_mul(
                    Mr1(34, 35), Mr1(32, 33), 1.0 / 127.0), chain=ve)
                WS_VE = ve.n
                p.emit("tensor", lambda e: e.matmul(
                    pbank(0)[:, 0:2], onesf32[0:1, :], Mr1(33, 35),
                    start=True, stop=True), waits=[(ve, WS_VE), (pe, pe.n)], inc=pe)
                p.emit("vector", lambda e: e.tensor_copy(
                    colsf[:, 5:7], pbank(0)[:, 0:2]), waits=[(pe, pe.n)], chain=ve)
                for ch in range(4):
                    z = ch * 2048
                    p.emit("vector", lambda e, z=z: e.tensor_scalar_mul(
                        R2f[:, z:z + 2048], R2f[:, z:z + 2048], colsf[:, 5:6]),
                        chain=ve)
                    p.emit("vector", lambda e, z=z: e.tensor_copy(
                        iscr[:, 0:2048], R2f[:, z:z + 2048]), chain=ve)
                    p.emit("vector", lambda e, z=z: e.tensor_copy(
                        R2f[:, z:z + 2048], iscr[:, 0:2048]), chain=ve)
                    p.emit("vector", lambda e, z=z: e.tensor_scalar(
                        R2f[:, z:z + 2048], R2f[:, z:z + 2048], 1.0, -1.0,
                        ALU.min, ALU.max), chain=ve)
                    p.emit("vector", lambda e, z=z: e.tensor_copy(
                        wA[:, z:z + 2048], R2f[:, z:z + 2048]), chain=ve)
                WQ_VE = ve.n
                p.emit("sync", lambda e: e.dma_start(wq_park.ap(), wA[:, 0:8192]),
                       waits=[(ve, WQ_VE)], inc=dcc, inc_by=16)
                wtgt, wsem = WQ_VE, ve
            else:
                p.emit("gpsimd", lambda e: e.dma_start(
                    wA[:, 0:8192], wq_park.ap()),
                    waits=[(pe, pe.n)], inc=dw, inc_by=16)
                wtgt, wsem = dw.n, dw

            prev_evac = []
            pe_tp = pe.n
            for jg in range(4):
                for jj in range(4):
                    j = jg * 4 + jj
                    w = [(ve, ve.n), (se, se.n)] if (jg == 0 and jj == 0) else []
                    if jj == 0 and jg > 0:
                        w.append((pe, pe_tp))
                    p.emit("vector", lambda e, j=j, jj=jj: e.tensor_scalar(
                        scrf[:, 0:1024], x_res[:, j * D:(j + 1) * D],
                        cz[:, j:j + 1], nbz[:, j:j + 1], ALU.mult, ALU.add),
                        waits=w, chain=ve)
                    p.emit("vector", lambda e, jj=jj: e.tensor_copy(
                        iscr[:, 1024:2048], scrf[:, 0:1024]), chain=ve)
                    p.emit("vector", lambda e, jj=jj: e.tensor_copy(
                        ysc[:, jj * 1024:(jj + 1) * 1024], iscr[:, 1024:2048]),
                        chain=ve)
                ve_app = ve.n
                for f in range(NF):
                    for jj in range(4):
                        w = []
                        if f == 0 and jj == 0:
                            w = [(ve, ve_app)] + prev_evac
                        p.emit("tensor", lambda e, f=f, jj=jj: e.transpose(
                            psb(f)[:, jj * 128:(jj + 1) * 128],
                            ysc[:, jj * 1024 + f * 128:jj * 1024 + (f + 1) * 128],
                            ident[:]),
                            waits=w, inc=pe if (f == NF - 1 and jj == 3) else None)
                pe_tp = pe.n
                prev_evac = []
                for f in range(NF):
                    p.emit("vector", lambda e, f=f, jg=jg: e.tensor_copy(
                        R1[:, f * T + jg * 512:f * T + (jg + 1) * 512],
                        psb(f)[:, 0:512]), waits=[(pe, pe_tp)], inc=ve)
                    prev_evac = [(ve, ve.n)]
            Z_VE = ve.n

            guards = [None] * 4
            tp_prev = []
            for f in range(NF):
                for n in range(4):
                    b = (f * 4 + n) % 4
                    for a in range(NF):
                        w = []
                        if a == 0:
                            w = [(wsem, wtgt), (ve, Z_VE)]
                            if guards[b] is not None:
                                w.append(guards[b])
                        p.emit("tensor", lambda e, f=f, n=n, a=a, b=b: e.matmul(
                            pbank(b),
                            wA[:, a * 1024 + f * 128:a * 1024 + (f + 1) * 128],
                            R1[:, a * T + n * 512:a * T + (n + 1) * 512],
                            start=(a == 0), stop=(a == NF - 1)),
                            waits=w, inc=pe if a == NF - 1 else None)
                    q_pe = pe.n
                    p.emit("vector", lambda e, b=b: e.tensor_scalar_mul(
                        probs[:, 0:512], pbank(b), colsf[:, 6:7]),
                        waits=[(pe, q_pe)] + tp_prev, inc=ve)
                    guards[b] = (ve, ve.n)
                    ev_ve = ve.n
                    for jj in range(4):
                        p.emit("tensor", lambda e, jj=jj: e.transpose(
                            psb(4)[:, jj * 128:(jj + 1) * 128],
                            probs[:, jj * 128:(jj + 1) * 128], ident[:]),
                            waits=[(ve, ev_ve)] if jj == 0 else [],
                            inc=pe if jj == 3 else None)
                    tp_pe = pe.n
                    tp_prev = [(pe, tp_pe)]
                    p.emit("vector", lambda e: e.tensor_copy(
                        probs[:, 512:1024], psb(4)[:, 0:512]),
                        waits=[(pe, tp_pe)], inc=ve)
                    pv_ve = ve.n
                    for jj in range(4):
                        j = n * 4 + jj
                        p.emit("vector", lambda e, j=j, jj=jj, f=f:
                               e.scalar_tensor_tensor(
                            x_res[:, j * D + f * 128:j * D + (f + 1) * 128],
                            probs[:, 512 + jj * 128:512 + (jj + 1) * 128],
                            av[:, j:j + 1],
                            x_res[:, j * D + f * 128:j * D + (f + 1) * 128],
                            ALU.mult, ALU.add),
                            waits=[(ve, pv_ve)] if jj == 0 else [], inc=ve)

        if n_sub >= 2:
            ffn_q(True)

        if n_sub >= 3:
            ln_stats(0, 1)
            p.emit("vector", lambda e: e.scalar_tensor_tensor(
                vslot(15), vslot(0), -1.0, vslot(1), ALU.mult, ALU.mult),
                waits=[(se, se.n)], chain=ve)
            apply_transpose(lambda j: x_res[:, j * D:(j + 1) * D], 1, 15, R1,
                            fold_cols=None, wait0=[(ve, ve.n), (se, se.n),
                                                   (pe, pe.n)])
            attn_sublayer(1, bqT, bkT, bvT, 1.0)
            wo2_t = load_slab(wB, boT, waits=[(pe, pe.n), (dcc, dcc.n)])
            tp_prev = []
            bank_guard2 = {}
            for c in range(4):
                for f in range(NF):
                    b = f % 4
                    for a in range(NF):
                        w = []
                        if a == 0:
                            w = [(dw, wo2_t), (ve, ve.n)]
                            if bank_guard2.get(b):
                                w.append(bank_guard2[b])
                        p.emit("tensor", lambda e, f=f, c=c, a=a, b=b: e.matmul(
                            pbank(b),
                            wB[:, a * 1024 + f * 128:a * 1024 + (f + 1) * 128],
                            R2[:, a * T + c * 512:a * T + (c + 1) * 512],
                            start=(a == 0), stop=(a == NF - 1)),
                            waits=w, inc=pe if a == NF - 1 else None)
                    wo_pe = pe.n
                    p.emit("vector", lambda e, b=b, f=f: e.tensor_scalar(
                        stg[:, 0:512], pbank(b), 1.0,
                        ln_cols[:, 2 * NF + f:2 * NF + f + 1], ALU.mult, ALU.add),
                        waits=[(pe, wo_pe)] + tp_prev, inc=ve)
                    bank_guard2[b] = (ve, ve.n)
                    ev_ve = ve.n
                    for jj in range(4):
                        p.emit("tensor", lambda e, jj=jj: e.transpose(
                            psb(4)[:, jj * 128:(jj + 1) * 128],
                            stg[:, jj * 128:(jj + 1) * 128], ident[:]),
                            waits=[(ve, ev_ve)] if jj == 0 else [],
                            inc=pe if jj == 3 else None)
                    tp_pe = pe.n
                    tp_prev = [(pe, tp_pe)]
                    p.emit("vector", lambda e, f=f, c=c: e.tensor_add(
                        flatap(x_res, c * 4 * D + f * 128,
                               [(NT * D, 128), (D, 4), (1, 128)]),
                        flatap(x_res, c * 4 * D + f * 128,
                               [(NT * D, 128), (D, 4), (1, 128)]),
                        psb(4)[:, 0:512].rearrange("p (a d) -> p a d", d=128)),
                        waits=[(pe, tp_pe), (ve, ve.n)], inc=ve)

        if n_sub >= 4:
            ffn_q(False)

        TAPS = []
        loc = locals()
        for name, fn in DEV_TAPS:
            ap_sb, shape, dtype = fn(loc)
            o = nc.dram_tensor(f"tap_{name}", shape, dtype, kind="ExternalOutput").ap()
            TAPS.append((o, ap_sb, dtype))

        p.emit("sync", lambda e: e.dma_start(
            xout.rearrange("(j p) d -> p j d", p=128),
            x_res[:].rearrange("p (j d) -> p j d", d=D)),
            waits=[(se, se.n), (ve, ve.n), (pe, pe.n), (po, po.n), (dw, dw.n),
                   (dcc, dcc.n), (dx, dx.n)],
            inc=od, inc_by=16)
        for o, ap_sb, dtype in TAPS:
            if dtype == F32:
                p.emit("gpsimd", lambda e, o=o, a=ap_sb: e.dma_start(o, a),
                       waits=[(od, 16)], inc=od, inc_by=16)
            else:
                p.emit("sync", lambda e, o=o, a=ap_sb: e.dma_start(o, a),
                       inc=od, inc_by=16)
        p.emit("sync", lambda e: None, waits=[(od, od.n)])

        p.finalize(nc)
    return nc


_CACHE = {}


def _grow(v):
    v = np.asarray(v, np.float32)
    r = np.zeros((2, 128), BF)
    r[0, 0:64] = v.astype(BF)
    r[1, 64:128] = v.astype(BF)
    return r


def _prep_inputs(inputs, h):
    idx, cos, sin, maskdup, ident, pswap = _host_aux(h)
    tp = lambda a: np.ascontiguousarray(np.asarray(a, np.float32).T)
    g2 = lambda v: np.tile(np.asarray(v, np.float32), 2).reshape(128, 1)
    d = {
        "wqT": tp(inputs["L0_attn_a_wq"]), "wkT": tp(inputs["L0_attn_a_wk"]),
        "wvT": tp(inputs["L0_attn_a_wv"]), "woT": tp(inputs["L0_attn_a_wo"]),
        "w1T": tp(inputs["L0_attn_a_w1"]), "w3T": tp(inputs["L0_attn_a_w3"]),
        "w2T": tp(inputs["L0_attn_a_w2"]),
        "fwT": tp(inputs["L0_ffn_weight"]),
        "bqT": tp(inputs["L0_attn_b_Wq"]), "bkT": tp(inputs["L0_attn_b_Wk"]),
        "bvT": tp(inputs["L0_attn_b_Wv"]), "boT": tp(inputs["L0_attn_b_Wo"]),
        "rcos": cos, "rsin": sin,
        "maskin": maskdup, "identin": ident, "pswapin": pswap,
        "gqc": g2(inputs["L0_attn_b_gq"]), "bqc": g2(inputs["L0_attn_b_bq"]),
        "gkc": g2(inputs["L0_attn_b_gk"]), "bkc": g2(inputs["L0_attn_b_bk"]),
        "boc": np.ascontiguousarray(
            np.asarray(inputs["L0_attn_b_bo"], np.float32).reshape(NF, 128).T),
        "growsq": _grow(inputs["L0_attn_b_gq"]),
        "growsk": _grow(inputs["L0_attn_b_gk"]),
        "nbrowq": _grow(-np.asarray(inputs["L0_attn_b_bq"], np.float64)),
        "nbrowk": _grow(-np.asarray(inputs["L0_attn_b_bk"], np.float64)),
        "nbdupq": np.tile(-np.asarray(inputs["L0_attn_b_bq"], np.float32),
                          2).reshape(1, 128).astype(BF),
        "nbdupk": np.tile(-np.asarray(inputs["L0_attn_b_bk"], np.float32),
                          2).reshape(1, 128).astype(BF),
        "n1c": np.ascontiguousarray(
            np.asarray(inputs["L0_attn_a_norm1_w"], np.float32).reshape(NF, 128).T),
        "n2c": np.ascontiguousarray(
            np.asarray(inputs["L0_attn_a_norm2_w"], np.float32).reshape(NF, 128).T),
    }
    return idx, d


def kernel(**inputs):
    inputs.get("L0_attn_b_mask")
    x = np.asarray(inputs["L0_attn_a_x"], np.float32)
    if "prog" not in _CACHE:
        _CACHE["prog"] = build_program()
    nc = _CACHE["prog"]

    in_maps = []
    idxs = []
    for c in range(8):
        b, h = c // 2, c % 2
        idx, d = _prep_inputs(inputs, h)
        d["xin"] = np.ascontiguousarray(x[b, idx, :])
        in_maps.append(d)
        idxs.append(idx)

    res = run_bass_kernel_spmd(nc, in_maps, core_ids=list(range(8)))

    out = np.empty_like(x)
    for c in range(8):
        b = c // 2
        out[b, idxs[c], :] = np.asarray(res.results[c]["xout"], np.float32)
    extras = {}
    for c in range(8):
        for k, v in res.results[c].items():
            if k.startswith("tap_"):
                extras.setdefault(k, []).append(np.asarray(v))
    kernel.last_taps = extras
    return out


# version 05 (final)
# speedup vs baseline: 1.5053x; 1.2767x over previous
import contextlib
import numpy as np
import ml_dtypes
import concourse.bass as bass
import concourse.mybir as mybir
from concourse.bass_utils import run_bass_kernel_spmd

F32 = mybir.dt.float32
BF16 = mybir.dt.bfloat16
I32 = mybir.dt.int32
AF = mybir.ActivationFunctionType
ALU = mybir.AluOpType
AX = mybir.AxisListType
BF = ml_dtypes.bfloat16

D = 1024
T = 2048
H = 16
DK = 64
DFF = 4096
NT = 16
NF = 8
NP = 8
EPS_LN = 1e-5
EPS_RMS = 1e-5
EPS_Q = 1e-8
ROPE_THETA = 10000.0
BSHIFT = 40.0

N_SUBLAYERS = 4
STOP_STAGE = 99
ATT_LIMIT = (99, 99, 99, 99)
STOP_LI = None
S5_PARTS = 9
DEV_TAPS = []


def _zigzag_idx(h):
    r = np.arange(T)
    return (2 * (r // 128) + h) * 128 + (r % 128)


def _rope_tables(pos):
    k = np.arange(DK) // 2
    inv = ROPE_THETA ** (-2.0 * k.astype(np.float64) / DK)
    ang = pos[None, :].astype(np.float64) * inv[:, None]
    cos = np.cos(ang)
    sgn = np.where(np.arange(DK) % 2 == 0, -1.0, 1.0)
    sin = np.sin(ang) * sgn[:, None]
    return cos.astype(np.float32), sin.astype(np.float32)


def _host_aux(h):
    idx = _zigzag_idx(h)
    cos, sin = _rope_tables(idx)
    stair = (np.arange(128)[:, None] <= np.arange(128)[None, :]).astype(BF)
    ones = np.ones((128, 128), BF)
    zeros = np.zeros((128, 128), BF)
    slots = [stair, zeros] if h == 0 else [ones, stair]
    maskdup = np.concatenate([np.concatenate([s, s], axis=1) for s in slots], axis=1)
    ident = np.eye(128, dtype=BF)
    pswap = np.zeros((128, 128), BF)
    dd = np.arange(128)
    sw = np.where(dd % 2 == 0, dd + 1, dd - 1)
    pswap[sw, dd] = 1.0
    return idx, cos, sin, maskdup, ident, pswap


class Sem:
    def __init__(self, es, nc, name):
        self.h = es.enter_context(nc.semaphore(name))
        self.n = 0

    def bump(self, k=1):
        self.n += k
        return self.n


class Prog:
    ENGS = ("sync", "gpsimd", "tensor", "vector", "scalar")

    def __init__(self):
        self.rec = {e: [] for e in self.ENGS}

    def emit(self, eng, builder, waits=(), inc=None, inc_by=1, chain=None):
        if chain is not None:
            waits = list(waits) + [(chain, chain.n)]
            inc = chain
            inc_by = 1
        waits = tuple((s, v) for (s, v) in waits)

        def fn(engine):
            for s, v in waits:
                engine.wait_ge(s.h, v)
            inst = builder(engine)
            if inc is not None:
                if inst is None:
                    engine.sem_inc(inc.h, inc_by)
                else:
                    inst.then_inc(inc.h, inc_by)
        self.rec[eng].append(fn)
        if inc is not None:
            inc.bump(inc_by)

    def finalize(self, nc):
        with nc.Block() as block:
            for e in self.ENGS:
                recs = tuple(self.rec[e])

                def body(engine, _r=recs):
                    for f in _r:
                        f(engine)
                getattr(block, e)(body)


def build_program(n_sub=None):
    if n_sub is None:
        n_sub = N_SUBLAYERS
    stop = STOP_STAGE if STOP_LI is None else 99
    nc = bass.Bass("TRN2", target_bir_lowering=False)

    def inp(name, shape, dtype=F32):
        return nc.dram_tensor(name, shape, dtype, kind="ExternalInput").ap()

    xin = inp("xin", [T, D])
    wqT = inp("wqT", [D, D]); wkT = inp("wkT", [D, D]); wvT = inp("wvT", [D, D])
    woT = inp("woT", [D, D])
    w1T = inp("w1T", [D, DFF]); w3T = inp("w3T", [D, DFF]); w2T = inp("w2T", [DFF, D])
    fwT = inp("fwT", [D, D])
    bqT = inp("bqT", [D, D]); bkT = inp("bkT", [D, D]); bvT = inp("bvT", [D, D])
    boT = inp("boT", [D, D])
    rcos = inp("rcos", [DK, T]); rsin = inp("rsin", [DK, T])
    maskin = inp("maskin", [128, 512], BF16)
    identin = inp("identin", [128, 128], BF16)
    pswapin = inp("pswapin", [128, 128], BF16)
    gqc = inp("gqc", [128, 1]); bqc = inp("bqc", [128, 1])
    gkc = inp("gkc", [128, 1]); bkc = inp("bkc", [128, 1])
    growsq = inp("growsq", [2, 128], BF16)
    growsk = inp("growsk", [2, 128], BF16)
    nbrowq = inp("nbrowq", [2, 128], BF16)
    nbrowk = inp("nbrowk", [2, 128], BF16)
    nbdupq = inp("nbdupq", [1, 128], BF16)
    nbdupk = inp("nbdupk", [1, 128], BF16)
    boc = inp("boc", [128, NF])
    n1c = inp("n1c", [128, NF]); n2c = inp("n2c", [128, NF])

    xout = nc.dram_tensor("xout", [T, D], F32, kind="ExternalOutput").ap()

    KSZ = NP * 128 * T
    VOFF = KSZ
    MOFF = 2 * KSZ
    CCE = 2 * KSZ + 2048
    cc_in_h = [nc.dram_tensor(f"cc_in{i}", [CCE // 2048, 2048], BF16) for i in range(2)]
    cc_out_h = [nc.dram_tensor(f"cc_out{i}", [2 * CCE // 2048, 2048], BF16)
                for i in range(2)]
    wq_park = nc.dram_tensor("wq_park", [128, 8192], BF16)
    w1bf = nc.dram_tensor("w1bf", [D * DFF // 2048, 2048], BF16)
    w3bf = nc.dram_tensor("w3bf", [D * DFF // 2048, 2048], BF16)
    w2bf = nc.dram_tensor("w2bf", [D * DFF // 2048, 2048], BF16)

    es = contextlib.ExitStack()
    with es:
        def sbt(name, shape, dtype=BF16):
            return es.enter_context(nc.sbuf_tensor(name, shape, dtype))

        def psumt(name, shape, dtype=F32):
            return es.enter_context(nc.psum_tensor(name, shape, dtype))

        x_res = sbt("x_res", [128, NT * D], F32)
        R1 = sbt("R1", [128, 16384])
        R2 = sbt("R2", [128, 16384])
        wA = sbt("wA", [128, 8256])
        wB = sbt("wB", [128, 10304])
        ysc = sbt("ysc", [128, 4096])
        stg = sbt("stg", [128, 4096])
        probs = sbt("probs", [128, 3072])
        scrf = stg[:].bitcast(F32)
        iscr = stg[:].bitcast(I32)
        yscf = ysc[:].bitcast(F32)
        rrowf = wB[0:1, 8256:10304].bitcast(F32)
        mask = sbt("mask", [128, 512])
        ident = sbt("ident", [128, 128])
        pswap = sbt("pswap", [128, 128])
        e1 = sbt("e1", [1, 128])
        e2 = sbt("e2", [1, 128])
        bd2 = sbt("bd2", [128, 4])
        onesrow = sbt("onesrow", [33, 512])
        growq = sbt("growq", [33, 128])
        growk = sbt("growk", [33, 128])
        nbq = sbt("nbq", [33, 128])
        nbk = sbt("nbk", [33, 128])
        growq2 = sbt("growq2", [1, 128])
        growk2 = sbt("growk2", [1, 128])
        nbqf = sbt("nbqf", [1, 128])
        nbkf = sbt("nbkf", [1, 128])
        onesf32 = es.enter_context(nc.sbuf_tensor("onesf32", [1, 128], F32))
        vecs = es.enter_context(nc.sbuf_tensor("vecs", [128, 256], F32))
        colsf = es.enter_context(nc.sbuf_tensor("colsf", [128, 16], F32))
        Mrow = es.enter_context(nc.sbuf_tensor("Mrow", [2, 64], F32))
        biascol = es.enter_context(nc.sbuf_tensor("biascol", [128, 16], F32))
        ln_cols = es.enter_context(nc.sbuf_tensor("ln_cols", [128, 3 * NF], F32))

        def Mq(z):
            return Mrow[0:2, z * 8:(z + 1) * 8]

        def Mr1(a, b):
            return Mrow[0:1, a:b]

        ps2 = [psumt(f"ps2_{i}", [128, 1024]) for i in range(4)]

        def pbank(i):
            return ps2[i // 2][:, (i % 2) * 512:(i % 2 + 1) * 512]

        def psb(i):
            return ps2[i // 2][:].bitcast(BF16)[:, (i % 2) * 1024:(i % 2 + 1) * 1024]

        p = Prog()
        SS = {n: Sem(es, nc, n) for n in
              ("dx", "dw", "dcc", "pe", "ve", "se", "po", "od")}
        dx, dw, dcc = SS["dx"], SS["dw"], SS["dcc"]
        pe, ve, se, po, od = SS["pe"], SS["ve"], SS["se"], SS["po"], SS["od"]

        def flatap(h, off, dims):
            return bass.AP(h, off, [list(d) for d in dims])

        p.emit("sync", lambda e: e.dma_start(
            x_res[:].rearrange("p (j d) -> p j d", d=D),
            xin.rearrange("(j p) d -> p j d", p=128)), inc=dx, inc_by=16)
        for dst, srcap in ((mask[:], maskin), (ident[:], identin),
                           (pswap[:], pswapin),
                           (colsf[:, 0:1], gqc), (colsf[:, 1:2], bqc),
                           (colsf[:, 2:3], gkc), (colsf[:, 3:4], bkc),
                           (ln_cols[:, 0:NF], n1c), (ln_cols[:, NF:2 * NF], n2c),
                           (ln_cols[:, 2 * NF:3 * NF], boc),
                           (growq[0:1, :], growsq[0:1, :]),
                           (growq[32:33, :], growsq[1:2, :]),
                           (growk[0:1, :], growsk[0:1, :]),
                           (growk[32:33, :], growsk[1:2, :]),
                           (nbq[0:1, :], nbrowq[0:1, :]),
                           (nbq[32:33, :], nbrowq[1:2, :]),
                           (nbk[0:1, :], nbrowk[0:1, :]),
                           (nbk[32:33, :], nbrowk[1:2, :]),
                           (growq2[0:1, :], growsq[1:2, :]),
                           (growk2[0:1, :], growsk[1:2, :]),
                           (nbqf[0:1, :], nbdupq),
                           (nbkf[0:1, :], nbdupk)):
            p.emit("sync", lambda e, d=dst, s=srcap: e.dma_start(d, s),
                   inc=dx, inc_by=16)
        DX0 = dx.n

        for ap_, val in ((e1[0:1, 0:64], 1.0), (e1[0:1, 64:128], 0.0),
                         (e2[0:1, 0:64], 0.0), (e2[0:1, 64:128], 1.0),
                         (bd2[0:64, 0:1], 1.0 / 64), (bd2[64:128, 0:1], 0.0),
                         (bd2[0:64, 1:2], 0.0), (bd2[64:128, 1:2], 1.0 / 64),
                         (bd2[0:64, 2:3], 1.0), (bd2[64:128, 2:3], 0.0),
                         (bd2[0:64, 3:4], 0.0), (bd2[64:128, 3:4], 1.0),
                         (onesrow[0:1, :], 1.0), (onesrow[32:33, :], 1.0),
                         (onesf32[:], 1.0)):
            p.emit("vector", lambda e, a=ap_, v=val: e.memset(a, v), chain=ve)
        for wbuf in (wA, wB):
            p.emit("vector", lambda e, wbuf=wbuf: e.memset(
                flatap(wbuf, 4096 + 64, [(wbuf.shape[1], 128), (130, 32), (1, 1)]),
                1.0), chain=ve)
            p.emit("vector", lambda e, wbuf=wbuf: e.memset(
                flatap(wbuf, 4096 + 129, [(wbuf.shape[1], 128), (130, 32), (1, 1)]),
                1.0), chain=ve)
        VE_CONST = ve.n

        def vslot(s_):
            return vecs[:, s_ * 16:(s_ + 1) * 16]

        def ln_stats(slot_m, slot_r, extra_rms_slot=None):
            m = vslot(slot_m); r = vslot(slot_r); nb = vslot(14)
            p.emit("vector", lambda e: e.reduce_sum(
                m, x_res[:].rearrange("p (j d) -> p j d", d=D), axis=AX.X),
                waits=[(dx, DX0)], chain=ve)
            p.emit("vector", lambda e: e.tensor_scalar_mul(m, m, 1.0 / D), chain=ve)
            for j in range(NT):
                half = (j % 2) * 1024
                wse = [(ve, ve.n)] if j >= 2 else [(dx, DX0), (se, se.n), (ve, ve.n)]
                p.emit("scalar", lambda e, j=j, half=half: e.activation(
                    scrf[:, half:half + 1024], x_res[:, j * D:(j + 1) * D],
                    AF.Square), waits=wse, inc=se)
                p.emit("vector", lambda e, j=j, half=half: e.reduce_sum(
                    r[:, j:j + 1],
                    scrf[:, half:half + 1024].rearrange("p (a d) -> p a d", a=1),
                    axis=AX.X), waits=[(se, se.n)], chain=ve)
            p.emit("vector", lambda e: e.tensor_scalar_mul(r, r, 1.0 / D), chain=ve)
            p.emit("vector", lambda e: e.tensor_mul(nb, m, m), chain=ve)
            p.emit("vector", lambda e: e.tensor_sub(r, r, nb), chain=ve)
            p.emit("vector", lambda e: e.tensor_scalar_add(r, r, EPS_LN), chain=ve)
            p.emit("scalar", lambda e: e.activation(r, r, AF.Ln),
                   waits=[(ve, ve.n)], chain=se)
            p.emit("scalar", lambda e: e.activation(r, r, AF.Exp, scale=-0.5),
                   chain=se)
            if extra_rms_slot is not None:
                s1 = vslot(extra_rms_slot)
                p.emit("vector", lambda e: e.tensor_mul(s1, r, r),
                       waits=[(se, se.n)], chain=ve)
                p.emit("vector", lambda e: e.tensor_scalar(
                    s1, s1, -EPS_LN, 1.0 + EPS_RMS, ALU.mult, ALU.add), chain=ve)
                p.emit("scalar", lambda e: e.activation(s1, s1, AF.Ln),
                       waits=[(ve, ve.n)], chain=se)
                p.emit("scalar", lambda e: e.activation(s1, s1, AF.Exp, scale=-0.5),
                       chain=se)
                p.emit("vector", lambda e: e.tensor_mul(s1, s1, r),
                       waits=[(se, se.n)], chain=ve)
            return se.n, ve.n

        def apply_transpose(src_fn, slot_s, slot_nb, dst, fold_cols=None,
                            jgs=(0, 1, 2, 3), wait0=()):
            s_ = vslot(slot_s); nb = vslot(slot_nb)
            prev_evac = []
            pe_tp = pe.n
            jgs = list(jgs)
            for gi, jg in enumerate(jgs):
                for jj in range(4):
                    j = jg * 4 + jj
                    w = list(wait0) if (jj == 0 and gi == 0) else []
                    if jj == 0 and gi > 0:
                        w.append((pe, pe_tp))
                    p.emit("vector", lambda e, j=j, jj=jj: e.tensor_scalar(
                        stg[:, jj * 1024:(jj + 1) * 1024], src_fn(j),
                        s_[:, j:j + 1], nb[:, j:j + 1], ALU.mult, ALU.add),
                        waits=w, inc=ve)
                ve_app = ve.n
                for f in range(NF):
                    for jj in range(4):
                        w = []
                        if f == 0 and jj == 0:
                            w = [(ve, ve_app)] + prev_evac
                        p.emit("tensor", lambda e, f=f, jj=jj: e.transpose(
                            psb(f)[:, jj * 128:(jj + 1) * 128],
                            stg[:, jj * 1024 + f * 128:jj * 1024 + (f + 1) * 128],
                            ident[:]),
                            waits=w, inc=pe if (f == NF - 1 and jj == 3) else None)
                pe_tp = pe.n
                prev_evac = []
                for f in range(NF):
                    if fold_cols is not None:
                        p.emit("vector", lambda e, f=f, jg=jg: e.tensor_scalar_mul(
                            dst[:, f * T + jg * 512:f * T + (jg + 1) * 512],
                            psb(f)[:, 0:512], fold_cols[:, f:f + 1]),
                            waits=[(pe, pe_tp)], inc=ve)
                    else:
                        p.emit("vector", lambda e, f=f, jg=jg: e.tensor_copy(
                            dst[:, f * T + jg * 512:f * T + (jg + 1) * 512],
                            psb(f)[:, 0:512]),
                            waits=[(pe, pe_tp)], inc=ve)
                    prev_evac = [(ve, ve.n)]
            return se.n, ve.n, pe.n

        def load_slab(dst, srcT, waits=()):
            p.emit("gpsimd", lambda e, dst=dst, srcT=srcT: e.dma_start(
                dst[:, 0:8192].rearrange("p (a o) -> p a o", o=D),
                srcT.rearrange("(a p) o -> p a o", p=128)),
                waits=waits, inc=dw, inc_by=16)
            return dw.n

        def proj_fm(slab, w_tgt, rhs_tgt, evac_fn, w_sem=None):
            if w_sem is None:
                w_sem = dw
            guards = [None, None, None, None]
            for f in range(NF):
                for n in range(4):
                    b = (f * 4 + n) % 4
                    for a in range(NF):
                        w = []
                        if a == 0:
                            w = [(w_sem, w_tgt), (ve, rhs_tgt)]
                            if guards[b] is not None:
                                w.append(guards[b])
                        p.emit("tensor", lambda e, f=f, n=n, a=a, b=b, slab=slab:
                               e.matmul(
                            pbank(b),
                            slab[:, a * 1024 + f * 128:a * 1024 + (f + 1) * 128],
                            R1[:, a * T + n * 512:a * T + (n + 1) * 512],
                            start=(a == 0), stop=(a == NF - 1)),
                            waits=w, inc=pe if a == NF - 1 else None)
                    guards[b] = evac_fn(f, n, b, pe.n)
            return pe.n

        ln_stats(0, 1, extra_rms_slot=2)
        p.emit("vector", lambda e: e.scalar_tensor_tensor(
            vslot(15), vslot(0), -1.0, vslot(2), ALU.mult, ALU.mult), chain=ve)
        p.emit("vector", lambda e: e.scalar_tensor_tensor(
            vslot(3), vslot(0), 1.0, vslot(1), ALU.mult, ALU.mult), chain=ve)
        p.emit("vector", lambda e: e.tensor_scalar_mul(
            vslot(13), vslot(3), -1.0), chain=ve)
        apply_transpose(lambda j: x_res[:, j * D:(j + 1) * D], 2, 15, R1,
                        fold_cols=ln_cols[:, 0:NF],
                        wait0=[(ve, ve.n), (se, se.n)])

        def attn_sublayer(li, wq_d, wk_d, wv_d, att_scale):
            cc_i, cc_o = cc_in_h[li], cc_out_h[li]
            stop = STOP_STAGE if (STOP_LI is None or li == STOP_LI) else 99
            lim = ATT_LIMIT if (STOP_LI is None or li == STOP_LI) \
                else (99, 99, 99, 99)
            if stop < 2:
                return 0, 0, 0
            if li == 0:
                for dstv, tbl in ((scrf, rcos), (yscf, rsin)):
                    for pp in range(2):
                        p.emit("sync", lambda e, dstv=dstv, tbl=tbl, pp=pp:
                               e.dma_start(dstv[pp * 64:(pp + 1) * 64, 0:2048], tbl),
                               waits=[(ve, ve.n), (pe, pe.n)] if pp == 0 else [],
                               inc=dx, inc_by=16)
            RT_DX = dx.n
            HFM_VE = ve.n

            wq_t = load_slab(wA, wq_d, waits=[(pe, pe.n)])
            wk_t = load_slab(wB, wk_d)

            def evac_q(f, n, b, pet):
                p.emit("vector", lambda e, f=f, n=n, b=b: e.tensor_copy(
                    R2[:, f * T + n * 512:f * T + (n + 1) * 512], pbank(b)),
                    waits=[(pe, pet)], inc=ve)
                return (ve, ve.n)
            proj_fm(wA, wq_t, HFM_VE, evac_q)
            wv_t = load_slab(wA, wv_d, waits=[(pe, pe.n)])

            first = {}

            def normfold(zi, f, wse):
                for n2 in range(2):
                    p.emit("tensor", lambda e, n2=n2: e.matmul(
                        ps2[3][0:2, 0:512], bd2[:, 2:4],
                        probs[:, 2048 + n2 * 512:2048 + (n2 + 1) * 512],
                        start=True, stop=True),
                        waits=wse + [(ve, ve.n)], inc=pe)
                    nm_pe = pe.n
                    p.emit("vector", lambda e: e.reduce_max(
                        Mq(6)[:, 0:1],
                        ps2[3][0:2, 0:512].rearrange("p (a n) -> p a n", a=1),
                        axis=AX.X), waits=[(pe, nm_pe)], chain=ve)
                    key = (zi, f)
                    if key not in first:
                        first[key] = True
                        p.emit("vector", lambda e, zi=zi, f=f: e.tensor_copy(
                            Mq(zi)[:, f:f + 1], Mq(6)[:, 0:1]), chain=ve)
                    else:
                        p.emit("vector", lambda e, zi=zi, f=f: e.tensor_tensor(
                            Mq(zi)[:, f:f + 1], Mq(zi)[:, f:f + 1], Mq(6)[:, 0:1],
                            ALU.max), chain=ve)
                    wse = []

            st = {"ktile_free": 0, "kship": 0, "sq_pe": 0}

            def qkln(buf_fn, grow, grow2, nbf):
                for ch2 in range(2):
                    z2 = ch2 * 1024
                    p.emit("scalar", lambda e, z2=z2: e.activation(
                        probs[:, 2048:3072], buf_fn(z2), AF.Square),
                        waits=[(ve, ve.n), (se, se.n), (pe, st["sq_pe"])],
                        inc=se)
                    sq_se = se.n
                    for ch in range(2):
                        for hd in range(2):
                            rp = hd * 32
                            p.emit("tensor", lambda e, z2=z2, ch=ch, hd=hd,
                                   rp=rp: e.matmul(
                                ps2[2][rp:rp + 1, ch * 512:(ch + 1) * 512],
                                bd2[:, hd:hd + 1],
                                buf_fn(z2)[:, ch * 512:(ch + 1) * 512],
                                start=True, stop=True),
                                waits=[(ve, ve.n), (se, sq_se)]
                                if (ch == 0 and hd == 0) else [], inc=None)
                            p.emit("tensor", lambda e, ch=ch, hd=hd, rp=rp:
                                   e.matmul(
                                ps2[3][rp:rp + 1, ch * 512:(ch + 1) * 512],
                                bd2[:, hd:hd + 1],
                                probs[:, 2048 + ch * 512:2048 + (ch + 1) * 512],
                                start=True, stop=True),
                                inc=pe if (ch == 1 and hd == 1) else None)
                    mm_pe = pe.n
                    p.emit("vector", lambda e: e.tensor_copy(
                        yscf[0:1, 0:1024], ps2[2][0:1, 0:1024]),
                        waits=[(pe, mm_pe)], chain=ve)
                    p.emit("vector", lambda e: e.tensor_copy(
                        scrf[0:1, 0:1024], ps2[2][32:33, 0:1024]), chain=ve)
                    p.emit("vector", lambda e: e.tensor_copy(
                        yscf[32:33, 0:1024], ps2[3][0:1, 0:1024]), chain=ve)
                    p.emit("vector", lambda e: e.tensor_copy(
                        scrf[32:33, 0:1024], ps2[3][32:33, 0:1024]), chain=ve)
                    for mrow, srow in ((yscf, yscf), (scrf, scrf)):
                        p.emit("vector", lambda e, mrow=mrow, srow=srow:
                               e.tensor_mul(
                            srow[32:33, 1024:2048], mrow[0:1, 0:1024],
                            mrow[0:1, 0:1024]), chain=ve)
                        p.emit("vector", lambda e, srow=srow: e.tensor_sub(
                            srow[32:33, 0:1024], srow[32:33, 0:1024],
                            srow[32:33, 1024:2048]), chain=ve)
                        p.emit("vector", lambda e, srow=srow:
                               e.tensor_scalar_add(
                            srow[32:33, 0:1024], srow[32:33, 0:1024],
                            EPS_LN), chain=ve)
                    for srow in (yscf, scrf):
                        p.emit("scalar", lambda e, srow=srow: e.activation(
                            srow[32:33, 0:1024], srow[32:33, 0:1024], AF.Ln),
                            waits=[(ve, ve.n)] if srow is yscf else [],
                            chain=se)
                        p.emit("scalar", lambda e, srow=srow: e.activation(
                            srow[0:1, 1024:2048], srow[32:33, 0:1024],
                            AF.Exp, scale=-0.5), chain=se)
                    for mrow in (yscf, scrf):
                        p.emit("vector", lambda e, mrow=mrow: e.tensor_mul(
                            mrow[32:33, 1024:2048], mrow[0:1, 0:1024],
                            mrow[0:1, 1024:2048]),
                            waits=[(se, se.n)] if mrow is yscf else [],
                            chain=ve)
                    p.emit("vector", lambda e: e.tensor_copy(
                        stg[0:1, 0:1024], yscf[0:1, 1024:2048]), chain=ve)
                    p.emit("vector", lambda e: e.tensor_copy(
                        stg[0:1, 1024:2048], scrf[0:1, 1024:2048]), chain=ve)
                    p.emit("vector", lambda e: e.tensor_copy(
                        stg[0:1, 2048:3072], yscf[32:33, 1024:2048]), chain=ve)
                    p.emit("vector", lambda e: e.tensor_copy(
                        stg[0:1, 3072:4096], scrf[32:33, 1024:2048]), chain=ve)
                    ROWS_VE = ve.n
                    for ch in range(2):
                        p.emit("tensor", lambda e, ch=ch, grow=grow: e.matmul(
                            ps2[2][:, ch * 512:(ch + 1) * 512],
                            grow[0:1, :],
                            stg[0:1, ch * 512:(ch + 1) * 512],
                            start=True, stop=False),
                            waits=[(ve, ROWS_VE)] if ch == 0 else [],
                            inc=None)
                        p.emit("tensor", lambda e, ch=ch, grow2=grow2: e.matmul(
                            ps2[2][:, ch * 512:(ch + 1) * 512],
                            grow2[0:1, :],
                            stg[0:1, 1024 + ch * 512:1024 + (ch + 1) * 512],
                            start=False, stop=True), inc=None)
                        p.emit("tensor", lambda e, ch=ch, grow=grow: e.matmul(
                            ps2[3][:, ch * 512:(ch + 1) * 512],
                            grow[0:1, :],
                            stg[0:1, 2048 + ch * 512:2048 + (ch + 1) * 512],
                            start=True, stop=False), inc=None)
                        p.emit("tensor", lambda e, ch=ch, grow2=grow2: e.matmul(
                            ps2[3][:, ch * 512:(ch + 1) * 512],
                            grow2[0:1, :],
                            stg[0:1, 3072 + ch * 512:3072 + (ch + 1) * 512],
                            start=False, stop=False), inc=None)
                        p.emit("tensor", lambda e, ch=ch, nbf=nbf: e.matmul(
                            ps2[3][:, ch * 512:(ch + 1) * 512],
                            nbf[0:1, :], onesrow[0:1, 0:512],
                            start=False, stop=True),
                            inc=pe if ch == 1 else None)
                    bc_pe = pe.n
                    p.emit("vector", lambda e, z2=z2: e.tensor_mul(
                        probs[:, 2048:3072], buf_fn(z2), ps2[2][:, 0:1024]),
                        waits=[(pe, bc_pe), (ve, ve.n), (se, se.n)], inc=ve)
                    p.emit("vector", lambda e, z2=z2: e.tensor_sub(
                        buf_fn(z2), probs[:, 2048:3072], ps2[3][:, 0:1024]),
                        inc=ve)
                    st["sq_pe"] = pe.n

            def ktransform(f):
                if li == 1:
                    qkln(lambda z: probs[:, z:z + 1024], growk, growk2, nbkf)
                for h2 in range(2):
                    z = h2 * 1024
                    if li == 0:
                        for n2 in range(2):
                            p.emit("tensor", lambda e, z=z, n2=n2: e.matmul(
                                ps2[2][:, n2 * 512:(n2 + 1) * 512], pswap[:],
                                probs[:, z + n2 * 512:z + (n2 + 1) * 512],
                                start=True, stop=True),
                                waits=[(ve, ve.n)], inc=pe if n2 == 1 else None)
                        swp = pe.n
                        p.emit("vector", lambda e, z=z: e.tensor_mul(
                            probs[:, 2048:3072], probs[:, z:z + 1024],
                            scrf[:, 0:1024] if z == 0 else scrf[:, 1024:2048]),
                            waits=[(dx, RT_DX), (pe, st["sq_pe"])], inc=ve)
                        p.emit("vector", lambda e, z=z: e.tensor_mul(
                            probs[:, z:z + 1024],
                            ps2[2][:].bitcast(F32)[:, 0:1024],
                            yscf[:, 0:1024] if z == 0 else yscf[:, 1024:2048]),
                            waits=[(pe, swp)], inc=ve)
                        p.emit("vector", lambda e, z=z: e.tensor_add(
                            probs[:, z:z + 1024], probs[:, 2048:3072],
                            probs[:, z:z + 1024]), inc=ve)
                    kr_ve = ve.n
                    p.emit("scalar", lambda e, z=z: e.activation(
                        probs[:, 2048:3072], probs[:, z:z + 1024], AF.Square),
                        waits=[(ve, kr_ve), (se, se.n), (pe, st["sq_pe"])], inc=se)
                    normfold(1, f, [(se, se.n)])
                    st["sq_pe"] = pe.n
                    p.emit("sync", lambda e, f=f, z=z, h2=h2: e.dma_start(
                        flatap(cc_i, f * 128 * T + h2 * 1024,
                               [(T, 128), (1, 1024)]),
                        probs[:, z:z + 1024]),
                        waits=[(ve, kr_ve)], inc=dcc, inc_by=16)
                st["kship"] = dcc.n
                st["ktile_free"] = ve.n

            def evac_k(f, n, b, pet):
                w = [(pe, pet)]
                if n == 0:
                    w += [(ve, st["ktile_free"]), (dcc, st["kship"])]
                p.emit("vector", lambda e, n=n, b=b: e.tensor_copy(
                    probs[:, n * 512:(n + 1) * 512], pbank(b)), waits=w, inc=ve)
                if n == 3:
                    ktransform(f)
                return (ve, ve.n)
            proj_fm(wB, wk_t, HFM_VE, evac_k)

            for f in range(NF):
                if li == 1:
                    qkln(lambda z, f=f: R2[:, f * T + z:f * T + z + 1024],
                         growq, growq2, nbqf)
                for h2 in range(2):
                    z = h2 * 1024
                    base = f * T + h2 * 1024
                    if li == 0:
                        for n2 in range(2):
                            p.emit("tensor", lambda e, base=base, n2=n2: e.matmul(
                                ps2[2][:, n2 * 512:(n2 + 1) * 512], pswap[:],
                                R2[:, base + n2 * 512:base + (n2 + 1) * 512],
                                start=True, stop=True),
                                waits=[(ve, ve.n)], inc=pe if n2 == 1 else None)
                        swp = pe.n
                        p.emit("vector", lambda e, base=base, z=z: e.tensor_mul(
                            probs[:, 0:1024], R2[:, base:base + 1024],
                            scrf[:, 0:1024] if z == 0 else scrf[:, 1024:2048]),
                            waits=[(dx, RT_DX), (ve, st["ktile_free"]),
                                   (dcc, st["kship"]), (pe, st["sq_pe"])], inc=ve)
                        p.emit("vector", lambda e, z=z: e.tensor_mul(
                            probs[:, 1024:2048],
                            ps2[2][:].bitcast(F32)[:, 0:1024],
                            yscf[:, 0:1024] if z == 0 else yscf[:, 1024:2048]),
                            waits=[(pe, swp)], inc=ve)
                        p.emit("vector", lambda e, base=base: e.tensor_add(
                            R2[:, base:base + 1024], probs[:, 0:1024],
                            probs[:, 1024:2048]), inc=ve)
                    p.emit("scalar", lambda e, base=base: e.activation(
                        probs[:, 2048:3072], R2[:, base:base + 1024], AF.Square),
                        waits=[(ve, ve.n), (se, se.n), (pe, st["sq_pe"])], inc=se)
                    normfold(0, f, [(se, se.n)])
                    st["sq_pe"] = pe.n
            Q_SE, Q_VE = se.n, ve.n

            st["vship"] = dcc.n

            def vtransform(f):
                vev = ve.n
                for g2 in range(2):
                    for jj in range(8):
                        w = [(ve, vev)] if jj == 0 else []
                        if jj == 0 and g2 == 0:
                            w.append((pe, st["sq_pe"]))
                        p.emit("tensor", lambda e, g2=g2, jj=jj: e.transpose(
                            ps2[2][:].bitcast(BF16)[:, jj * 128:(jj + 1) * 128],
                            probs[:, g2 * 1024 + jj * 128:g2 * 1024 + (jj + 1) * 128],
                            ident[:]), waits=w, inc=pe if jj == 7 else None)
                    tp_pe = pe.n
                    p.emit("vector", lambda e: e.tensor_copy(
                        probs[:, 2048:3072], ps2[2][:].bitcast(BF16)[:, 0:1024]),
                        waits=[(pe, tp_pe), (dcc, st["vship"])], inc=ve)
                    evv = ve.n
                    p.emit("sync", lambda e, f=f, g2=g2: e.dma_start(
                        flatap(cc_i, VOFF + f * T * 128 + g2 * 1024 * 128,
                               [(128, 128), (16384, 8), (1, 128)]),
                        probs[:, 2048:3072].rearrange("p (a d) -> p a d", d=128)),
                        waits=[(ve, evv)], inc=dcc, inc_by=16)
                    st["vship"] = dcc.n
                    vev = ve.n
                st["sq_pe"] = pe.n

            def evac_v(f, n, b, pet):
                w = [(pe, pet)]
                if n == 0:
                    w.append((dcc, st["vship"]))
                if f == 0 and n == 0:
                    w += [(ve, Q_VE), (se, Q_SE)]
                p.emit("vector", lambda e, n=n, b=b: e.tensor_copy(
                    probs[:, n * 512:(n + 1) * 512], pbank(b)), waits=w, inc=ve)
                if n == 3:
                    vtransform(f)
                return (ve, ve.n)
            proj_fm(wA, wv_t, HFM_VE, evac_v)

            p.emit("sync", lambda e: e.dma_start(
                flatap(cc_i, MOFF, [(16, 2), (1, 16)]), Mq(1).bitcast(BF16)),
                waits=[(ve, ve.n)], inc=dcc, inc_by=16)
            CC_W = dcc.n
            if stop < 3:
                return 0, 0, 0

            p.emit("gpsimd", lambda e: e.collective_compute(
                "AllGather", ALU.bypass,
                replica_groups=[[0, 1], [2, 3], [4, 5], [6, 7]],
                ins=[cc_i.ap().opt()], outs=[cc_o.ap().opt()]),
                waits=[(dcc, CC_W)], inc=dcc, inc_by=1)
            CC_DONE = dcc.n

            for r in range(2):
                p.emit("sync", lambda e, r=r: e.dma_start(
                    Mq(2 + r).bitcast(BF16),
                    flatap(cc_o, r * CCE + MOFF, [(16, 2), (1, 16)])),
                    waits=[(dcc, CC_DONE)] if r == 0 else [], inc=dx, inc_by=16)
            KM_DX = dx.n
            p.emit("vector", lambda e: e.tensor_tensor(Mq(4), Mq(2), Mq(3), ALU.max),
                   waits=[(dx, KM_DX), (ve, ve.n)], chain=ve)
            p.emit("vector", lambda e: e.tensor_mul(Mq(5), Mq(0), Mq(4)), chain=ve)
            p.emit("sync", lambda e: e.dma_start(Mr1(48, 64), Mq(5)),
                   waits=[(ve, ve.n)], inc=dx, inc_by=16)
            p.emit("vector", lambda e: e.tensor_tensor(
                Mr1(32, 40), Mr1(48, 56), Mr1(56, 64), ALU.max),
                waits=[(dx, dx.n)], chain=ve)
            p.emit("scalar", lambda e: e.activation(Mr1(32, 40), Mr1(32, 40), AF.Ln),
                   waits=[(ve, ve.n)], chain=se)
            p.emit("scalar", lambda e: e.activation(
                Mr1(32, 40), Mr1(32, 40), AF.Exp, scale=0.5), chain=se)
            p.emit("vector", lambda e: e.tensor_scalar(
                Mr1(32, 40), Mr1(32, 40), -att_scale, BSHIFT, ALU.mult, ALU.add),
                waits=[(se, se.n)], chain=ve)
            p.emit("vector", lambda e: e.tensor_copy(probs[0:1, 0:8], Mr1(32, 40)),
                   chain=ve)
            BR_VE = ve.n
            p.emit("tensor", lambda e: e.matmul(
                pbank(0)[:, 0:8], onesrow[0:1, 0:128], probs[0:1, 0:8],
                start=True, stop=True), waits=[(ve, BR_VE)], inc=pe)
            for wbuf in (wA, wB):
                p.emit("vector", lambda e, wbuf=wbuf: e.memset(
                    flatap(wbuf, 4096 + 64, [(wbuf.shape[1], 128), (130, 32), (1, 1)]),
                    1.0), waits=[(pe, pe.n), (dcc, CC_W)], chain=ve)
                p.emit("vector", lambda e, wbuf=wbuf: e.memset(
                    flatap(wbuf, 4096 + 129, [(wbuf.shape[1], 128), (130, 32), (1, 1)]),
                    1.0), chain=ve)
            p.emit("vector", lambda e: e.tensor_copy(
                biascol[:, 0:8], pbank(0)[:, 0:8]), waits=[(pe, pe.n)], inc=ve)
            BIAS_VE = ve.n
            if stop < 4:
                return 0, 0, 0

            exp_t = {}
            pv_t = {}
            pair_done = {}
            prev_chunk_gate = [(ve, BIAS_VE)]
            LP, LC, LT, LE = lim
            for pr in range(min(NP, LP)):
                wbuf = wA if pr % 2 == 0 else wB
                W0 = wbuf.shape[1]
                w0 = [(dcc, CC_DONE)]
                if pr >= 2:
                    w0.append((pe, pair_done[pr - 2]))
                for r in range(2):
                    p.emit("sync", lambda e, r=r, pr=pr, wbuf=wbuf: e.dma_start(
                        wbuf[:, r * 2048:(r + 1) * 2048],
                        flatap(cc_o, r * CCE + pr * 128 * T,
                               [(T, 128), (1, 2048)])),
                        waits=w0 if r == 0 else [], inc=dcc, inc_by=16)
                    for s_ in range(2):
                        p.emit("sync", lambda e, r=r, s_=s_, pr=pr, wbuf=wbuf,
                               W0=W0: e.dma_start(
                            flatap(wbuf, 4096 + r * 130 + s_ * 65,
                                   [(W0, 128), (260, 16), (1, 64)]),
                            flatap(cc_o, r * CCE + VOFF + pr * T * 128 + s_ * 64,
                                   [(128, 128), (16384, 16), (1, 64)])),
                            inc=dcc, inc_by=16)
                load_t = dcc.n

                def emit_pv(pr, c, t, sl, qoff, gate, wbuf, nkt):
                    for hd in range(2):
                        p.emit("tensor", lambda e, qoff=qoff, sl=sl, hd=hd,
                               t=t, wbuf=wbuf, nkt=nkt: e.matmul(
                            ps2[3][0:65, hd * 512 + qoff:(hd + 1) * 512],
                            wbuf[:, 4096 + t * 130 + hd * 65:
                                 4096 + t * 130 + hd * 65 + 65],
                            probs[:, sl * 1024 + hd * 512 + qoff:
                                  sl * 1024 + (hd + 1) * 512],
                            start=(t == 0), stop=(t == nkt - 1)),
                            waits=[gate] if hd == 0 else [],
                            inc=pe if hd == 1 else None)
                    pv_t[(pr, c, t)] = pe.n

                pend = None
                for c in range(min(4, LC)):
                    nkt = 8 * c + 8
                    for t in range(min(nkt, LT)):
                        sl = t % 3
                        jmin = max(4 * c, -(-(t - 1) // 2))
                        qoff = (jmin - 4 * c) * 128
                        qbase = pr * T + c * 512
                        tcol = (t % 2) * 2048 + (t // 2) * 128
                        w = [(dcc, load_t)]
                        if (pr, c, t - 3) in exp_t:
                            w.append((se, exp_t[(pr, c, t - 3)]))
                        if t == 0:
                            w += prev_chunk_gate
                        p.emit("tensor", lambda e, sl=sl, qoff=qoff, qbase=qbase,
                               tcol=tcol, wbuf=wbuf: e.matmul(
                            ps2[sl][:, qoff:512],
                            wbuf[0:64, tcol:tcol + 128],
                            R2[0:64, qbase + qoff:qbase + 512],
                            start=True, stop=True, tile_position=(0, 0)),
                            waits=w)
                        p.emit("tensor", lambda e, sl=sl, qoff=qoff, qbase=qbase,
                               tcol=tcol, wbuf=wbuf: e.matmul(
                            ps2[sl][:, 512 + qoff:1024],
                            wbuf[64:128, tcol:tcol + 128],
                            R2[64:128, qbase + qoff:qbase + 512],
                            start=True, stop=True, tile_position=(64, 0)),
                            inc=pe)
                        qk_pe = pe.n
                        if pend is not None:
                            emit_pv(*pend)
                            pend = None
                        wexp = [(pe, qk_pe)]
                        if (pr, c, t - 3) in pv_t:
                            wexp.append((pe, pv_t[(pr, c, t - 3)]))
                        N_t = 512 - qoff
                        p.emit("scalar", lambda e, sl=sl, qoff=qoff, pr=pr,
                               N_t=N_t: e.activation(
                            flatap(probs, sl * 1024 + qoff,
                                   [(3072, 128), (512, 2), (1, N_t)]),
                            flatap(ps2[sl], qoff, [(1024, 128), (512, 2), (1, N_t)]),
                            AF.Exp, bias=biascol[:, pr:pr + 1], scale=att_scale),
                            waits=wexp, inc=se)
                        exp_t[(pr, c, t)] = se.n
                        gate = (se, se.n)
                        jj = t // 2
                        if 4 * c <= jj < 4 * c + 4:
                            sk = t % 2
                            p.emit("vector", lambda e, sl=sl, qoff=qoff, sk=sk:
                                   e.tensor_mul(
                                flatap(probs, sl * 1024 + qoff,
                                       [(3072, 128), (512, 2), (1, 128)]),
                                flatap(probs, sl * 1024 + qoff,
                                       [(3072, 128), (512, 2), (1, 128)]),
                                flatap(mask, sk * 256,
                                       [(512, 128), (128, 2), (1, 128)])),
                                waits=[(se, se.n)], inc=ve)
                            gate = (ve, ve.n)
                        pend = (pr, c, t, sl, qoff, gate, wbuf, nkt)
                    if pend is not None:
                        emit_pv(*pend)
                        pend = None
                    if LE == 1:
                        p.emit("vector", lambda e: e.tensor_copy(
                            R2[:].bitcast(F32)[:, 0:512], ps2[3][:, 0:512]),
                            waits=[(pe, pv_t[(pr, c, nkt - 1)]),
                                   (se, exp_t[(pr, c, nkt - 1)]), (ve, ve.n)],
                            chain=ve)
                        p.emit("vector", lambda e: e.tensor_copy(
                            R2[:].bitcast(F32)[:, 512:1024], ps2[3][:, 512:1024]),
                            chain=ve)
                        prev_chunk_gate = [(ve, ve.n)]
                        continue
                    if LE == 0 or LT < nkt:
                        prev_chunk_gate = [(ve, ve.n)]
                        continue
                    PV_GATE = exp_t[(pr, c, nkt - 1)]
                    PV_END = pv_t[(pr, c, nkt - 1)]
                    p.emit("vector", lambda e: e.reciprocal(
                        rrowf[0:1, 0:512], ps2[3][64:65, 0:512]),
                        waits=[(pe, PV_END), (ve, ve.n)], chain=ve)
                    p.emit("vector", lambda e: e.reciprocal(
                        rrowf[0:1, 512:1024], ps2[3][64:65, 512:1024]), chain=ve)
                    p.emit("vector", lambda e: e.tensor_copy(
                        probs[0:1, 0:1024], rrowf[0:1, 0:1024]), chain=ve)
                    RB_VE = ve.n
                    if LE == 2:
                        prev_chunk_gate = [(ve, ve.n)]
                        continue
                    p.emit("tensor", lambda e: e.matmul(
                        ps2[0][:, 0:512], e1[0:1, :], probs[0:1, 0:512],
                        start=True, stop=False),
                        waits=[(ve, RB_VE), (se, PV_GATE)])
                    p.emit("tensor", lambda e: e.matmul(
                        ps2[0][:, 0:512], e2[0:1, :], probs[0:1, 512:1024],
                        start=False, stop=True), inc=pe)
                    RB_PE = pe.n
                    if LE == 3:
                        prev_chunk_gate = [(ve, ve.n)]
                        continue
                    p.emit("vector", lambda e: e.tensor_copy(
                        probs[:, 1024:1536], ps2[0][:, 0:512]),
                        waits=[(pe, RB_PE)], inc=ve)
                    RB2_VE = ve.n
                    if LE == 4:
                        prev_chunk_gate = [(ve, ve.n)]
                        continue
                    for hd in range(2):
                        p.emit("vector", lambda e, hd=hd, pr=pr, c=c: e.tensor_mul(
                            R2[hd * 64:(hd + 1) * 64,
                               pr * T + c * 512:pr * T + (c + 1) * 512],
                            ps2[3][0:64, hd * 512:(hd + 1) * 512],
                            probs[hd * 64:(hd + 1) * 64, 1024:1536]),
                            waits=[(ve, RB2_VE)] if hd == 0 else [], inc=ve)
                    prev_chunk_gate = [(ve, ve.n)]
                pair_done[pr] = pe.n
            return ve.n, pe.n, se.n

        attn_sublayer(0, wqT, wkT, wvT, 0.125)

        wo_t = load_slab(wB, woT, waits=[(pe, pe.n), (dcc, dcc.n)])
        s5p = S5_PARTS
        for c in (range(4) if stop >= 5 else []):
            tp_prev = []
            for f in range(NF):
                b = f % 4
                for a in range(NF):
                    w = []
                    if a == 0:
                        w = [(dw, wo_t), (ve, ve.n)]
                    p.emit("tensor", lambda e, f=f, c=c, a=a, b=b: e.matmul(
                        pbank(b),
                        wB[:, a * 1024 + f * 128:a * 1024 + (f + 1) * 128],
                        R2[:, a * T + c * 512:a * T + (c + 1) * 512],
                        start=(a == 0), stop=(a == NF - 1)),
                        waits=w, inc=pe if a == NF - 1 else None)
                wo_pe = pe.n
                p.emit("vector", lambda e, b=b: e.tensor_copy(
                    stg[:, 0:512], pbank(b)), waits=[(pe, wo_pe)] + tp_prev, inc=ve)
                ev_ve = ve.n
                for jj in range(4):
                    p.emit("tensor", lambda e, jj=jj: e.transpose(
                        psb(4)[:, jj * 128:(jj + 1) * 128],
                        stg[:, jj * 128:(jj + 1) * 128], ident[:]),
                        waits=[(ve, ev_ve)] if jj == 0 else [],
                        inc=pe if jj == 3 else None)
                tp_pe = pe.n
                tp_prev = [(pe, tp_pe)]
                p.emit("vector", lambda e, f=f: e.tensor_copy(
                    flatap(ysc, f * 128, [(4096, 128), (1024, 4), (1, 128)]),
                    psb(4)[:, 0:512].rearrange("p (a d) -> p a d", d=128)),
                    waits=[(pe, tp_pe)], inc=ve)
            YT_VE = ve.n
            if s5p < 2:
                continue
            for jj in range(4):
                j = c * 4 + jj
                p.emit("vector", lambda e, j=j, jj=jj: e.scalar_tensor_tensor(
                    ysc[:, jj * 1024:(jj + 1) * 1024],
                    x_res[:, j * D:(j + 1) * D], vslot(1)[:, j:j + 1],
                    ysc[:, jj * 1024:(jj + 1) * 1024], ALU.mult, ALU.add),
                    waits=[(ve, YT_VE)] if jj == 0 else [], inc=ve)
            T_VE = ve.n
            if s5p < 3:
                continue
            for jj in range(4):
                j = c * 4 + jj
                wsq = [(ve, T_VE), (se, se.n)] if jj < 2 else [(ve, ve.n)]
                p.emit("scalar", lambda e, j=j, jj=jj: e.activation(
                    scrf[:, (jj % 2) * 1024:(jj % 2) * 1024 + 1024],
                    ysc[:, jj * 1024:(jj + 1) * 1024], AF.Square,
                    bias=vslot(13)[:, j:j + 1]), waits=wsq, inc=se)
                p.emit("vector", lambda e, j=j, jj=jj: e.reduce_sum(
                    vslot(4)[:, j:j + 1],
                    scrf[:, (jj % 2) * 1024:(jj % 2) * 1024 + 1024].rearrange(
                        "p (a d) -> p a d", a=1), axis=AX.X),
                    waits=[(se, se.n)], chain=ve)
            c4 = vslot(5)[:, c * 4:c * 4 + 4]
            s4 = vslot(4)[:, c * 4:c * 4 + 4]
            p.emit("vector", lambda e, c4=c4, s4=s4: e.tensor_scalar(
                c4, s4, 1.0 / D, EPS_RMS, ALU.mult, ALU.add), chain=ve)
            p.emit("scalar", lambda e, c4=c4: e.activation(c4, c4, AF.Ln),
                   waits=[(ve, ve.n)], chain=se)
            p.emit("scalar", lambda e, c4=c4: e.activation(
                c4, c4, AF.Exp, scale=-0.5), chain=se)
            n4 = vslot(6)[:, c * 4:c * 4 + 4]
            m4 = vslot(3)[:, c * 4:c * 4 + 4]
            p.emit("vector", lambda e, n4=n4, m4=m4, c4=c4:
                   e.scalar_tensor_tensor(n4, m4, -1.0, c4, ALU.mult, ALU.mult),
                   waits=[(se, se.n)], chain=ve)
            apply_transpose(lambda j, c=c: ysc[:, (j - c * 4) * 1024:
                                               (j - c * 4 + 1) * 1024],
                            5, 6, R1, fold_cols=ln_cols[:, NF:2 * NF], jgs=(c,),
                            wait0=[(ve, ve.n), (se, se.n)])
            if s5p < 4:
                continue
            for jj in range(4):
                j = c * 4 + jj
                p.emit("vector", lambda e, j=j, jj=jj: e.tensor_add(
                    x_res[:, j * D:(j + 1) * D], x_res[:, j * D:(j + 1) * D],
                    ysc[:, jj * 1024:(jj + 1) * 1024]),
                    waits=[(ve, ve.n)] if jj == 0 else [], inc=ve)
                p.emit("vector", lambda e, j=j: e.tensor_scalar(
                    x_res[:, j * D:(j + 1) * D], x_res[:, j * D:(j + 1) * D],
                    vslot(3)[:, j:j + 1], None, ALU.subtract), inc=ve)
        H2_VE, H2_SE, H2_PE = ve.n, se.n, pe.n

        for srcT, dstbf, OW in ((w1T, w1bf, DFF), (w3T, w3bf, DFF), (w2T, w2bf, D)):
            ng = (D * DFF) // (128 * OW)
            p.emit("gpsimd", lambda e, s=srcT, d=dstbf, OW=OW, ng=ng: e.dma_start(
                flatap(d, 0, [(OW, 128), (OW * 128, ng), (1, OW)]),
                s.rearrange("(g p) o -> p g o", p=128)),
                inc=dw, inc_by=16)
        PRECAST = dw.n

        for c in (range(4) if stop >= 6 else []):
            wslab_t = {}

            def emit_ffn_load(gh, c, wv):
                zA = (gh % 2) * 4096
                p.emit("gpsimd", lambda e, gh=gh, zA=zA: e.dma_start(
                    wA[:, zA:zA + 4096].rearrange("p (a o) -> p a o", o=512),
                    flatap(w1bf, gh * 512, [(DFF, 128), (DFF * 128, 8), (1, 512)])),
                    waits=wv, inc=dw, inc_by=16)
                p.emit("gpsimd", lambda e, gh=gh, zA=zA: e.dma_start(
                    wB[:, zA:zA + 4096].rearrange("p (a o) -> p a o", o=512),
                    flatap(w3bf, gh * 512, [(DFF, 128), (DFF * 128, 8), (1, 512)])),
                    inc=dw, inc_by=16)
                wslab_t[gh] = dw.n

            emit_ffn_load(0, c, [(dw, PRECAST), (pe, pe.n)])
            emit_ffn_load(1, c, [])
            for gh in range(8):
                zA = (gh % 2) * 4096
                for dt in range(4):
                    dtg = gh * 4 + dt
                    b1 = dtg % 2
                    b3 = 2 + dtg % 2
                    for a in range(NF):
                        w = [] if a else [(dw, wslab_t[gh]),
                                          (se, se.n), (ve, ve.n)]
                        p.emit("tensor", lambda e, zA=zA, dt=dt, a=a, b1=b1, c=c:
                               e.matmul(
                            pbank(b1),
                            wA[:, zA + a * 512 + dt * 128:zA + a * 512 + (dt + 1) * 128],
                            R1[:, a * T + c * 512:a * T + (c + 1) * 512],
                            start=(a == 0), stop=(a == NF - 1)),
                            waits=w, inc=pe if a == NF - 1 else None)
                    p1_pe = pe.n
                    for a in range(NF):
                        p.emit("tensor", lambda e, zA=zA, dt=dt, a=a, b3=b3, c=c:
                               e.matmul(
                            pbank(b3),
                            wB[:, zA + a * 512 + dt * 128:zA + a * 512 + (dt + 1) * 128],
                            R1[:, a * T + c * 512:a * T + (c + 1) * 512],
                            start=(a == 0), stop=(a == NF - 1)),
                            waits=[(ve, ve.n)] if a == 0 else [],
                            inc=pe if a == NF - 1 else None)
                    p3_pe = pe.n
                    p.emit("scalar", lambda e, b1=b1, dtg=dtg: e.activation(
                        probs[:, 0:512] if dtg % 2 == 0 else probs[:, 512:1024],
                        pbank(b1), AF.Silu),
                        waits=[(pe, p1_pe), (se, se.n), (ve, ve.n)], inc=se)
                    p.emit("vector", lambda e, b3=b3, dtg=dtg: e.tensor_mul(
                        R2[:, dtg * 512:(dtg + 1) * 512],
                        probs[:, 0:512] if dtg % 2 == 0 else probs[:, 512:1024],
                        pbank(b3)),
                        waits=[(pe, p3_pe), (se, se.n)], inc=ve)
                if gh + 2 < 8:
                    emit_ffn_load(gh + 2, c, [(pe, pe.n)])
            P_VE = ve.n
            w2_t = {}

            def emit_w2_load(f, wv):
                zB = (f % 2) * 4096
                p.emit("gpsimd", lambda e, f=f, zB=zB: e.dma_start(
                    wA[:, zB:zB + 4096].rearrange("p (a o) -> p a o", o=128),
                    flatap(w2bf, f * 128, [(D, 128), (D * 128, 32), (1, 128)])),
                    waits=wv, inc=dw, inc_by=16)
                w2_t[f] = dw.n

            emit_w2_load(0, [(dw, PRECAST), (ve, P_VE), (pe, pe.n)])
            emit_w2_load(1, [])
            tp_prev = []
            bank_guard = {0: None, 1: None}
            for f in range(NF):
                zB = (f % 2) * 4096
                b = f % 2
                for dt in range(32):
                    w = [] if dt else ([(dw, w2_t[f]), (ve, P_VE)] +
                                       ([bank_guard[b]] if bank_guard[b] else []))
                    p.emit("tensor", lambda e, zB=zB, dt=dt, f=f, b=b: e.matmul(
                        pbank(b), wA[:, zB + dt * 128:zB + (dt + 1) * 128],
                        R2[:, dt * 512:(dt + 1) * 512],
                        start=(dt == 0), stop=(dt == 31)),
                        waits=w, inc=pe if dt == 31 else None)
                ff_pe = pe.n
                p.emit("vector", lambda e, b=b: e.tensor_copy(
                    stg[:, 2048:2560], pbank(b)),
                    waits=[(pe, ff_pe)] + tp_prev, inc=ve)
                bank_guard[b] = (ve, ve.n)
                ev_ve = ve.n
                for jj in range(4):
                    p.emit("tensor", lambda e, jj=jj: e.transpose(
                        psb(4)[:, jj * 128:(jj + 1) * 128],
                        stg[:, 2048 + jj * 128:2048 + (jj + 1) * 128], ident[:]),
                        waits=[(ve, ev_ve)] if jj == 0 else [],
                        inc=pe if jj == 3 else None)
                tp_pe = pe.n
                tp_prev = [(pe, tp_pe)]
                p.emit("vector", lambda e, f=f, c=c: e.tensor_add(
                    flatap(x_res, c * 4 * D + f * 128, [(NT * D, 128), (D, 4), (1, 128)]),
                    flatap(x_res, c * 4 * D + f * 128, [(NT * D, 128), (D, 4), (1, 128)]),
                    psb(4)[:, 0:512].rearrange("p (a d) -> p a d", d=128)),
                    waits=[(pe, tp_pe), (ve, ve.n)], inc=ve)
                if f + 2 < NF:
                    emit_w2_load(f + 2, [(pe, pe.n)])
        S1_VE, S1_SE, S1_PE = ve.n, se.n, pe.n

        def ffn_q(first):
            ln_stats(7, 8)
            mx = vslot(9); mn = vslot(10); av = vslot(11); cz = vslot(12)
            nbz = vslot(14)
            p.emit("vector", lambda e: e.reduce_max(
                mx, x_res[:].rearrange("p (j d) -> p j d", d=D), axis=AX.X),
                chain=ve)
            p.emit("vector", lambda e: e.tensor_reduce(
                mn, x_res[:].rearrange("p (j d) -> p j d", d=D), axis=AX.X,
                op=ALU.min), chain=ve)
            p.emit("vector", lambda e: e.tensor_sub(mx, mx, vslot(7)), chain=ve)
            p.emit("vector", lambda e: e.tensor_sub(mn, vslot(7), mn), chain=ve)
            p.emit("vector", lambda e: e.tensor_tensor(av, mx, mn, ALU.max), chain=ve)
            p.emit("vector", lambda e: e.tensor_mul(av, av, vslot(8)),
                   waits=[(se, se.n)], chain=ve)
            p.emit("vector", lambda e: e.tensor_scalar_add(av, av, EPS_Q), chain=ve)
            p.emit("vector", lambda e: e.reciprocal(cz, av), chain=ve)
            p.emit("vector", lambda e: e.tensor_scalar_mul(cz, cz, 127.0), chain=ve)
            p.emit("vector", lambda e: e.tensor_mul(cz, cz, vslot(8)), chain=ve)
            p.emit("vector", lambda e: e.scalar_tensor_tensor(
                nbz, vslot(7), -1.0, cz, ALU.mult, ALU.mult), chain=ve)
            LNQ_VE, LNQ_SE = ve.n, se.n

            if first:
                R2f = R2[:].bitcast(F32)
                p.emit("sync", lambda e: e.dma_start(
                    R2f[:, 0:8192].rearrange("p (a o) -> p a o", o=D),
                    fwT.rearrange("(a p) o -> p a o", p=128)),
                    waits=[(pe, pe.n), (ve, ve.n)], inc=dx, inc_by=16)
                FW_DX = dx.n
                p.emit("vector", lambda e: e.reduce_sum(
                    colsf[:, 4:5], R2f[:, 0:8192].rearrange(
                        "p (a o) -> p a o", a=1), axis=AX.X,
                    apply_absolute_value=True),
                    waits=[(dx, FW_DX)], chain=ve)
                p.emit("sync", lambda e: e.dma_start(
                    scrf[0:1, 0:128], colsf[:, 4:5]),
                    waits=[(ve, ve.n)], inc=dx, inc_by=16)
                p.emit("vector", lambda e: e.reduce_sum(
                    Mr1(32, 33), scrf[0:1, 0:128].rearrange("p (a o) -> p a o", a=1),
                    axis=AX.X), waits=[(dx, dx.n)], chain=ve)
                p.emit("vector", lambda e: e.tensor_scalar(
                    Mr1(32, 33), Mr1(32, 33), 1.0 / (D * D), EPS_Q,
                    ALU.mult, ALU.add), chain=ve)
                p.emit("vector", lambda e: e.reciprocal(
                    Mr1(33, 34), Mr1(32, 33)), chain=ve)
                p.emit("vector", lambda e: e.tensor_scalar_mul(
                    Mr1(34, 35), Mr1(32, 33), 1.0 / 127.0), chain=ve)
                WS_VE = ve.n
                p.emit("tensor", lambda e: e.matmul(
                    pbank(0)[:, 0:2], onesf32[0:1, :], Mr1(33, 35),
                    start=True, stop=True), waits=[(ve, WS_VE), (pe, pe.n)], inc=pe)
                p.emit("vector", lambda e: e.tensor_copy(
                    colsf[:, 5:7], pbank(0)[:, 0:2]), waits=[(pe, pe.n)], chain=ve)
                for ch in range(4):
                    z = ch * 2048
                    p.emit("vector", lambda e, z=z: e.tensor_scalar_mul(
                        R2f[:, z:z + 2048], R2f[:, z:z + 2048], colsf[:, 5:6]),
                        chain=ve)
                    p.emit("vector", lambda e, z=z: e.tensor_copy(
                        iscr[:, 0:2048], R2f[:, z:z + 2048]), chain=ve)
                    p.emit("vector", lambda e, z=z: e.tensor_copy(
                        R2f[:, z:z + 2048], iscr[:, 0:2048]), chain=ve)
                    p.emit("vector", lambda e, z=z: e.tensor_scalar(
                        R2f[:, z:z + 2048], R2f[:, z:z + 2048], 1.0, -1.0,
                        ALU.min, ALU.max), chain=ve)
                    p.emit("vector", lambda e, z=z: e.tensor_copy(
                        wA[:, z:z + 2048], R2f[:, z:z + 2048]), chain=ve)
                WQ_VE = ve.n
                p.emit("sync", lambda e: e.dma_start(wq_park.ap(), wA[:, 0:8192]),
                       waits=[(ve, WQ_VE)], inc=dcc, inc_by=16)
                wtgt, wsem = WQ_VE, ve
            else:
                p.emit("gpsimd", lambda e: e.dma_start(
                    wA[:, 0:8192], wq_park.ap()),
                    waits=[(pe, pe.n)], inc=dw, inc_by=16)
                wtgt, wsem = dw.n, dw

            prev_evac = []
            pe_tp = pe.n
            for jg in range(4):
                for jj in range(4):
                    j = jg * 4 + jj
                    w = [(ve, ve.n), (se, se.n)] if (jg == 0 and jj == 0) else []
                    if jj == 0 and jg > 0:
                        w.append((pe, pe_tp))
                    p.emit("vector", lambda e, j=j, jj=jj: e.tensor_scalar(
                        scrf[:, 0:1024], x_res[:, j * D:(j + 1) * D],
                        cz[:, j:j + 1], nbz[:, j:j + 1], ALU.mult, ALU.add),
                        waits=w, chain=ve)
                    p.emit("vector", lambda e, jj=jj: e.tensor_copy(
                        iscr[:, 1024:2048], scrf[:, 0:1024]), chain=ve)
                    p.emit("vector", lambda e, jj=jj: e.tensor_copy(
                        ysc[:, jj * 1024:(jj + 1) * 1024], iscr[:, 1024:2048]),
                        chain=ve)
                ve_app = ve.n
                for f in range(NF):
                    for jj in range(4):
                        w = []
                        if f == 0 and jj == 0:
                            w = [(ve, ve_app)] + prev_evac
                        p.emit("tensor", lambda e, f=f, jj=jj: e.transpose(
                            psb(f)[:, jj * 128:(jj + 1) * 128],
                            ysc[:, jj * 1024 + f * 128:jj * 1024 + (f + 1) * 128],
                            ident[:]),
                            waits=w, inc=pe if (f == NF - 1 and jj == 3) else None)
                pe_tp = pe.n
                prev_evac = []
                for f in range(NF):
                    p.emit("vector", lambda e, f=f, jg=jg: e.tensor_copy(
                        R1[:, f * T + jg * 512:f * T + (jg + 1) * 512],
                        psb(f)[:, 0:512]), waits=[(pe, pe_tp)], inc=ve)
                    prev_evac = [(ve, ve.n)]
            Z_VE = ve.n

            guards = [None] * 4
            tp_prev = []
            for f in range(NF):
                for n in range(4):
                    b = (f * 4 + n) % 4
                    for a in range(NF):
                        w = []
                        if a == 0:
                            w = [(wsem, wtgt), (ve, Z_VE)]
                            if guards[b] is not None:
                                w.append(guards[b])
                        p.emit("tensor", lambda e, f=f, n=n, a=a, b=b: e.matmul(
                            pbank(b),
                            wA[:, a * 1024 + f * 128:a * 1024 + (f + 1) * 128],
                            R1[:, a * T + n * 512:a * T + (n + 1) * 512],
                            start=(a == 0), stop=(a == NF - 1)),
                            waits=w, inc=pe if a == NF - 1 else None)
                    q_pe = pe.n
                    p.emit("vector", lambda e, b=b: e.tensor_scalar_mul(
                        probs[:, 0:512], pbank(b), colsf[:, 6:7]),
                        waits=[(pe, q_pe)] + tp_prev, inc=ve)
                    guards[b] = (ve, ve.n)
                    ev_ve = ve.n
                    for jj in range(4):
                        p.emit("tensor", lambda e, jj=jj: e.transpose(
                            psb(4)[:, jj * 128:(jj + 1) * 128],
                            probs[:, jj * 128:(jj + 1) * 128], ident[:]),
                            waits=[(ve, ev_ve)] if jj == 0 else [],
                            inc=pe if jj == 3 else None)
                    tp_pe = pe.n
                    tp_prev = [(pe, tp_pe)]
                    p.emit("vector", lambda e: e.tensor_copy(
                        probs[:, 512:1024], psb(4)[:, 0:512]),
                        waits=[(pe, tp_pe)], inc=ve)
                    pv_ve = ve.n
                    for jj in range(4):
                        j = n * 4 + jj
                        p.emit("vector", lambda e, j=j, jj=jj, f=f:
                               e.scalar_tensor_tensor(
                            x_res[:, j * D + f * 128:j * D + (f + 1) * 128],
                            probs[:, 512 + jj * 128:512 + (jj + 1) * 128],
                            av[:, j:j + 1],
                            x_res[:, j * D + f * 128:j * D + (f + 1) * 128],
                            ALU.mult, ALU.add),
                            waits=[(ve, pv_ve)] if jj == 0 else [], inc=ve)

        if n_sub >= 2:
            ffn_q(True)

        if n_sub >= 3:
            ln_stats(0, 1)
            p.emit("vector", lambda e: e.scalar_tensor_tensor(
                vslot(15), vslot(0), -1.0, vslot(1), ALU.mult, ALU.mult),
                waits=[(se, se.n)], chain=ve)
            apply_transpose(lambda j: x_res[:, j * D:(j + 1) * D], 1, 15, R1,
                            fold_cols=None, wait0=[(ve, ve.n), (se, se.n),
                                                   (pe, pe.n)])
            attn_sublayer(1, bqT, bkT, bvT, 1.0)
            wo2_t = load_slab(wB, boT, waits=[(pe, pe.n), (dcc, dcc.n)])
            tp_prev = []
            bank_guard2 = {}
            for c in range(4):
                for f in range(NF):
                    b = f % 4
                    for a in range(NF):
                        w = []
                        if a == 0:
                            w = [(dw, wo2_t), (ve, ve.n)]
                            if bank_guard2.get(b):
                                w.append(bank_guard2[b])
                        p.emit("tensor", lambda e, f=f, c=c, a=a, b=b: e.matmul(
                            pbank(b),
                            wB[:, a * 1024 + f * 128:a * 1024 + (f + 1) * 128],
                            R2[:, a * T + c * 512:a * T + (c + 1) * 512],
                            start=(a == 0), stop=(a == NF - 1)),
                            waits=w, inc=pe if a == NF - 1 else None)
                    wo_pe = pe.n
                    p.emit("vector", lambda e, b=b, f=f: e.tensor_scalar(
                        stg[:, 0:512], pbank(b), 1.0,
                        ln_cols[:, 2 * NF + f:2 * NF + f + 1], ALU.mult, ALU.add),
                        waits=[(pe, wo_pe)] + tp_prev, inc=ve)
                    bank_guard2[b] = (ve, ve.n)
                    ev_ve = ve.n
                    for jj in range(4):
                        p.emit("tensor", lambda e, jj=jj: e.transpose(
                            psb(4)[:, jj * 128:(jj + 1) * 128],
                            stg[:, jj * 128:(jj + 1) * 128], ident[:]),
                            waits=[(ve, ev_ve)] if jj == 0 else [],
                            inc=pe if jj == 3 else None)
                    tp_pe = pe.n
                    tp_prev = [(pe, tp_pe)]
                    p.emit("vector", lambda e, f=f, c=c: e.tensor_add(
                        flatap(x_res, c * 4 * D + f * 128,
                               [(NT * D, 128), (D, 4), (1, 128)]),
                        flatap(x_res, c * 4 * D + f * 128,
                               [(NT * D, 128), (D, 4), (1, 128)]),
                        psb(4)[:, 0:512].rearrange("p (a d) -> p a d", d=128)),
                        waits=[(pe, tp_pe), (ve, ve.n)], inc=ve)

        if n_sub >= 4:
            ffn_q(False)

        TAPS = []
        loc = locals()
        for name, fn in DEV_TAPS:
            ap_sb, shape, dtype = fn(loc)
            o = nc.dram_tensor(f"tap_{name}", shape, dtype, kind="ExternalOutput").ap()
            TAPS.append((o, ap_sb, dtype))

        p.emit("sync", lambda e: e.dma_start(
            xout.rearrange("(j p) d -> p j d", p=128),
            x_res[:].rearrange("p (j d) -> p j d", d=D)),
            waits=[(se, se.n), (ve, ve.n), (pe, pe.n), (po, po.n), (dw, dw.n),
                   (dcc, dcc.n), (dx, dx.n)],
            inc=od, inc_by=16)
        for o, ap_sb, dtype in TAPS:
            if dtype == F32:
                p.emit("gpsimd", lambda e, o=o, a=ap_sb: e.dma_start(o, a),
                       waits=[(od, 16)], inc=od, inc_by=16)
            else:
                p.emit("sync", lambda e, o=o, a=ap_sb: e.dma_start(o, a),
                       inc=od, inc_by=16)
        p.emit("sync", lambda e: None, waits=[(od, od.n)])

        p.finalize(nc)
    return nc


_CACHE = {}


def _grow(v):
    v = np.asarray(v, np.float32)
    r = np.zeros((2, 128), BF)
    r[0, 0:64] = v.astype(BF)
    r[1, 64:128] = v.astype(BF)
    return r


def _prep_inputs(inputs, h):
    idx, cos, sin, maskdup, ident, pswap = _host_aux(h)
    tp = lambda a: np.ascontiguousarray(np.asarray(a, np.float32).T)
    g2 = lambda v: np.tile(np.asarray(v, np.float32), 2).reshape(128, 1)
    d = {
        "wqT": tp(inputs["L0_attn_a_wq"]), "wkT": tp(inputs["L0_attn_a_wk"]),
        "wvT": tp(inputs["L0_attn_a_wv"]), "woT": tp(inputs["L0_attn_a_wo"]),
        "w1T": tp(inputs["L0_attn_a_w1"]), "w3T": tp(inputs["L0_attn_a_w3"]),
        "w2T": tp(inputs["L0_attn_a_w2"]),
        "fwT": tp(inputs["L0_ffn_weight"]),
        "bqT": tp(inputs["L0_attn_b_Wq"]), "bkT": tp(inputs["L0_attn_b_Wk"]),
        "bvT": tp(inputs["L0_attn_b_Wv"]), "boT": tp(inputs["L0_attn_b_Wo"]),
        "rcos": cos, "rsin": sin,
        "maskin": maskdup, "identin": ident, "pswapin": pswap,
        "gqc": g2(inputs["L0_attn_b_gq"]), "bqc": g2(inputs["L0_attn_b_bq"]),
        "gkc": g2(inputs["L0_attn_b_gk"]), "bkc": g2(inputs["L0_attn_b_bk"]),
        "boc": np.ascontiguousarray(
            np.asarray(inputs["L0_attn_b_bo"], np.float32).reshape(NF, 128).T),
        "growsq": _grow(inputs["L0_attn_b_gq"]),
        "growsk": _grow(inputs["L0_attn_b_gk"]),
        "nbrowq": _grow(-np.asarray(inputs["L0_attn_b_bq"], np.float64)),
        "nbrowk": _grow(-np.asarray(inputs["L0_attn_b_bk"], np.float64)),
        "nbdupq": np.tile(-np.asarray(inputs["L0_attn_b_bq"], np.float32),
                          2).reshape(1, 128).astype(BF),
        "nbdupk": np.tile(-np.asarray(inputs["L0_attn_b_bk"], np.float32),
                          2).reshape(1, 128).astype(BF),
        "n1c": np.ascontiguousarray(
            np.asarray(inputs["L0_attn_a_norm1_w"], np.float32).reshape(NF, 128).T),
        "n2c": np.ascontiguousarray(
            np.asarray(inputs["L0_attn_a_norm2_w"], np.float32).reshape(NF, 128).T),
    }
    return idx, d


def kernel(**inputs):
    inputs.get("L0_attn_b_mask")
    x = np.asarray(inputs["L0_attn_a_x"], np.float32)
    if "prog" not in _CACHE:
        _CACHE["prog"] = build_program()
    nc = _CACHE["prog"]

    in_maps = []
    idxs = []
    for c in range(8):
        b, h = c // 2, c % 2
        idx, d = _prep_inputs(inputs, h)
        d["xin"] = np.ascontiguousarray(x[b, idx, :])
        in_maps.append(d)
        idxs.append(idx)

    res = run_bass_kernel_spmd(nc, in_maps, core_ids=list(range(8)))

    out = np.empty_like(x)
    for c in range(8):
        b = c // 2
        out[b, idxs[c], :] = np.asarray(res.results[c]["xout"], np.float32)
    extras = {}
    for c in range(8):
        for k, v in res.results[c].items():
            if k.startswith("tap_"):
                extras.setdefault(k, []).append(np.asarray(v))
    kernel.last_taps = extras
    return out


# version 06
# speedup vs baseline: 1.5149x; 1.0063x over previous
import contextlib
import numpy as np
import ml_dtypes
import concourse.bass as bass
import concourse.mybir as mybir
from concourse.bass_utils import run_bass_kernel_spmd

F32 = mybir.dt.float32
BF16 = mybir.dt.bfloat16
I32 = mybir.dt.int32
AF = mybir.ActivationFunctionType
ALU = mybir.AluOpType
AX = mybir.AxisListType
BF = ml_dtypes.bfloat16

D = 1024
T = 2048
H = 16
DK = 64
DFF = 4096
NT = 16
NF = 8
NP = 8
EPS_LN = 1e-5
EPS_RMS = 1e-5
EPS_Q = 1e-8
ROPE_THETA = 10000.0
BSHIFT = 40.0

N_SUBLAYERS = 4
STOP_STAGE = 99
ATT_LIMIT = (99, 99, 99, 99)
STOP_LI = None
S5_PARTS = 9
DEV_TAPS = []


def _zigzag_idx(h):
    r = np.arange(T)
    return (2 * (r // 128) + h) * 128 + (r % 128)


def _rope_tables(pos):
    k = np.arange(DK) // 2
    inv = ROPE_THETA ** (-2.0 * k.astype(np.float64) / DK)
    ang = pos[None, :].astype(np.float64) * inv[:, None]
    cos = np.cos(ang)
    sgn = np.where(np.arange(DK) % 2 == 0, -1.0, 1.0)
    sin = np.sin(ang) * sgn[:, None]
    return cos.astype(np.float32), sin.astype(np.float32)


def _host_aux(h):
    idx = _zigzag_idx(h)
    cos, sin = _rope_tables(idx)
    stair = (np.arange(128)[:, None] <= np.arange(128)[None, :]).astype(BF)
    ones = np.ones((128, 128), BF)
    zeros = np.zeros((128, 128), BF)
    slots = [stair, zeros] if h == 0 else [ones, stair]
    maskdup = np.concatenate([np.concatenate([s, s], axis=1) for s in slots], axis=1)
    ident = np.eye(128, dtype=BF)
    pswap = np.zeros((128, 128), BF)
    dd = np.arange(128)
    sw = np.where(dd % 2 == 0, dd + 1, dd - 1)
    pswap[sw, dd] = 1.0
    return idx, cos, sin, maskdup, ident, pswap


class Sem:
    def __init__(self, es, nc, name):
        self.h = es.enter_context(nc.semaphore(name))
        self.n = 0

    def bump(self, k=1):
        self.n += k
        return self.n


class Prog:
    ENGS = ("sync", "gpsimd", "tensor", "vector", "scalar")

    def __init__(self):
        self.rec = {e: [] for e in self.ENGS}

    def emit(self, eng, builder, waits=(), inc=None, inc_by=1, chain=None):
        if chain is not None:
            waits = list(waits) + [(chain, chain.n)]
            inc = chain
            inc_by = 1
        waits = tuple((s, v) for (s, v) in waits)

        def fn(engine):
            for s, v in waits:
                engine.wait_ge(s.h, v)
            inst = builder(engine)
            if inc is not None:
                if inst is None:
                    engine.sem_inc(inc.h, inc_by)
                else:
                    inst.then_inc(inc.h, inc_by)
        self.rec[eng].append(fn)
        if inc is not None:
            inc.bump(inc_by)

    def finalize(self, nc):
        with nc.Block() as block:
            for e in self.ENGS:
                recs = tuple(self.rec[e])

                def body(engine, _r=recs):
                    for f in _r:
                        f(engine)
                getattr(block, e)(body)


def build_program(n_sub=None):
    if n_sub is None:
        n_sub = N_SUBLAYERS
    stop = STOP_STAGE if STOP_LI is None else 99
    nc = bass.Bass("TRN2", target_bir_lowering=False)

    def inp(name, shape, dtype=F32):
        return nc.dram_tensor(name, shape, dtype, kind="ExternalInput").ap()

    xin = inp("xin", [T, D])
    wqT = inp("wqT", [D, D]); wkT = inp("wkT", [D, D]); wvT = inp("wvT", [D, D])
    woT = inp("woT", [D, D])
    w1T = inp("w1T", [D, DFF]); w3T = inp("w3T", [D, DFF]); w2T = inp("w2T", [DFF, D])
    fwT = inp("fwT", [D, D])
    bqT = inp("bqT", [D, D]); bkT = inp("bkT", [D, D]); bvT = inp("bvT", [D, D])
    boT = inp("boT", [D, D])
    rcos = inp("rcos", [DK, T]); rsin = inp("rsin", [DK, T])
    maskin = inp("maskin", [128, 512], BF16)
    identin = inp("identin", [128, 128], BF16)
    pswapin = inp("pswapin", [128, 128], BF16)
    gqc = inp("gqc", [128, 1]); bqc = inp("bqc", [128, 1])
    gkc = inp("gkc", [128, 1]); bkc = inp("bkc", [128, 1])
    growsq = inp("growsq", [2, 128], BF16)
    growsk = inp("growsk", [2, 128], BF16)
    nbrowq = inp("nbrowq", [2, 128], BF16)
    nbrowk = inp("nbrowk", [2, 128], BF16)
    nbdupq = inp("nbdupq", [1, 128], BF16)
    nbdupk = inp("nbdupk", [1, 128], BF16)
    boc = inp("boc", [128, NF])
    n1c = inp("n1c", [128, NF]); n2c = inp("n2c", [128, NF])

    xout = nc.dram_tensor("xout", [T, D], F32, kind="ExternalOutput").ap()

    KSZ = NP * 128 * T
    VOFF = KSZ
    MOFF = 2 * KSZ
    CCE = 2 * KSZ + 2048
    cc_in_h = [nc.dram_tensor(f"cc_in{i}", [CCE // 2048, 2048], BF16) for i in range(2)]
    cc_out_h = [nc.dram_tensor(f"cc_out{i}", [2 * CCE // 2048, 2048], BF16)
                for i in range(2)]
    wq_park = nc.dram_tensor("wq_park", [128, 8192], BF16)
    w1bf = nc.dram_tensor("w1bf", [D * DFF // 2048, 2048], BF16)
    w3bf = nc.dram_tensor("w3bf", [D * DFF // 2048, 2048], BF16)
    w2bf = nc.dram_tensor("w2bf", [D * DFF // 2048, 2048], BF16)

    es = contextlib.ExitStack()
    with es:
        def sbt(name, shape, dtype=BF16):
            return es.enter_context(nc.sbuf_tensor(name, shape, dtype))

        def psumt(name, shape, dtype=F32):
            return es.enter_context(nc.psum_tensor(name, shape, dtype))

        x_res = sbt("x_res", [128, NT * D], F32)
        R1 = sbt("R1", [128, 16384])
        R2 = sbt("R2", [128, 16384])
        wA = sbt("wA", [128, 8256])
        wB = sbt("wB", [128, 10304])
        ysc = sbt("ysc", [128, 4096])
        stg = sbt("stg", [128, 4096])
        probs = sbt("probs", [128, 3072])
        scrf = stg[:].bitcast(F32)
        iscr = stg[:].bitcast(I32)
        yscf = ysc[:].bitcast(F32)
        rrowf = wB[0:1, 8256:10304].bitcast(F32)
        mask = sbt("mask", [128, 512])
        ident = sbt("ident", [128, 128])
        pswap = sbt("pswap", [128, 128])
        e1 = sbt("e1", [1, 128])
        e2 = sbt("e2", [1, 128])
        bd2 = sbt("bd2", [128, 4])
        onesrow = sbt("onesrow", [33, 512])
        growq = sbt("growq", [33, 128])
        growk = sbt("growk", [33, 128])
        nbq = sbt("nbq", [33, 128])
        nbk = sbt("nbk", [33, 128])
        growq2 = sbt("growq2", [1, 128])
        growk2 = sbt("growk2", [1, 128])
        nbqf = sbt("nbqf", [1, 128])
        nbkf = sbt("nbkf", [1, 128])
        onesf32 = es.enter_context(nc.sbuf_tensor("onesf32", [1, 128], F32))
        vecs = es.enter_context(nc.sbuf_tensor("vecs", [128, 256], F32))
        colsf = es.enter_context(nc.sbuf_tensor("colsf", [128, 16], F32))
        Mrow = es.enter_context(nc.sbuf_tensor("Mrow", [2, 64], F32))
        biascol = es.enter_context(nc.sbuf_tensor("biascol", [128, 16], F32))
        ln_cols = es.enter_context(nc.sbuf_tensor("ln_cols", [128, 3 * NF], F32))

        def Mq(z):
            return Mrow[0:2, z * 8:(z + 1) * 8]

        def Mr1(a, b):
            return Mrow[0:1, a:b]

        ps2 = [psumt(f"ps2_{i}", [128, 1024]) for i in range(4)]

        def pbank(i):
            return ps2[i // 2][:, (i % 2) * 512:(i % 2 + 1) * 512]

        def psb(i):
            return ps2[i // 2][:].bitcast(BF16)[:, (i % 2) * 1024:(i % 2 + 1) * 1024]

        p = Prog()
        SS = {n: Sem(es, nc, n) for n in
              ("dx", "dw", "dcc", "pe", "ve", "se", "po", "od")}
        dx, dw, dcc = SS["dx"], SS["dw"], SS["dcc"]
        pe, ve, se, po, od = SS["pe"], SS["ve"], SS["se"], SS["po"], SS["od"]

        def flatap(h, off, dims):
            return bass.AP(h, off, [list(d) for d in dims])

        p.emit("sync", lambda e: e.dma_start(
            x_res[:].rearrange("p (j d) -> p j d", d=D),
            xin.rearrange("(j p) d -> p j d", p=128)), inc=dx, inc_by=16)
        for dst, srcap in ((mask[:], maskin), (ident[:], identin),
                           (pswap[:], pswapin),
                           (colsf[:, 0:1], gqc), (colsf[:, 1:2], bqc),
                           (colsf[:, 2:3], gkc), (colsf[:, 3:4], bkc),
                           (ln_cols[:, 0:NF], n1c), (ln_cols[:, NF:2 * NF], n2c),
                           (ln_cols[:, 2 * NF:3 * NF], boc),
                           (growq[0:1, :], growsq[0:1, :]),
                           (growq[32:33, :], growsq[1:2, :]),
                           (growk[0:1, :], growsk[0:1, :]),
                           (growk[32:33, :], growsk[1:2, :]),
                           (nbq[0:1, :], nbrowq[0:1, :]),
                           (nbq[32:33, :], nbrowq[1:2, :]),
                           (nbk[0:1, :], nbrowk[0:1, :]),
                           (nbk[32:33, :], nbrowk[1:2, :]),
                           (growq2[0:1, :], growsq[1:2, :]),
                           (growk2[0:1, :], growsk[1:2, :]),
                           (nbqf[0:1, :], nbdupq),
                           (nbkf[0:1, :], nbdupk)):
            p.emit("sync", lambda e, d=dst, s=srcap: e.dma_start(d, s),
                   inc=dx, inc_by=16)
        DX0 = dx.n

        for ap_, val in ((colsf[:, 15:16], EPS_LN),
                         (e1[0:1, 0:64], 1.0), (e1[0:1, 64:128], 0.0),
                         (e2[0:1, 0:64], 0.0), (e2[0:1, 64:128], 1.0),
                         (bd2[0:64, 0:1], 1.0 / 64), (bd2[64:128, 0:1], 0.0),
                         (bd2[0:64, 1:2], 0.0), (bd2[64:128, 1:2], 1.0 / 64),
                         (bd2[0:64, 2:3], 1.0), (bd2[64:128, 2:3], 0.0),
                         (bd2[0:64, 3:4], 0.0), (bd2[64:128, 3:4], 1.0),
                         (onesrow[0:1, :], 1.0), (onesrow[32:33, :], 1.0),
                         (onesf32[:], 1.0)):
            p.emit("vector", lambda e, a=ap_, v=val: e.memset(a, v), chain=ve)
        for wbuf in (wA, wB):
            p.emit("vector", lambda e, wbuf=wbuf: e.memset(
                flatap(wbuf, 4096 + 64, [(wbuf.shape[1], 128), (130, 32), (1, 1)]),
                1.0), chain=ve)
            p.emit("vector", lambda e, wbuf=wbuf: e.memset(
                flatap(wbuf, 4096 + 129, [(wbuf.shape[1], 128), (130, 32), (1, 1)]),
                1.0), chain=ve)
        VE_CONST = ve.n

        def vslot(s_):
            return vecs[:, s_ * 16:(s_ + 1) * 16]

        def ln_stats(slot_m, slot_r, extra_rms_slot=None):
            m = vslot(slot_m); r = vslot(slot_r); nb = vslot(14)
            p.emit("vector", lambda e: e.reduce_sum(
                m, x_res[:].rearrange("p (j d) -> p j d", d=D), axis=AX.X),
                waits=[(dx, DX0)], chain=ve)
            p.emit("vector", lambda e: e.tensor_scalar_mul(m, m, 1.0 / D), chain=ve)
            for j in range(NT):
                half = (j % 2) * 1024
                wse = [(ve, ve.n)] if j >= 2 else [(dx, DX0), (se, se.n), (ve, ve.n)]
                p.emit("scalar", lambda e, j=j, half=half: e.activation(
                    scrf[:, half:half + 1024], x_res[:, j * D:(j + 1) * D],
                    AF.Square), waits=wse, inc=se)
                p.emit("vector", lambda e, j=j, half=half: e.reduce_sum(
                    r[:, j:j + 1],
                    scrf[:, half:half + 1024].rearrange("p (a d) -> p a d", a=1),
                    axis=AX.X), waits=[(se, se.n)], chain=ve)
            p.emit("vector", lambda e: e.tensor_scalar_mul(r, r, 1.0 / D), chain=ve)
            p.emit("vector", lambda e: e.tensor_mul(nb, m, m), chain=ve)
            p.emit("vector", lambda e: e.tensor_sub(r, r, nb), chain=ve)
            p.emit("scalar", lambda e: e.activation(
                r, r, AF.Ln, bias=colsf[:, 15:16]),
                   waits=[(ve, ve.n)], chain=se)
            p.emit("scalar", lambda e: e.activation(r, r, AF.Exp, scale=-0.5),
                   chain=se)
            if extra_rms_slot is not None:
                s1 = vslot(extra_rms_slot)
                p.emit("vector", lambda e: e.tensor_mul(s1, r, r),
                       waits=[(se, se.n)], chain=ve)
                p.emit("vector", lambda e: e.tensor_scalar(
                    s1, s1, -EPS_LN, 1.0 + EPS_RMS, ALU.mult, ALU.add), chain=ve)
                p.emit("scalar", lambda e: e.activation(s1, s1, AF.Ln),
                       waits=[(ve, ve.n)], chain=se)
                p.emit("scalar", lambda e: e.activation(s1, s1, AF.Exp, scale=-0.5),
                       chain=se)
                p.emit("vector", lambda e: e.tensor_mul(s1, s1, r),
                       waits=[(se, se.n)], chain=ve)
            return se.n, ve.n

        def apply_transpose(src_fn, slot_s, slot_nb, dst, fold_cols=None,
                            jgs=(0, 1, 2, 3), wait0=()):
            s_ = vslot(slot_s); nb = vslot(slot_nb)
            prev_evac = []
            pe_tp = pe.n
            jgs = list(jgs)
            for gi, jg in enumerate(jgs):
                for jj in range(4):
                    j = jg * 4 + jj
                    w = list(wait0) if (jj == 0 and gi == 0) else []
                    if jj == 0 and gi > 0:
                        w.append((pe, pe_tp))
                    p.emit("vector", lambda e, j=j, jj=jj: e.tensor_scalar(
                        stg[:, jj * 1024:(jj + 1) * 1024], src_fn(j),
                        s_[:, j:j + 1], nb[:, j:j + 1], ALU.mult, ALU.add),
                        waits=w, inc=ve)
                ve_app = ve.n
                for f in range(NF):
                    for jj in range(4):
                        w = []
                        if f == 0 and jj == 0:
                            w = [(ve, ve_app)] + prev_evac
                        p.emit("tensor", lambda e, f=f, jj=jj: e.transpose(
                            psb(f)[:, jj * 128:(jj + 1) * 128],
                            stg[:, jj * 1024 + f * 128:jj * 1024 + (f + 1) * 128],
                            ident[:]),
                            waits=w, inc=pe if (f == NF - 1 and jj == 3) else None)
                pe_tp = pe.n
                prev_evac = []
                for f in range(NF):
                    if fold_cols is not None:
                        p.emit("vector", lambda e, f=f, jg=jg: e.tensor_scalar_mul(
                            dst[:, f * T + jg * 512:f * T + (jg + 1) * 512],
                            psb(f)[:, 0:512], fold_cols[:, f:f + 1]),
                            waits=[(pe, pe_tp)], inc=ve)
                    else:
                        p.emit("vector", lambda e, f=f, jg=jg: e.tensor_copy(
                            dst[:, f * T + jg * 512:f * T + (jg + 1) * 512],
                            psb(f)[:, 0:512]),
                            waits=[(pe, pe_tp)], inc=ve)
                    prev_evac = [(ve, ve.n)]
            return se.n, ve.n, pe.n

        def load_slab(dst, srcT, waits=()):
            p.emit("gpsimd", lambda e, dst=dst, srcT=srcT: e.dma_start(
                dst[:, 0:8192].rearrange("p (a o) -> p a o", o=D),
                srcT.rearrange("(a p) o -> p a o", p=128)),
                waits=waits, inc=dw, inc_by=16)
            return dw.n

        def proj_fm(slab, w_tgt, rhs_tgt, evac_fn, w_sem=None):
            if w_sem is None:
                w_sem = dw
            guards = [None, None, None, None]
            for f in range(NF):
                for n in range(4):
                    b = (f * 4 + n) % 4
                    for a in range(NF):
                        w = []
                        if a == 0:
                            w = [(w_sem, w_tgt), (ve, rhs_tgt)]
                            if guards[b] is not None:
                                w.append(guards[b])
                        p.emit("tensor", lambda e, f=f, n=n, a=a, b=b, slab=slab:
                               e.matmul(
                            pbank(b),
                            slab[:, a * 1024 + f * 128:a * 1024 + (f + 1) * 128],
                            R1[:, a * T + n * 512:a * T + (n + 1) * 512],
                            start=(a == 0), stop=(a == NF - 1)),
                            waits=w, inc=pe if a == NF - 1 else None)
                    guards[b] = evac_fn(f, n, b, pe.n)
            return pe.n

        ln_stats(0, 1, extra_rms_slot=2)
        p.emit("vector", lambda e: e.scalar_tensor_tensor(
            vslot(15), vslot(0), -1.0, vslot(2), ALU.mult, ALU.mult), chain=ve)
        p.emit("vector", lambda e: e.scalar_tensor_tensor(
            vslot(3), vslot(0), 1.0, vslot(1), ALU.mult, ALU.mult), chain=ve)
        p.emit("vector", lambda e: e.tensor_scalar_mul(
            vslot(13), vslot(3), -1.0), chain=ve)
        apply_transpose(lambda j: x_res[:, j * D:(j + 1) * D], 2, 15, R1,
                        fold_cols=ln_cols[:, 0:NF],
                        wait0=[(ve, ve.n), (se, se.n)])

        def attn_sublayer(li, wq_d, wk_d, wv_d, att_scale):
            cc_i, cc_o = cc_in_h[li], cc_out_h[li]
            stop = STOP_STAGE if (STOP_LI is None or li == STOP_LI) else 99
            lim = ATT_LIMIT if (STOP_LI is None or li == STOP_LI) \
                else (99, 99, 99, 99)
            if stop < 2:
                return 0, 0, 0
            if li == 0:
                for dstv, tbl in ((scrf, rcos), (yscf, rsin)):
                    for pp in range(2):
                        p.emit("sync", lambda e, dstv=dstv, tbl=tbl, pp=pp:
                               e.dma_start(dstv[pp * 64:(pp + 1) * 64, 0:2048], tbl),
                               waits=[(ve, ve.n), (pe, pe.n)] if pp == 0 else [],
                               inc=dx, inc_by=16)
            RT_DX = dx.n
            HFM_VE = ve.n

            wq_t = load_slab(wA, wq_d, waits=[(pe, pe.n)])
            wk_t = load_slab(wB, wk_d)

            def evac_q(f, n, b, pet):
                p.emit("vector", lambda e, f=f, n=n, b=b: e.tensor_copy(
                    R2[:, f * T + n * 512:f * T + (n + 1) * 512], pbank(b)),
                    waits=[(pe, pet)], inc=ve)
                return (ve, ve.n)
            proj_fm(wA, wq_t, HFM_VE, evac_q)
            wv_t = load_slab(wA, wv_d, waits=[(pe, pe.n)])

            first = {}

            def normfold(zi, f, wse):
                for n2 in range(2):
                    p.emit("tensor", lambda e, n2=n2: e.matmul(
                        ps2[3][0:2, 0:512], bd2[:, 2:4],
                        probs[:, 2048 + n2 * 512:2048 + (n2 + 1) * 512],
                        start=True, stop=True),
                        waits=wse + [(ve, ve.n)], inc=pe)
                    nm_pe = pe.n
                    p.emit("vector", lambda e: e.reduce_max(
                        Mq(6)[:, 0:1],
                        ps2[3][0:2, 0:512].rearrange("p (a n) -> p a n", a=1),
                        axis=AX.X), waits=[(pe, nm_pe)], chain=ve)
                    key = (zi, f)
                    if key not in first:
                        first[key] = True
                        p.emit("vector", lambda e, zi=zi, f=f: e.tensor_copy(
                            Mq(zi)[:, f:f + 1], Mq(6)[:, 0:1]), chain=ve)
                    else:
                        p.emit("vector", lambda e, zi=zi, f=f: e.tensor_tensor(
                            Mq(zi)[:, f:f + 1], Mq(zi)[:, f:f + 1], Mq(6)[:, 0:1],
                            ALU.max), chain=ve)
                    wse = []

            st = {"ktile_free": 0, "kship": 0, "sq_pe": 0}

            def qkln(buf_fn, grow, grow2, nbf):
                for ch2 in range(2):
                    z2 = ch2 * 1024
                    p.emit("scalar", lambda e, z2=z2: e.activation(
                        probs[:, 2048:3072], buf_fn(z2), AF.Square),
                        waits=[(ve, ve.n), (se, se.n), (pe, st["sq_pe"])],
                        inc=se)
                    sq_se = se.n
                    for ch in range(2):
                        for hd in range(2):
                            rp = hd * 32
                            p.emit("tensor", lambda e, z2=z2, ch=ch, hd=hd,
                                   rp=rp: e.matmul(
                                ps2[2][rp:rp + 1, ch * 512:(ch + 1) * 512],
                                bd2[:, hd:hd + 1],
                                buf_fn(z2)[:, ch * 512:(ch + 1) * 512],
                                start=True, stop=True),
                                waits=[(ve, ve.n), (se, sq_se)]
                                if (ch == 0 and hd == 0) else [], inc=None)
                            p.emit("tensor", lambda e, ch=ch, hd=hd, rp=rp:
                                   e.matmul(
                                ps2[3][rp:rp + 1, ch * 512:(ch + 1) * 512],
                                bd2[:, hd:hd + 1],
                                probs[:, 2048 + ch * 512:2048 + (ch + 1) * 512],
                                start=True, stop=True),
                                inc=pe if (ch == 1 and hd == 1) else None)
                    mm_pe = pe.n
                    p.emit("vector", lambda e: e.tensor_copy(
                        yscf[0:1, 0:1024], ps2[2][0:1, 0:1024]),
                        waits=[(pe, mm_pe)], chain=ve)
                    p.emit("vector", lambda e: e.tensor_copy(
                        scrf[0:1, 0:1024], ps2[2][32:33, 0:1024]), chain=ve)
                    p.emit("vector", lambda e: e.tensor_copy(
                        yscf[32:33, 0:1024], ps2[3][0:1, 0:1024]), chain=ve)
                    p.emit("vector", lambda e: e.tensor_copy(
                        scrf[32:33, 0:1024], ps2[3][32:33, 0:1024]), chain=ve)
                    for mrow, srow in ((yscf, yscf), (scrf, scrf)):
                        p.emit("vector", lambda e, mrow=mrow, srow=srow:
                               e.tensor_mul(
                            srow[32:33, 1024:2048], mrow[0:1, 0:1024],
                            mrow[0:1, 0:1024]), chain=ve)
                        p.emit("vector", lambda e, srow=srow: e.tensor_sub(
                            srow[32:33, 0:1024], srow[32:33, 0:1024],
                            srow[32:33, 1024:2048]), chain=ve)
                    for srow in (yscf, scrf):
                        p.emit("scalar", lambda e, srow=srow: e.activation(
                            srow[32:33, 0:1024], srow[32:33, 0:1024], AF.Ln,
                            bias=colsf[32:33, 15:16]),
                            waits=[(ve, ve.n)] if srow is yscf else [],
                            chain=se)
                        p.emit("scalar", lambda e, srow=srow: e.activation(
                            srow[0:1, 1024:2048], srow[32:33, 0:1024],
                            AF.Exp, scale=-0.5), chain=se)
                    for mrow in (yscf, scrf):
                        p.emit("vector", lambda e, mrow=mrow: e.tensor_mul(
                            mrow[32:33, 1024:2048], mrow[0:1, 0:1024],
                            mrow[0:1, 1024:2048]),
                            waits=[(se, se.n)] if mrow is yscf else [],
                            chain=ve)
                    p.emit("vector", lambda e: e.tensor_copy(
                        stg[0:1, 0:1024], yscf[0:1, 1024:2048]), chain=ve)
                    p.emit("vector", lambda e: e.tensor_copy(
                        stg[0:1, 1024:2048], scrf[0:1, 1024:2048]), chain=ve)
                    p.emit("vector", lambda e: e.tensor_copy(
                        stg[0:1, 2048:3072], yscf[32:33, 1024:2048]), chain=ve)
                    p.emit("vector", lambda e: e.tensor_copy(
                        stg[0:1, 3072:4096], scrf[32:33, 1024:2048]), chain=ve)
                    ROWS_VE = ve.n
                    for ch in range(2):
                        p.emit("tensor", lambda e, ch=ch, grow=grow: e.matmul(
                            ps2[2][:, ch * 512:(ch + 1) * 512],
                            grow[0:1, :],
                            stg[0:1, ch * 512:(ch + 1) * 512],
                            start=True, stop=False),
                            waits=[(ve, ROWS_VE)] if ch == 0 else [],
                            inc=None)
                        p.emit("tensor", lambda e, ch=ch, grow2=grow2: e.matmul(
                            ps2[2][:, ch * 512:(ch + 1) * 512],
                            grow2[0:1, :],
                            stg[0:1, 1024 + ch * 512:1024 + (ch + 1) * 512],
                            start=False, stop=True), inc=None)
                        p.emit("tensor", lambda e, ch=ch, grow=grow: e.matmul(
                            ps2[3][:, ch * 512:(ch + 1) * 512],
                            grow[0:1, :],
                            stg[0:1, 2048 + ch * 512:2048 + (ch + 1) * 512],
                            start=True, stop=False), inc=None)
                        p.emit("tensor", lambda e, ch=ch, grow2=grow2: e.matmul(
                            ps2[3][:, ch * 512:(ch + 1) * 512],
                            grow2[0:1, :],
                            stg[0:1, 3072 + ch * 512:3072 + (ch + 1) * 512],
                            start=False, stop=False), inc=None)
                        p.emit("tensor", lambda e, ch=ch, nbf=nbf: e.matmul(
                            ps2[3][:, ch * 512:(ch + 1) * 512],
                            nbf[0:1, :], onesrow[0:1, 0:512],
                            start=False, stop=True),
                            inc=pe if ch == 1 else None)
                    bc_pe = pe.n
                    p.emit("vector", lambda e, z2=z2: e.tensor_mul(
                        probs[:, 2048:3072], buf_fn(z2), ps2[2][:, 0:1024]),
                        waits=[(pe, bc_pe), (ve, ve.n), (se, se.n)], inc=ve)
                    p.emit("vector", lambda e, z2=z2: e.tensor_sub(
                        buf_fn(z2), probs[:, 2048:3072], ps2[3][:, 0:1024]),
                        inc=ve)
                    st["sq_pe"] = pe.n

            def ktransform(f):
                if li == 1:
                    qkln(lambda z: probs[:, z:z + 1024], growk, growk2, nbkf)
                for h2 in range(2):
                    z = h2 * 1024
                    if li == 0:
                        for n2 in range(2):
                            p.emit("tensor", lambda e, z=z, n2=n2: e.matmul(
                                ps2[2][:, n2 * 512:(n2 + 1) * 512], pswap[:],
                                probs[:, z + n2 * 512:z + (n2 + 1) * 512],
                                start=True, stop=True),
                                waits=[(ve, ve.n)], inc=pe if n2 == 1 else None)
                        swp = pe.n
                        p.emit("vector", lambda e, z=z: e.tensor_mul(
                            probs[:, 2048:3072], probs[:, z:z + 1024],
                            scrf[:, 0:1024] if z == 0 else scrf[:, 1024:2048]),
                            waits=[(dx, RT_DX), (pe, st["sq_pe"])], inc=ve)
                        p.emit("vector", lambda e, z=z: e.tensor_mul(
                            probs[:, z:z + 1024],
                            ps2[2][:].bitcast(F32)[:, 0:1024],
                            yscf[:, 0:1024] if z == 0 else yscf[:, 1024:2048]),
                            waits=[(pe, swp)], inc=ve)
                        p.emit("vector", lambda e, z=z: e.tensor_add(
                            probs[:, z:z + 1024], probs[:, 2048:3072],
                            probs[:, z:z + 1024]), inc=ve)
                    kr_ve = ve.n
                    p.emit("scalar", lambda e, z=z: e.activation(
                        probs[:, 2048:3072], probs[:, z:z + 1024], AF.Square),
                        waits=[(ve, kr_ve), (se, se.n), (pe, st["sq_pe"])], inc=se)
                    normfold(1, f, [(se, se.n)])
                    st["sq_pe"] = pe.n
                    p.emit("sync", lambda e, f=f, z=z, h2=h2: e.dma_start(
                        flatap(cc_i, f * 128 * T + h2 * 1024,
                               [(T, 128), (1, 1024)]),
                        probs[:, z:z + 1024]),
                        waits=[(ve, kr_ve)], inc=dcc, inc_by=16)
                st["kship"] = dcc.n
                st["ktile_free"] = ve.n

            def evac_k(f, n, b, pet):
                w = [(pe, pet)]
                if n == 0:
                    w += [(ve, st["ktile_free"]), (dcc, st["kship"])]
                p.emit("vector", lambda e, n=n, b=b: e.tensor_copy(
                    probs[:, n * 512:(n + 1) * 512], pbank(b)), waits=w, inc=ve)
                if n == 3:
                    ktransform(f)
                return (ve, ve.n)
            proj_fm(wB, wk_t, HFM_VE, evac_k)

            for f in range(NF):
                if li == 1:
                    qkln(lambda z, f=f: R2[:, f * T + z:f * T + z + 1024],
                         growq, growq2, nbqf)
                for h2 in range(2):
                    z = h2 * 1024
                    base = f * T + h2 * 1024
                    if li == 0:
                        for n2 in range(2):
                            p.emit("tensor", lambda e, base=base, n2=n2: e.matmul(
                                ps2[2][:, n2 * 512:(n2 + 1) * 512], pswap[:],
                                R2[:, base + n2 * 512:base + (n2 + 1) * 512],
                                start=True, stop=True),
                                waits=[(ve, ve.n)], inc=pe if n2 == 1 else None)
                        swp = pe.n
                        p.emit("vector", lambda e, base=base, z=z: e.tensor_mul(
                            probs[:, 0:1024], R2[:, base:base + 1024],
                            scrf[:, 0:1024] if z == 0 else scrf[:, 1024:2048]),
                            waits=[(dx, RT_DX), (ve, st["ktile_free"]),
                                   (dcc, st["kship"]), (pe, st["sq_pe"])], inc=ve)
                        p.emit("vector", lambda e, z=z: e.tensor_mul(
                            probs[:, 1024:2048],
                            ps2[2][:].bitcast(F32)[:, 0:1024],
                            yscf[:, 0:1024] if z == 0 else yscf[:, 1024:2048]),
                            waits=[(pe, swp)], inc=ve)
                        p.emit("vector", lambda e, base=base: e.tensor_add(
                            R2[:, base:base + 1024], probs[:, 0:1024],
                            probs[:, 1024:2048]), inc=ve)
                    p.emit("scalar", lambda e, base=base: e.activation(
                        probs[:, 2048:3072], R2[:, base:base + 1024], AF.Square),
                        waits=[(ve, ve.n), (se, se.n), (pe, st["sq_pe"])], inc=se)
                    normfold(0, f, [(se, se.n)])
                    st["sq_pe"] = pe.n
            Q_SE, Q_VE = se.n, ve.n

            st["vship"] = dcc.n

            def vtransform(f):
                vev = ve.n
                for g2 in range(2):
                    for jj in range(8):
                        w = [(ve, vev)] if jj == 0 else []
                        if jj == 0 and g2 == 0:
                            w.append((pe, st["sq_pe"]))
                        p.emit("tensor", lambda e, g2=g2, jj=jj: e.transpose(
                            ps2[2][:].bitcast(BF16)[:, jj * 128:(jj + 1) * 128],
                            probs[:, g2 * 1024 + jj * 128:g2 * 1024 + (jj + 1) * 128],
                            ident[:]), waits=w, inc=pe if jj == 7 else None)
                    tp_pe = pe.n
                    p.emit("vector", lambda e: e.tensor_copy(
                        probs[:, 2048:3072], ps2[2][:].bitcast(BF16)[:, 0:1024]),
                        waits=[(pe, tp_pe), (dcc, st["vship"])], inc=ve)
                    evv = ve.n
                    p.emit("sync", lambda e, f=f, g2=g2: e.dma_start(
                        flatap(cc_i, VOFF + f * T * 128 + g2 * 1024 * 128,
                               [(128, 128), (16384, 8), (1, 128)]),
                        probs[:, 2048:3072].rearrange("p (a d) -> p a d", d=128)),
                        waits=[(ve, evv)], inc=dcc, inc_by=16)
                    st["vship"] = dcc.n
                    vev = ve.n
                st["sq_pe"] = pe.n

            def evac_v(f, n, b, pet):
                w = [(pe, pet)]
                if n == 0:
                    w.append((dcc, st["vship"]))
                if f == 0 and n == 0:
                    w += [(ve, Q_VE), (se, Q_SE)]
                p.emit("vector", lambda e, n=n, b=b: e.tensor_copy(
                    probs[:, n * 512:(n + 1) * 512], pbank(b)), waits=w, inc=ve)
                if n == 3:
                    vtransform(f)
                return (ve, ve.n)
            proj_fm(wA, wv_t, HFM_VE, evac_v)

            p.emit("sync", lambda e: e.dma_start(
                flatap(cc_i, MOFF, [(16, 2), (1, 16)]), Mq(1).bitcast(BF16)),
                waits=[(ve, ve.n)], inc=dcc, inc_by=16)
            CC_W = dcc.n
            if stop < 3:
                return 0, 0, 0

            p.emit("gpsimd", lambda e: e.collective_compute(
                "AllGather", ALU.bypass,
                replica_groups=[[0, 1], [2, 3], [4, 5], [6, 7]],
                ins=[cc_i.ap().opt()], outs=[cc_o.ap().opt()]),
                waits=[(dcc, CC_W)], inc=dcc, inc_by=1)
            CC_DONE = dcc.n

            for r in range(2):
                p.emit("sync", lambda e, r=r: e.dma_start(
                    Mq(2 + r).bitcast(BF16),
                    flatap(cc_o, r * CCE + MOFF, [(16, 2), (1, 16)])),
                    waits=[(dcc, CC_DONE)] if r == 0 else [], inc=dx, inc_by=16)
            KM_DX = dx.n
            p.emit("vector", lambda e: e.tensor_tensor(Mq(4), Mq(2), Mq(3), ALU.max),
                   waits=[(dx, KM_DX), (ve, ve.n)], chain=ve)
            p.emit("vector", lambda e: e.tensor_mul(Mq(5), Mq(0), Mq(4)), chain=ve)
            p.emit("sync", lambda e: e.dma_start(Mr1(48, 64), Mq(5)),
                   waits=[(ve, ve.n)], inc=dx, inc_by=16)
            p.emit("vector", lambda e: e.tensor_tensor(
                Mr1(32, 40), Mr1(48, 56), Mr1(56, 64), ALU.max),
                waits=[(dx, dx.n)], chain=ve)
            p.emit("scalar", lambda e: e.activation(Mr1(32, 40), Mr1(32, 40), AF.Ln),
                   waits=[(ve, ve.n)], chain=se)
            p.emit("scalar", lambda e: e.activation(
                Mr1(32, 40), Mr1(32, 40), AF.Exp, scale=0.5), chain=se)
            p.emit("vector", lambda e: e.tensor_scalar(
                Mr1(32, 40), Mr1(32, 40), -att_scale, BSHIFT, ALU.mult, ALU.add),
                waits=[(se, se.n)], chain=ve)
            p.emit("vector", lambda e: e.tensor_copy(probs[0:1, 0:8], Mr1(32, 40)),
                   chain=ve)
            BR_VE = ve.n
            p.emit("tensor", lambda e: e.matmul(
                pbank(0)[:, 0:8], onesrow[0:1, 0:128], probs[0:1, 0:8],
                start=True, stop=True), waits=[(ve, BR_VE)], inc=pe)
            for wbuf in (wA, wB):
                p.emit("vector", lambda e, wbuf=wbuf: e.memset(
                    flatap(wbuf, 4096 + 64, [(wbuf.shape[1], 128), (130, 32), (1, 1)]),
                    1.0), waits=[(pe, pe.n), (dcc, CC_W)], chain=ve)
                p.emit("vector", lambda e, wbuf=wbuf: e.memset(
                    flatap(wbuf, 4096 + 129, [(wbuf.shape[1], 128), (130, 32), (1, 1)]),
                    1.0), chain=ve)
            p.emit("vector", lambda e: e.tensor_copy(
                biascol[:, 0:8], pbank(0)[:, 0:8]), waits=[(pe, pe.n)], inc=ve)
            BIAS_VE = ve.n
            if stop < 4:
                return 0, 0, 0

            exp_t = {}
            pv_t = {}
            pair_done = {}
            prev_chunk_gate = [(ve, BIAS_VE)]
            LP, LC, LT, LE = lim
            for pr in range(min(NP, LP)):
                wbuf = wA if pr % 2 == 0 else wB
                W0 = wbuf.shape[1]
                w0 = [(dcc, CC_DONE)]
                if pr >= 2:
                    w0.append((pe, pair_done[pr - 2]))
                for r in range(2):
                    p.emit("sync", lambda e, r=r, pr=pr, wbuf=wbuf: e.dma_start(
                        wbuf[:, r * 2048:(r + 1) * 2048],
                        flatap(cc_o, r * CCE + pr * 128 * T,
                               [(T, 128), (1, 2048)])),
                        waits=w0 if r == 0 else [], inc=dcc, inc_by=16)
                    for s_ in range(2):
                        p.emit("sync", lambda e, r=r, s_=s_, pr=pr, wbuf=wbuf,
                               W0=W0: e.dma_start(
                            flatap(wbuf, 4096 + r * 130 + s_ * 65,
                                   [(W0, 128), (260, 16), (1, 64)]),
                            flatap(cc_o, r * CCE + VOFF + pr * T * 128 + s_ * 64,
                                   [(128, 128), (16384, 16), (1, 64)])),
                            inc=dcc, inc_by=16)
                load_t = dcc.n

                def emit_pv(pr, c, t, sl, qoff, gate, wbuf, nkt):
                    for hd in range(2):
                        p.emit("tensor", lambda e, qoff=qoff, sl=sl, hd=hd,
                               t=t, wbuf=wbuf, nkt=nkt: e.matmul(
                            ps2[3][0:65, hd * 512 + qoff:(hd + 1) * 512],
                            wbuf[:, 4096 + t * 130 + hd * 65:
                                 4096 + t * 130 + hd * 65 + 65],
                            probs[:, sl * 1024 + hd * 512 + qoff:
                                  sl * 1024 + (hd + 1) * 512],
                            start=(t == 0), stop=(t == nkt - 1)),
                            waits=[gate] if hd == 0 else [],
                            inc=pe if hd == 1 else None)
                    pv_t[(pr, c, t)] = pe.n

                pend = None
                for c in range(min(4, LC)):
                    nkt = 8 * c + 8
                    for t in range(min(nkt, LT)):
                        sl = t % 3
                        jmin = max(4 * c, -(-(t - 1) // 2))
                        qoff = (jmin - 4 * c) * 128
                        qbase = pr * T + c * 512
                        tcol = (t % 2) * 2048 + (t // 2) * 128
                        w = [(dcc, load_t)]
                        if (pr, c, t - 3) in exp_t:
                            w.append((se, exp_t[(pr, c, t - 3)]))
                        if t == 0:
                            w += prev_chunk_gate
                        p.emit("tensor", lambda e, sl=sl, qoff=qoff, qbase=qbase,
                               tcol=tcol, wbuf=wbuf: e.matmul(
                            ps2[sl][:, qoff:512],
                            wbuf[0:64, tcol:tcol + 128],
                            R2[0:64, qbase + qoff:qbase + 512],
                            start=True, stop=True, tile_position=(0, 0)),
                            waits=w)
                        p.emit("tensor", lambda e, sl=sl, qoff=qoff, qbase=qbase,
                               tcol=tcol, wbuf=wbuf: e.matmul(
                            ps2[sl][:, 512 + qoff:1024],
                            wbuf[64:128, tcol:tcol + 128],
                            R2[64:128, qbase + qoff:qbase + 512],
                            start=True, stop=True, tile_position=(64, 0)),
                            inc=pe)
                        qk_pe = pe.n
                        if pend is not None:
                            emit_pv(*pend)
                            pend = None
                        wexp = [(pe, qk_pe)]
                        if (pr, c, t - 3) in pv_t:
                            wexp.append((pe, pv_t[(pr, c, t - 3)]))
                        N_t = 512 - qoff
                        p.emit("scalar", lambda e, sl=sl, qoff=qoff, pr=pr,
                               N_t=N_t: e.activation(
                            flatap(probs, sl * 1024 + qoff,
                                   [(3072, 128), (512, 2), (1, N_t)]),
                            flatap(ps2[sl], qoff, [(1024, 128), (512, 2), (1, N_t)]),
                            AF.Exp, bias=biascol[:, pr:pr + 1], scale=att_scale),
                            waits=wexp, inc=se)
                        exp_t[(pr, c, t)] = se.n
                        gate = (se, se.n)
                        jj = t // 2
                        if 4 * c <= jj < 4 * c + 4:
                            sk = t % 2
                            p.emit("vector", lambda e, sl=sl, qoff=qoff, sk=sk:
                                   e.tensor_mul(
                                flatap(probs, sl * 1024 + qoff,
                                       [(3072, 128), (512, 2), (1, 128)]),
                                flatap(probs, sl * 1024 + qoff,
                                       [(3072, 128), (512, 2), (1, 128)]),
                                flatap(mask, sk * 256,
                                       [(512, 128), (128, 2), (1, 128)])),
                                waits=[(se, se.n)], inc=ve)
                            gate = (ve, ve.n)
                        pend = (pr, c, t, sl, qoff, gate, wbuf, nkt)
                    if pend is not None:
                        emit_pv(*pend)
                        pend = None
                    if LE == 1:
                        p.emit("vector", lambda e: e.tensor_copy(
                            R2[:].bitcast(F32)[:, 0:512], ps2[3][:, 0:512]),
                            waits=[(pe, pv_t[(pr, c, nkt - 1)]),
                                   (se, exp_t[(pr, c, nkt - 1)]), (ve, ve.n)],
                            chain=ve)
                        p.emit("vector", lambda e: e.tensor_copy(
                            R2[:].bitcast(F32)[:, 512:1024], ps2[3][:, 512:1024]),
                            chain=ve)
                        prev_chunk_gate = [(ve, ve.n)]
                        continue
                    if LE == 0 or LT < nkt:
                        prev_chunk_gate = [(ve, ve.n)]
                        continue
                    PV_GATE = exp_t[(pr, c, nkt - 1)]
                    PV_END = pv_t[(pr, c, nkt - 1)]
                    p.emit("vector", lambda e: e.reciprocal(
                        rrowf[0:1, 0:512], ps2[3][64:65, 0:512]),
                        waits=[(pe, PV_END), (ve, ve.n)], chain=ve)
                    p.emit("vector", lambda e: e.reciprocal(
                        rrowf[0:1, 512:1024], ps2[3][64:65, 512:1024]), chain=ve)
                    p.emit("vector", lambda e: e.tensor_copy(
                        probs[0:1, 0:1024], rrowf[0:1, 0:1024]), chain=ve)
                    RB_VE = ve.n
                    if LE == 2:
                        prev_chunk_gate = [(ve, ve.n)]
                        continue
                    p.emit("tensor", lambda e: e.matmul(
                        ps2[0][:, 0:512], e1[0:1, :], probs[0:1, 0:512],
                        start=True, stop=False),
                        waits=[(ve, RB_VE), (se, PV_GATE)])
                    p.emit("tensor", lambda e: e.matmul(
                        ps2[0][:, 0:512], e2[0:1, :], probs[0:1, 512:1024],
                        start=False, stop=True), inc=pe)
                    RB_PE = pe.n
                    if LE == 3:
                        prev_chunk_gate = [(ve, ve.n)]
                        continue
                    p.emit("vector", lambda e: e.tensor_copy(
                        probs[:, 1024:1536], ps2[0][:, 0:512]),
                        waits=[(pe, RB_PE)], inc=ve)
                    RB2_VE = ve.n
                    if LE == 4:
                        prev_chunk_gate = [(ve, ve.n)]
                        continue
                    for hd in range(2):
                        p.emit("vector", lambda e, hd=hd, pr=pr, c=c: e.tensor_mul(
                            R2[hd * 64:(hd + 1) * 64,
                               pr * T + c * 512:pr * T + (c + 1) * 512],
                            ps2[3][0:64, hd * 512:(hd + 1) * 512],
                            probs[hd * 64:(hd + 1) * 64, 1024:1536]),
                            waits=[(ve, RB2_VE)] if hd == 0 else [], inc=ve)
                    prev_chunk_gate = [(ve, ve.n)]
                pair_done[pr] = pe.n
            return ve.n, pe.n, se.n

        attn_sublayer(0, wqT, wkT, wvT, 0.125)

        wo_t = load_slab(wB, woT, waits=[(pe, pe.n), (dcc, dcc.n)])
        s5p = S5_PARTS
        for c in (range(4) if stop >= 5 else []):
            tp_prev = []
            for f in range(NF):
                b = f % 4
                for a in range(NF):
                    w = []
                    if a == 0:
                        w = [(dw, wo_t), (ve, ve.n)]
                    p.emit("tensor", lambda e, f=f, c=c, a=a, b=b: e.matmul(
                        pbank(b),
                        wB[:, a * 1024 + f * 128:a * 1024 + (f + 1) * 128],
                        R2[:, a * T + c * 512:a * T + (c + 1) * 512],
                        start=(a == 0), stop=(a == NF - 1)),
                        waits=w, inc=pe if a == NF - 1 else None)
                wo_pe = pe.n
                p.emit("vector", lambda e, b=b: e.tensor_copy(
                    stg[:, 0:512], pbank(b)), waits=[(pe, wo_pe)] + tp_prev, inc=ve)
                ev_ve = ve.n
                for jj in range(4):
                    p.emit("tensor", lambda e, jj=jj: e.transpose(
                        psb(4)[:, jj * 128:(jj + 1) * 128],
                        stg[:, jj * 128:(jj + 1) * 128], ident[:]),
                        waits=[(ve, ev_ve)] if jj == 0 else [],
                        inc=pe if jj == 3 else None)
                tp_pe = pe.n
                tp_prev = [(pe, tp_pe)]
                p.emit("vector", lambda e, f=f: e.tensor_copy(
                    flatap(ysc, f * 128, [(4096, 128), (1024, 4), (1, 128)]),
                    psb(4)[:, 0:512].rearrange("p (a d) -> p a d", d=128)),
                    waits=[(pe, tp_pe)], inc=ve)
            YT_VE = ve.n
            if s5p < 2:
                continue
            for jj in range(4):
                j = c * 4 + jj
                p.emit("vector", lambda e, j=j, jj=jj: e.scalar_tensor_tensor(
                    ysc[:, jj * 1024:(jj + 1) * 1024],
                    x_res[:, j * D:(j + 1) * D], vslot(1)[:, j:j + 1],
                    ysc[:, jj * 1024:(jj + 1) * 1024], ALU.mult, ALU.add),
                    waits=[(ve, YT_VE)] if jj == 0 else [], inc=ve)
            T_VE = ve.n
            if s5p < 3:
                continue
            for jj in range(4):
                j = c * 4 + jj
                wsq = [(ve, T_VE), (se, se.n)] if jj < 2 else [(ve, ve.n)]
                p.emit("scalar", lambda e, j=j, jj=jj: e.activation(
                    scrf[:, (jj % 2) * 1024:(jj % 2) * 1024 + 1024],
                    ysc[:, jj * 1024:(jj + 1) * 1024], AF.Square,
                    bias=vslot(13)[:, j:j + 1]), waits=wsq, inc=se)
                p.emit("vector", lambda e, j=j, jj=jj: e.reduce_sum(
                    vslot(4)[:, j:j + 1],
                    scrf[:, (jj % 2) * 1024:(jj % 2) * 1024 + 1024].rearrange(
                        "p (a d) -> p a d", a=1), axis=AX.X),
                    waits=[(se, se.n)], chain=ve)
            c4 = vslot(5)[:, c * 4:c * 4 + 4]
            s4 = vslot(4)[:, c * 4:c * 4 + 4]
            p.emit("vector", lambda e, c4=c4, s4=s4: e.tensor_scalar(
                c4, s4, 1.0 / D, EPS_RMS, ALU.mult, ALU.add), chain=ve)
            p.emit("scalar", lambda e, c4=c4: e.activation(c4, c4, AF.Ln),
                   waits=[(ve, ve.n)], chain=se)
            p.emit("scalar", lambda e, c4=c4: e.activation(
                c4, c4, AF.Exp, scale=-0.5), chain=se)
            n4 = vslot(6)[:, c * 4:c * 4 + 4]
            m4 = vslot(3)[:, c * 4:c * 4 + 4]
            p.emit("vector", lambda e, n4=n4, m4=m4, c4=c4:
                   e.scalar_tensor_tensor(n4, m4, -1.0, c4, ALU.mult, ALU.mult),
                   waits=[(se, se.n)], chain=ve)
            apply_transpose(lambda j, c=c: ysc[:, (j - c * 4) * 1024:
                                               (j - c * 4 + 1) * 1024],
                            5, 6, R1, fold_cols=ln_cols[:, NF:2 * NF], jgs=(c,),
                            wait0=[(ve, ve.n), (se, se.n)])
            if s5p < 4:
                continue
            for jj in range(4):
                j = c * 4 + jj
                p.emit("vector", lambda e, j=j, jj=jj: e.tensor_add(
                    x_res[:, j * D:(j + 1) * D], x_res[:, j * D:(j + 1) * D],
                    ysc[:, jj * 1024:(jj + 1) * 1024]),
                    waits=[(ve, ve.n)] if jj == 0 else [], inc=ve)
                p.emit("vector", lambda e, j=j: e.tensor_scalar(
                    x_res[:, j * D:(j + 1) * D], x_res[:, j * D:(j + 1) * D],
                    vslot(3)[:, j:j + 1], None, ALU.subtract), inc=ve)
        H2_VE, H2_SE, H2_PE = ve.n, se.n, pe.n

        for srcT, dstbf, OW in ((w1T, w1bf, DFF), (w3T, w3bf, DFF), (w2T, w2bf, D)):
            ng = (D * DFF) // (128 * OW)
            p.emit("gpsimd", lambda e, s=srcT, d=dstbf, OW=OW, ng=ng: e.dma_start(
                flatap(d, 0, [(OW, 128), (OW * 128, ng), (1, OW)]),
                s.rearrange("(g p) o -> p g o", p=128)),
                inc=dw, inc_by=16)
        PRECAST = dw.n

        for c in (range(4) if stop >= 6 else []):
            wslab_t = {}

            def emit_ffn_load(gh, c, wv):
                zA = (gh % 2) * 4096
                p.emit("gpsimd", lambda e, gh=gh, zA=zA: e.dma_start(
                    wA[:, zA:zA + 4096].rearrange("p (a o) -> p a o", o=512),
                    flatap(w1bf, gh * 512, [(DFF, 128), (DFF * 128, 8), (1, 512)])),
                    waits=wv, inc=dw, inc_by=16)
                p.emit("gpsimd", lambda e, gh=gh, zA=zA: e.dma_start(
                    wB[:, zA:zA + 4096].rearrange("p (a o) -> p a o", o=512),
                    flatap(w3bf, gh * 512, [(DFF, 128), (DFF * 128, 8), (1, 512)])),
                    inc=dw, inc_by=16)
                wslab_t[gh] = dw.n

            emit_ffn_load(0, c, [(dw, PRECAST), (pe, pe.n)])
            emit_ffn_load(1, c, [])
            for gh in range(8):
                zA = (gh % 2) * 4096
                for dt in range(4):
                    dtg = gh * 4 + dt
                    b1 = dtg % 2
                    b3 = 2 + dtg % 2
                    for a in range(NF):
                        w = [] if a else [(dw, wslab_t[gh]),
                                          (se, se.n), (ve, ve.n)]
                        p.emit("tensor", lambda e, zA=zA, dt=dt, a=a, b1=b1, c=c:
                               e.matmul(
                            pbank(b1),
                            wA[:, zA + a * 512 + dt * 128:zA + a * 512 + (dt + 1) * 128],
                            R1[:, a * T + c * 512:a * T + (c + 1) * 512],
                            start=(a == 0), stop=(a == NF - 1)),
                            waits=w, inc=pe if a == NF - 1 else None)
                    p1_pe = pe.n
                    for a in range(NF):
                        p.emit("tensor", lambda e, zA=zA, dt=dt, a=a, b3=b3, c=c:
                               e.matmul(
                            pbank(b3),
                            wB[:, zA + a * 512 + dt * 128:zA + a * 512 + (dt + 1) * 128],
                            R1[:, a * T + c * 512:a * T + (c + 1) * 512],
                            start=(a == 0), stop=(a == NF - 1)),
                            waits=[(ve, ve.n)] if a == 0 else [],
                            inc=pe if a == NF - 1 else None)
                    p3_pe = pe.n
                    p.emit("scalar", lambda e, b1=b1, dtg=dtg: e.activation(
                        probs[:, 0:512] if dtg % 2 == 0 else probs[:, 512:1024],
                        pbank(b1), AF.Silu),
                        waits=[(pe, p1_pe), (se, se.n), (ve, ve.n)], inc=se)
                    p.emit("vector", lambda e, b3=b3, dtg=dtg: e.tensor_mul(
                        R2[:, dtg * 512:(dtg + 1) * 512],
                        probs[:, 0:512] if dtg % 2 == 0 else probs[:, 512:1024],
                        pbank(b3)),
                        waits=[(pe, p3_pe), (se, se.n)], inc=ve)
                if gh + 2 < 8:
                    emit_ffn_load(gh + 2, c, [(pe, pe.n)])
            P_VE = ve.n
            w2_t = {}

            def emit_w2_load(f, wv):
                zB = (f % 2) * 4096
                p.emit("gpsimd", lambda e, f=f, zB=zB: e.dma_start(
                    wA[:, zB:zB + 4096].rearrange("p (a o) -> p a o", o=128),
                    flatap(w2bf, f * 128, [(D, 128), (D * 128, 32), (1, 128)])),
                    waits=wv, inc=dw, inc_by=16)
                w2_t[f] = dw.n

            emit_w2_load(0, [(dw, PRECAST), (ve, P_VE), (pe, pe.n)])
            emit_w2_load(1, [])
            tp_prev = []
            bank_guard = {0: None, 1: None}
            for f in range(NF):
                zB = (f % 2) * 4096
                b = f % 2
                for dt in range(32):
                    w = [] if dt else ([(dw, w2_t[f]), (ve, P_VE)] +
                                       ([bank_guard[b]] if bank_guard[b] else []))
                    p.emit("tensor", lambda e, zB=zB, dt=dt, f=f, b=b: e.matmul(
                        pbank(b), wA[:, zB + dt * 128:zB + (dt + 1) * 128],
                        R2[:, dt * 512:(dt + 1) * 512],
                        start=(dt == 0), stop=(dt == 31)),
                        waits=w, inc=pe if dt == 31 else None)
                ff_pe = pe.n
                p.emit("vector", lambda e, b=b: e.tensor_copy(
                    stg[:, 2048:2560], pbank(b)),
                    waits=[(pe, ff_pe)] + tp_prev, inc=ve)
                bank_guard[b] = (ve, ve.n)
                ev_ve = ve.n
                for jj in range(4):
                    p.emit("tensor", lambda e, jj=jj: e.transpose(
                        psb(4)[:, jj * 128:(jj + 1) * 128],
                        stg[:, 2048 + jj * 128:2048 + (jj + 1) * 128], ident[:]),
                        waits=[(ve, ev_ve)] if jj == 0 else [],
                        inc=pe if jj == 3 else None)
                tp_pe = pe.n
                tp_prev = [(pe, tp_pe)]
                p.emit("vector", lambda e, f=f, c=c: e.tensor_add(
                    flatap(x_res, c * 4 * D + f * 128, [(NT * D, 128), (D, 4), (1, 128)]),
                    flatap(x_res, c * 4 * D + f * 128, [(NT * D, 128), (D, 4), (1, 128)]),
                    psb(4)[:, 0:512].rearrange("p (a d) -> p a d", d=128)),
                    waits=[(pe, tp_pe), (ve, ve.n)], inc=ve)
                if f + 2 < NF:
                    emit_w2_load(f + 2, [(pe, pe.n)])
        S1_VE, S1_SE, S1_PE = ve.n, se.n, pe.n

        def ffn_q(first):
            ln_stats(7, 8)
            mx = vslot(9); mn = vslot(10); av = vslot(11); cz = vslot(12)
            nbz = vslot(14)
            p.emit("vector", lambda e: e.reduce_max(
                mx, x_res[:].rearrange("p (j d) -> p j d", d=D), axis=AX.X),
                chain=ve)
            p.emit("vector", lambda e: e.tensor_reduce(
                mn, x_res[:].rearrange("p (j d) -> p j d", d=D), axis=AX.X,
                op=ALU.min), chain=ve)
            p.emit("vector", lambda e: e.tensor_sub(mx, mx, vslot(7)), chain=ve)
            p.emit("vector", lambda e: e.tensor_sub(mn, vslot(7), mn), chain=ve)
            p.emit("vector", lambda e: e.tensor_tensor(av, mx, mn, ALU.max), chain=ve)
            p.emit("vector", lambda e: e.tensor_mul(av, av, vslot(8)),
                   waits=[(se, se.n)], chain=ve)
            p.emit("vector", lambda e: e.tensor_scalar_add(av, av, EPS_Q), chain=ve)
            p.emit("vector", lambda e: e.reciprocal(cz, av), chain=ve)
            p.emit("vector", lambda e: e.tensor_scalar_mul(cz, cz, 127.0), chain=ve)
            p.emit("vector", lambda e: e.tensor_mul(cz, cz, vslot(8)), chain=ve)
            p.emit("vector", lambda e: e.scalar_tensor_tensor(
                nbz, vslot(7), -1.0, cz, ALU.mult, ALU.mult), chain=ve)
            LNQ_VE, LNQ_SE = ve.n, se.n

            if first:
                R2f = R2[:].bitcast(F32)
                p.emit("sync", lambda e: e.dma_start(
                    R2f[:, 0:8192].rearrange("p (a o) -> p a o", o=D),
                    fwT.rearrange("(a p) o -> p a o", p=128)),
                    waits=[(pe, pe.n), (ve, ve.n)], inc=dx, inc_by=16)
                FW_DX = dx.n
                p.emit("vector", lambda e: e.reduce_sum(
                    colsf[:, 4:5], R2f[:, 0:8192].rearrange(
                        "p (a o) -> p a o", a=1), axis=AX.X,
                    apply_absolute_value=True),
                    waits=[(dx, FW_DX)], chain=ve)
                p.emit("sync", lambda e: e.dma_start(
                    scrf[0:1, 0:128], colsf[:, 4:5]),
                    waits=[(ve, ve.n)], inc=dx, inc_by=16)
                p.emit("vector", lambda e: e.reduce_sum(
                    Mr1(32, 33), scrf[0:1, 0:128].rearrange("p (a o) -> p a o", a=1),
                    axis=AX.X), waits=[(dx, dx.n)], chain=ve)
                p.emit("vector", lambda e: e.tensor_scalar(
                    Mr1(32, 33), Mr1(32, 33), 1.0 / (D * D), EPS_Q,
                    ALU.mult, ALU.add), chain=ve)
                p.emit("vector", lambda e: e.reciprocal(
                    Mr1(33, 34), Mr1(32, 33)), chain=ve)
                p.emit("vector", lambda e: e.tensor_scalar_mul(
                    Mr1(34, 35), Mr1(32, 33), 1.0 / 127.0), chain=ve)
                WS_VE = ve.n
                p.emit("tensor", lambda e: e.matmul(
                    pbank(0)[:, 0:2], onesf32[0:1, :], Mr1(33, 35),
                    start=True, stop=True), waits=[(ve, WS_VE), (pe, pe.n)], inc=pe)
                p.emit("vector", lambda e: e.tensor_copy(
                    colsf[:, 5:7], pbank(0)[:, 0:2]), waits=[(pe, pe.n)], chain=ve)
                for ch in range(4):
                    z = ch * 2048
                    p.emit("vector", lambda e, z=z: e.tensor_scalar_mul(
                        R2f[:, z:z + 2048], R2f[:, z:z + 2048], colsf[:, 5:6]),
                        chain=ve)
                    p.emit("vector", lambda e, z=z: e.tensor_copy(
                        iscr[:, 0:2048], R2f[:, z:z + 2048]), chain=ve)
                    p.emit("vector", lambda e, z=z: e.tensor_copy(
                        R2f[:, z:z + 2048], iscr[:, 0:2048]), chain=ve)
                    p.emit("vector", lambda e, z=z: e.tensor_scalar(
                        R2f[:, z:z + 2048], R2f[:, z:z + 2048], 1.0, -1.0,
                        ALU.min, ALU.max), chain=ve)
                    p.emit("vector", lambda e, z=z: e.tensor_copy(
                        wA[:, z:z + 2048], R2f[:, z:z + 2048]), chain=ve)
                WQ_VE = ve.n
                p.emit("sync", lambda e: e.dma_start(wq_park.ap(), wA[:, 0:8192]),
                       waits=[(ve, WQ_VE)], inc=dcc, inc_by=16)
                wtgt, wsem = WQ_VE, ve
            else:
                p.emit("gpsimd", lambda e: e.dma_start(
                    wA[:, 0:8192], wq_park.ap()),
                    waits=[(pe, pe.n)], inc=dw, inc_by=16)
                wtgt, wsem = dw.n, dw

            prev_evac = []
            pe_tp = pe.n
            for jg in range(4):
                for jj in range(4):
                    j = jg * 4 + jj
                    w = [(ve, ve.n), (se, se.n)] if (jg == 0 and jj == 0) else []
                    if jj == 0 and jg > 0:
                        w.append((pe, pe_tp))
                    p.emit("vector", lambda e, j=j, jj=jj: e.tensor_scalar(
                        scrf[:, 0:1024], x_res[:, j * D:(j + 1) * D],
                        cz[:, j:j + 1], nbz[:, j:j + 1], ALU.mult, ALU.add),
                        waits=w, chain=ve)
                    p.emit("vector", lambda e, jj=jj: e.tensor_copy(
                        iscr[:, 1024:2048], scrf[:, 0:1024]), chain=ve)
                    p.emit("vector", lambda e, jj=jj: e.tensor_copy(
                        ysc[:, jj * 1024:(jj + 1) * 1024], iscr[:, 1024:2048]),
                        chain=ve)
                ve_app = ve.n
                for f in range(NF):
                    for jj in range(4):
                        w = []
                        if f == 0 and jj == 0:
                            w = [(ve, ve_app)] + prev_evac
                        p.emit("tensor", lambda e, f=f, jj=jj: e.transpose(
                            psb(f)[:, jj * 128:(jj + 1) * 128],
                            ysc[:, jj * 1024 + f * 128:jj * 1024 + (f + 1) * 128],
                            ident[:]),
                            waits=w, inc=pe if (f == NF - 1 and jj == 3) else None)
                pe_tp = pe.n
                prev_evac = []
                for f in range(NF):
                    p.emit("vector", lambda e, f=f, jg=jg: e.tensor_copy(
                        R1[:, f * T + jg * 512:f * T + (jg + 1) * 512],
                        psb(f)[:, 0:512]), waits=[(pe, pe_tp)], inc=ve)
                    prev_evac = [(ve, ve.n)]
            Z_VE = ve.n

            guards = [None] * 4
            tp_prev = []
            for f in range(NF):
                for n in range(4):
                    b = (f * 4 + n) % 4
                    for a in range(NF):
                        w = []
                        if a == 0:
                            w = [(wsem, wtgt), (ve, Z_VE)]
                            if guards[b] is not None:
                                w.append(guards[b])
                        p.emit("tensor", lambda e, f=f, n=n, a=a, b=b: e.matmul(
                            pbank(b),
                            wA[:, a * 1024 + f * 128:a * 1024 + (f + 1) * 128],
                            R1[:, a * T + n * 512:a * T + (n + 1) * 512],
                            start=(a == 0), stop=(a == NF - 1)),
                            waits=w, inc=pe if a == NF - 1 else None)
                    q_pe = pe.n
                    p.emit("vector", lambda e, b=b: e.tensor_scalar_mul(
                        probs[:, 0:512], pbank(b), colsf[:, 6:7]),
                        waits=[(pe, q_pe)] + tp_prev, inc=ve)
                    guards[b] = (ve, ve.n)
                    ev_ve = ve.n
                    for jj in range(4):
                        p.emit("tensor", lambda e, jj=jj: e.transpose(
                            psb(4)[:, jj * 128:(jj + 1) * 128],
                            probs[:, jj * 128:(jj + 1) * 128], ident[:]),
                            waits=[(ve, ev_ve)] if jj == 0 else [],
                            inc=pe if jj == 3 else None)
                    tp_pe = pe.n
                    tp_prev = [(pe, tp_pe)]
                    p.emit("vector", lambda e: e.tensor_copy(
                        probs[:, 512:1024], psb(4)[:, 0:512]),
                        waits=[(pe, tp_pe)], inc=ve)
                    pv_ve = ve.n
                    for jj in range(4):
                        j = n * 4 + jj
                        p.emit("vector", lambda e, j=j, jj=jj, f=f:
                               e.scalar_tensor_tensor(
                            x_res[:, j * D + f * 128:j * D + (f + 1) * 128],
                            probs[:, 512 + jj * 128:512 + (jj + 1) * 128],
                            av[:, j:j + 1],
                            x_res[:, j * D + f * 128:j * D + (f + 1) * 128],
                            ALU.mult, ALU.add),
                            waits=[(ve, pv_ve)] if jj == 0 else [], inc=ve)

        if n_sub >= 2:
            ffn_q(True)

        if n_sub >= 3:
            ln_stats(0, 1)
            p.emit("vector", lambda e: e.scalar_tensor_tensor(
                vslot(15), vslot(0), -1.0, vslot(1), ALU.mult, ALU.mult),
                waits=[(se, se.n)], chain=ve)
            apply_transpose(lambda j: x_res[:, j * D:(j + 1) * D], 1, 15, R1,
                            fold_cols=None, wait0=[(ve, ve.n), (se, se.n),
                                                   (pe, pe.n)])
            attn_sublayer(1, bqT, bkT, bvT, 1.0)
            wo2_t = load_slab(wB, boT, waits=[(pe, pe.n), (dcc, dcc.n)])
            tp_prev = []
            bank_guard2 = {}
            for c in range(4):
                for f in range(NF):
                    b = f % 4
                    for a in range(NF):
                        w = []
                        if a == 0:
                            w = [(dw, wo2_t), (ve, ve.n)]
                            if bank_guard2.get(b):
                                w.append(bank_guard2[b])
                        p.emit("tensor", lambda e, f=f, c=c, a=a, b=b: e.matmul(
                            pbank(b),
                            wB[:, a * 1024 + f * 128:a * 1024 + (f + 1) * 128],
                            R2[:, a * T + c * 512:a * T + (c + 1) * 512],
                            start=(a == 0), stop=(a == NF - 1)),
                            waits=w, inc=pe if a == NF - 1 else None)
                    wo_pe = pe.n
                    p.emit("vector", lambda e, b=b, f=f: e.tensor_scalar(
                        stg[:, 0:512], pbank(b), 1.0,
                        ln_cols[:, 2 * NF + f:2 * NF + f + 1], ALU.mult, ALU.add),
                        waits=[(pe, wo_pe)] + tp_prev, inc=ve)
                    bank_guard2[b] = (ve, ve.n)
                    ev_ve = ve.n
                    for jj in range(4):
                        p.emit("tensor", lambda e, jj=jj: e.transpose(
                            psb(4)[:, jj * 128:(jj + 1) * 128],
                            stg[:, jj * 128:(jj + 1) * 128], ident[:]),
                            waits=[(ve, ev_ve)] if jj == 0 else [],
                            inc=pe if jj == 3 else None)
                    tp_pe = pe.n
                    tp_prev = [(pe, tp_pe)]
                    p.emit("vector", lambda e, f=f, c=c: e.tensor_add(
                        flatap(x_res, c * 4 * D + f * 128,
                               [(NT * D, 128), (D, 4), (1, 128)]),
                        flatap(x_res, c * 4 * D + f * 128,
                               [(NT * D, 128), (D, 4), (1, 128)]),
                        psb(4)[:, 0:512].rearrange("p (a d) -> p a d", d=128)),
                        waits=[(pe, tp_pe), (ve, ve.n)], inc=ve)

        if n_sub >= 4:
            ffn_q(False)

        TAPS = []
        loc = locals()
        for name, fn in DEV_TAPS:
            ap_sb, shape, dtype = fn(loc)
            o = nc.dram_tensor(f"tap_{name}", shape, dtype, kind="ExternalOutput").ap()
            TAPS.append((o, ap_sb, dtype))

        p.emit("sync", lambda e: e.dma_start(
            xout.rearrange("(j p) d -> p j d", p=128),
            x_res[:].rearrange("p (j d) -> p j d", d=D)),
            waits=[(se, se.n), (ve, ve.n), (pe, pe.n), (po, po.n), (dw, dw.n),
                   (dcc, dcc.n), (dx, dx.n)],
            inc=od, inc_by=16)
        for o, ap_sb, dtype in TAPS:
            if dtype == F32:
                p.emit("gpsimd", lambda e, o=o, a=ap_sb: e.dma_start(o, a),
                       waits=[(od, 16)], inc=od, inc_by=16)
            else:
                p.emit("sync", lambda e, o=o, a=ap_sb: e.dma_start(o, a),
                       inc=od, inc_by=16)
        p.emit("sync", lambda e: None, waits=[(od, od.n)])

        p.finalize(nc)
    return nc


_CACHE = {}


def _grow(v):
    v = np.asarray(v, np.float32)
    r = np.zeros((2, 128), BF)
    r[0, 0:64] = v.astype(BF)
    r[1, 64:128] = v.astype(BF)
    return r


def _prep_inputs(inputs, h):
    idx, cos, sin, maskdup, ident, pswap = _host_aux(h)
    tp = lambda a: np.ascontiguousarray(np.asarray(a, np.float32).T)
    g2 = lambda v: np.tile(np.asarray(v, np.float32), 2).reshape(128, 1)
    d = {
        "wqT": tp(inputs["L0_attn_a_wq"]), "wkT": tp(inputs["L0_attn_a_wk"]),
        "wvT": tp(inputs["L0_attn_a_wv"]), "woT": tp(inputs["L0_attn_a_wo"]),
        "w1T": tp(inputs["L0_attn_a_w1"]), "w3T": tp(inputs["L0_attn_a_w3"]),
        "w2T": tp(inputs["L0_attn_a_w2"]),
        "fwT": tp(inputs["L0_ffn_weight"]),
        "bqT": tp(inputs["L0_attn_b_Wq"]), "bkT": tp(inputs["L0_attn_b_Wk"]),
        "bvT": tp(inputs["L0_attn_b_Wv"]), "boT": tp(inputs["L0_attn_b_Wo"]),
        "rcos": cos, "rsin": sin,
        "maskin": maskdup, "identin": ident, "pswapin": pswap,
        "gqc": g2(inputs["L0_attn_b_gq"]), "bqc": g2(inputs["L0_attn_b_bq"]),
        "gkc": g2(inputs["L0_attn_b_gk"]), "bkc": g2(inputs["L0_attn_b_bk"]),
        "boc": np.ascontiguousarray(
            np.asarray(inputs["L0_attn_b_bo"], np.float32).reshape(NF, 128).T),
        "growsq": _grow(inputs["L0_attn_b_gq"]),
        "growsk": _grow(inputs["L0_attn_b_gk"]),
        "nbrowq": _grow(-np.asarray(inputs["L0_attn_b_bq"], np.float64)),
        "nbrowk": _grow(-np.asarray(inputs["L0_attn_b_bk"], np.float64)),
        "nbdupq": np.tile(-np.asarray(inputs["L0_attn_b_bq"], np.float32),
                          2).reshape(1, 128).astype(BF),
        "nbdupk": np.tile(-np.asarray(inputs["L0_attn_b_bk"], np.float32),
                          2).reshape(1, 128).astype(BF),
        "n1c": np.ascontiguousarray(
            np.asarray(inputs["L0_attn_a_norm1_w"], np.float32).reshape(NF, 128).T),
        "n2c": np.ascontiguousarray(
            np.asarray(inputs["L0_attn_a_norm2_w"], np.float32).reshape(NF, 128).T),
    }
    return idx, d


def kernel(**inputs):
    inputs.get("L0_attn_b_mask")
    x = np.asarray(inputs["L0_attn_a_x"], np.float32)
    if "prog" not in _CACHE:
        _CACHE["prog"] = build_program()
    nc = _CACHE["prog"]

    in_maps = []
    idxs = []
    for c in range(8):
        b, h = c // 2, c % 2
        idx, d = _prep_inputs(inputs, h)
        d["xin"] = np.ascontiguousarray(x[b, idx, :])
        in_maps.append(d)
        idxs.append(idx)

    res = run_bass_kernel_spmd(nc, in_maps, core_ids=list(range(8)))

    out = np.empty_like(x)
    for c in range(8):
        b = c // 2
        out[b, idxs[c], :] = np.asarray(res.results[c]["xout"], np.float32)
    extras = {}
    for c in range(8):
        for k, v in res.results[c].items():
            if k.startswith("tap_"):
                extras.setdefault(k, []).append(np.asarray(v))
    kernel.last_taps = extras
    return out
